# version 05
# baseline (speedup 1.0000x reference)
import numpy as np
from contextlib import ExitStack

import concourse.bass as bass
import concourse.mybir as mybir
from concourse.bass_utils import run_bass_kernel_spmd
from concourse.bass import IndirectOffsetOnAxis

F32 = mybir.dt.float32
BF16 = mybir.dt.bfloat16
I32 = mybir.dt.int32
AF = mybir.ActivationFunctionType
ALU = mybir.AluOpType
AX = mybir.AxisListType

NCORES = 8
B, S, D = 16, 2048, 1024
H, DH = 16, 64
SEQ_PER_CORE = B // NCORES
NEG = -1.0e30


class Sch:
    ENGINES = ("sync", "tensor", "vector", "scalar", "gpsimd")

    def __init__(self, nc, ctx):
        self.nc = nc
        self.ops = {e: [] for e in self.ENGINES}
        self.cnt = {e: 0 for e in self.ENGINES}
        self.sems = {e: ctx.enter_context(nc.semaphore(f"s_{e}")) for e in self.ENGINES}
        self.nblk = 0
        sems = list(self.sems.values())

        def clear_all(e, _s=tuple(sems)):
            for s_ in _s:
                e.sem_clear(s_)
        self.ops["gpsimd"].append(lambda e: clear_all(e))
        self.flush()

    def emit(self, eng, fn, inc=1, waits=()):
        best = {}
        for w in waits:
            if w is None:
                continue
            we, wv = w
            if we == eng and wv > self.cnt[eng]:
                raise AssertionError("same-engine wait on future op")
            if wv > best.get(we, -1):
                best[we] = wv
        waits = tuple(best.items())
        sem_map = self.sems
        if inc:
            self.cnt[eng] += inc
        tok = (eng, self.cnt[eng])

        def run(e, _fn=fn, _w=waits, _inc=inc, _sem=sem_map[eng]):
            for (we, wv) in _w:
                e.wait_ge(sem_map[we], wv)
            ins = _fn(e)
            if _inc:
                if ins is not None:
                    ins.then_inc(_sem, _inc)
                else:
                    e.nop().then_inc(_sem, _inc)
        self.ops[eng].append(run)
        return tok

    def dma(self, eng, out, in_, waits=()):
        tok = self.emit(eng, lambda e, o=out, i=in_: e.dma_start(out=o, in_=i),
                        inc=16, waits=waits)
        self.emit(eng, lambda e: None, inc=0, waits=[tok])
        return tok

    def dma_slow(self, eng, out, in_, waits=()):
        return self.emit(eng, lambda e, o=out, i=in_: e.dma_start(
            out=o, in_=i, allow_slow_non_contiguous=True), inc=16, waits=waits)

    def tok(self, eng):
        return (eng, self.cnt[eng])

    def flush(self):
        self.nblk += 1
        live = [(e, self.ops[e]) for e in self.ENGINES if self.ops[e]]
        if not live:
            return
        with self.nc.Block(f"blk{self.nblk}") as blk:
            for ename, oplist in live:
                dec = getattr(blk, ename)

                def body(engobj, _ops=tuple(oplist)):
                    for f in _ops:
                        f(engobj)
                dec(body)
        for e in self.ENGINES:
            self.ops[e] = []


_NAMEC = [0]


def sbuf(nc, ctx, name, shape, dt):
    _NAMEC[0] += 1
    return ctx.enter_context(nc.sbuf_tensor(f"{name}_{_NAMEC[0]}", shape, dt))


def psum(nc, ctx, name, shape, dt):
    _NAMEC[0] += 1
    return ctx.enter_context(nc.psum_tensor(f"{name}_{_NAMEC[0]}", shape, dt))


class PHalf:

    def __init__(self, t, off):
        self.t, self.off = t, off

    def __getitem__(self, idx):
        ps, fs = idx
        lo = fs.start or 0
        hi = fs.stop if fs.stop is not None else 512
        return self.t[ps, self.off + lo: self.off + hi]


def make_ln_state(nc, ctx, tag=""):
    st = {}
    st["xin"] = [sbuf(nc, ctx, f"lnx{i}{tag}", [128, D], F32) for i in range(2)]
    st["sq_scr"] = [sbuf(nc, ctx, f"lnsq{i}{tag}", [128, D], F32) for i in range(2)]
    st["ln_tm"] = [sbuf(nc, ctx, f"lnt{i}{tag}", [128, D], BF16) for i in range(2)]
    st["stat"] = sbuf(nc, ctx, f"lnstat{tag}", [128, 16], F32)
    st["id_bf"] = sbuf(nc, ctx, f"idbf{tag}", [128, 128], BF16)
    st["epsb"] = sbuf(nc, ctx, f"epsb{tag}", [128, 1], F32)
    st["xin_free"] = [None, None]
    st["sqw_free"] = [None, None]
    st["stat_free"] = [None, None]
    st["ln_free"] = [None, None]
    st["ptr_free"] = [None, None]
    return st


def emit_load_ident(sch, st, id_h):
    idf = st["xin"][0]
    t = sch.dma("sync", idf[:, 0:128], id_h)
    t2 = sch.emit("vector", lambda e: e.tensor_copy(st["id_bf"][:], idf[:, 0:128]),
                  waits=[t])
    st["xin_free"][0] = t2
    st["t_eps"] = sch.emit("gpsimd", lambda e: e.memset(st["epsb"][:], 1e-5))


def emit_ln_to_fm(sch, st, x_h, row0, ntiles, ln_fm, fm_stride, fm_col0,
                  dram_out=None, dram_rows=None, post_tile=None):
    sums = st["stat"]
    for tt in range(ntiles):
        p = tt % 2
        co = 8 * p
        xb = st["xin"][p]
        sq_scr = st["sq_scr"][p]
        lnb = st["ln_tm"][p]
        t_x = sch.dma("sync", xb[:], x_h[row0 + tt * 128: row0 + (tt + 1) * 128, :],
                      waits=[st["xin_free"][p]])
        t_sum = sch.emit("vector", lambda e, x=xb, co=co: e.tensor_reduce(
            sums[:, co + 0:co + 1], x[:], AX.X, ALU.add),
            waits=[t_x, st["stat_free"][p]])
        t_sq0 = sch.emit("vector", lambda e, x=xb, sq=sq_scr: e.tensor_tensor(
            sq[:], x[:], x[:], ALU.mult), waits=[t_x, st["sqw_free"][p]])
        t_sq = sch.emit("vector", lambda e, sq=sq_scr, co=co: e.tensor_reduce(
            sums[:, co + 4:co + 5], sq[:], AX.X, ALU.add),
            waits=[t_sq0, st["stat_free"][p]])
        st["sqw_free"][p] = t_sq
        t_mean = sch.emit("vector", lambda e, co=co: e.tensor_scalar(
            sums[:, co + 1:co + 2], sums[:, co + 0:co + 1], 1.0 / D, None, ALU.mult),
            waits=[t_sum])
        t_msq = sch.emit("vector", lambda e, co=co: e.tensor_scalar(
            sums[:, co + 2:co + 3], sums[:, co + 4:co + 5], 1.0 / D, None, ALU.mult),
            waits=[t_sq])
        t_mm = sch.emit("vector", lambda e, co=co: e.tensor_tensor(
            sums[:, co + 3:co + 4], sums[:, co + 1:co + 2], sums[:, co + 1:co + 2],
            ALU.mult), waits=[t_mean])
        t_var = sch.emit("vector", lambda e, co=co: e.tensor_tensor(
            sums[:, co + 5:co + 6], sums[:, co + 2:co + 3], sums[:, co + 3:co + 4],
            ALU.subtract), waits=[t_msq, t_mm])
        t_sd = sch.emit("scalar", lambda e, co=co: e.activation(
            sums[:, co + 6:co + 7], sums[:, co + 5:co + 6], AF.Sqrt,
            bias=st["epsb"][:, 0:1]), waits=[t_var, st["t_eps"]])
        t_sd2 = sch.emit("scalar", lambda e, co=co: e.activation(
            sums[:, co + 4:co + 5], sums[:, co + 5:co + 6], AF.Sqrt,
            bias=st["epsb"][:, 0:1]), waits=[t_sd])
        t_rstd = sch.emit("vector", lambda e, co=co: e.reciprocal(
            sums[:, co + 7:co + 8], sums[:, co + 6:co + 7]), waits=[t_sd2])
        t_ap = sch.emit("vector", lambda e, x=xb, l=lnb, co=co: e.tensor_scalar(
            l[:], x[:], sums[:, co + 1:co + 2], sums[:, co + 7:co + 8],
            ALU.subtract, ALU.mult), waits=[t_rstd, st["ln_free"][p]])
        st["xin_free"][p] = t_ap
        st["stat_free"][p] = t_ap
        if dram_out is not None:
            r0d = dram_rows[tt]
            t_dma = sch.dma("sync", dram_out[r0d:r0d + 128, :],
                            st["ln_tm"][p][:], waits=[t_ap])
            st["ln_free"][p] = t_dma
            if post_tile is not None:
                st["ln_free"][p] = post_tile(tt, p, t_dma)
            sch.flush()
            continue
        last_cp = None
        for c in range(8):
            pb = st["ptr"][c % 2]
            t_tr = sch.emit("tensor", lambda e, l=lnb, c=c, pb=pb: e.transpose(
                pb[0:128, 0:128], l[:, c * 128:(c + 1) * 128], st["id_bf"][:]),
                waits=[t_ap, st["ptr_free"][c % 2]])
            t_cp = sch.emit("vector", lambda e, c=c, pb=pb, tt=tt: e.tensor_copy(
                ln_fm[:, c * fm_stride + fm_col0 + tt * 128:
                      c * fm_stride + fm_col0 + (tt + 1) * 128], pb[0:128, 0:128]),
                waits=[t_tr])
            st["ptr_free"][c % 2] = t_cp
            last_cp = t_cp
        st["ln_free"][p] = last_cp
        sch.flush()


def emit_attn0(nc, sch, ctx, x_h, y_h, wq_h, wkv_h, wo_h, bq_h, bkv_h, bo_h,
               cs_h, sc_h, id_h, dm_h):
    if True:
        st = make_ln_state(nc, ctx)
        ln_fm = sbuf(nc, ctx, "lnfm", [128, 8 * S], BF16)
        attn_fm = sbuf(nc, ctx, "attnfm", [128, 8 * S], BF16)
        w_f = [sbuf(nc, ctx, f"wf{i}", [128, 1024], F32) for i in range(2)]
        wqA = sbuf(nc, ctx, "wqA", [128, 1024], BF16)
        wqB = sbuf(nc, ctx, "wqB", [128, 1024], BF16)
        wkA = sbuf(nc, ctx, "wkA", [128, 1024], BF16)
        wkB = sbuf(nc, ctx, "wkB", [128, 1024], BF16)
        wv_b = sbuf(nc, ctx, "wvb", [128, 1024], BF16)
        csf = sbuf(nc, ctx, "csf", [128, S], F32)
        scf = sbuf(nc, ctx, "scf", [128, S], F32)
        ropeS = [sbuf(nc, ctx, f"ropeS{i}", [128, 512], F32) for i in range(2)]
        qr_b = sbuf(nc, ctx, "qrb", [128, S], BF16)
        kr_b = sbuf(nc, ctx, "krb", [128, S], BF16)
        v_b = sbuf(nc, ctx, "vb", [128, S], BF16)
        vhat = [sbuf(nc, ctx, f"vhat{h}", [128, 16 * 66], BF16) for h in range(2)]
        idf32 = sbuf(nc, ctx, "idf32", [128, 128], F32)
        bstg = sbuf(nc, ctx, "bstg", [16, 128], F32)
        bqpair = sbuf(nc, ctx, "bqpair", [128, 8], F32)
        bkpair = sbuf(nc, ctx, "bkpair", [128, 8], F32)
        biaqA = sbuf(nc, ctx, "biaqA", [128, 8], F32)
        biaqB = sbuf(nc, ctx, "biaqB", [128, 8], F32)
        biakA = sbuf(nc, ctx, "biakA", [128, 8], F32)
        biakB = sbuf(nc, ctx, "biakB", [128, 8], F32)
        biav = sbuf(nc, ctx, "biav", [128, 8], F32)
        eA = [sbuf(nc, ctx, f"eA{i}", [128, 512], BF16) for i in range(2)]
        eB = [sbuf(nc, ctx, f"eB{i}", [128, 512], BF16) for i in range(2)]
        dmask = sbuf(nc, ctx, "dmask", [128, 4 * 512], BF16)
        rcp = sbuf(nc, ctx, "rcp", [1, 1024], F32)
        sbc = sbuf(nc, ctx, "sbc", [128, 1024], F32)
        onesf = sbuf(nc, ctx, "onesf", [1, 128], F32)
        bm4 = sbuf(nc, ctx, "bm4", [128, 1], F32)
        bo_bc = sbuf(nc, ctx, "bobc", [128, D], F32)
        borow = sbuf(nc, ctx, "borow", [1, D], F32)
        wob = sbuf(nc, ctx, "wob", [128, 8 * 1024], BF16)
        xrt = [sbuf(nc, ctx, f"xrt{i}", [128, D], F32) for i in range(2)]
        yt = [sbuf(nc, ctx, f"yt{i}", [128, D], F32) for i in range(2)]

        psA = psum(nc, ctx, "psA", [128, 512], F32)
        psB = psum(nc, ctx, "psB", [128, 512], F32)
        psA2 = psum(nc, ctx, "psA2", [128, 512], F32)
        psB2 = psum(nc, ctx, "psB2", [128, 512], F32)
        poA = psum(nc, ctx, "poA", [128, 512], F32)
        poB = psum(nc, ctx, "poB", [128, 512], F32)
        pT = psum(nc, ctx, "pT", [128, 1024], BF16)
        pT2 = psum(nc, ctx, "pT2", [128, 1024], BF16)
        st["ptr"] = [PHalf(pT, 0), PHalf(pT2, 0)]

        t_dm = sch.dma("sync", dmask[:], dm_h)
        emit_load_ident(sch, st, id_h)
        sch.dma("sync", csf[:], cs_h)
        sch.dma("sync", scf[:], sc_h)
        t_bo = sch.dma("sync", borow[:], bo_h)
        t_idf = sch.dma("sync", idf32[:], id_h)
        t_l1 = sch.dma("sync", bstg[0:8, :],
                       bq_h.rearrange("(c p) o -> c (p o)", c=8))
        tt1 = sch.emit("tensor", lambda e: e.transpose(
            psA2[:, 0:8], bstg[0:8, :], idf32[0:8, 0:8]), waits=[t_l1, t_idf])
        t_bq = sch.emit("vector", lambda e: e.tensor_copy(
            bqpair[:], psA2[:, 0:8]), waits=[tt1])
        t_l2 = sch.dma("sync", bstg[:],
                       bkv_h.rearrange("(c p) o -> c (p o)", c=16), waits=[t_bq])
        tt2 = sch.emit("tensor", lambda e: e.transpose(
            psB2[:, 0:16], bstg[:], idf32[0:16, 0:16]), waits=[t_l2, t_idf, t_bq])
        t_bk = sch.emit("vector", lambda e: e.tensor_copy(
            bkpair[:], psB2[:, 0:8]), waits=[tt2])
        t_bv = sch.emit("vector", lambda e: e.tensor_copy(
            biav[:], psB2[:, 8:16]), waits=[tt2])
        for (dst, srcb, seg0, twt) in ((biaqA, bqpair, 0, t_bq),
                                       (biaqB, bqpair, 32, t_bq),
                                       (biakA, bkpair, 0, t_bk),
                                       (biakB, bkpair, 32, t_bk)):
            for h_ in range(2):
                for dd in range(2):
                    sch.emit("vector", lambda e, dst=dst, srcb=srcb, seg0=seg0,
                             h_=h_, dd=dd: e.tensor_copy(
                        dst[h_ * 64 + dd * 32:h_ * 64 + dd * 32 + 32, :],
                        srcb[h_ * 64 + seg0:h_ * 64 + seg0 + 32, :]), waits=[twt])
        t_bias = sch.tok("vector")
        t_on = sch.emit("gpsimd", lambda e: e.memset(onesf[:], 1.0))
        sch.emit("gpsimd", lambda e: e.memset(bm4[:], -4.0))
        for hh in range(2):
            for kt in range(16):
                sch.emit("gpsimd", lambda e, h=hh, k=kt: e.memset(
                    vhat[h][:, k * 66 + 64: k * 66 + 65], 1.0))
        tb1 = sch.emit("tensor", lambda e: e.matmul(
            psA[:, 0:512], onesf[:, 0:128], borow[:, 0:512], start=True, stop=True),
            waits=[t_bo, t_on])
        tb2 = sch.emit("tensor", lambda e: e.matmul(
            psB[:, 0:512], onesf[:, 0:128], borow[:, 512:1024], start=True,
            stop=True), waits=[t_bo, t_on])
        sch.emit("vector", lambda e: e.tensor_copy(bo_bc[:, 0:512], psA[:, 0:512]),
                 waits=[tb1])
        t_bobc = sch.emit("vector", lambda e: e.tensor_copy(
            bo_bc[:, 512:1024], psB[:, 0:512]), waits=[tb2])
        sch.flush()

        wo_loaded = False
        for s in range(SEQ_PER_CORE):
            row0 = s * S
            emit_ln_to_fm(sch, st, x_h, row0, 16, ln_fm, S, 0)
            sch.flush()

            for hp in range(8):
                c0 = hp * 128

                def dup_load(col_h, colbase, seg, dst_bf, wslot):
                    wf = w_f[wslot % 2]
                    toks = []
                    for h_ in range(2):
                        srcap = col_h[:, colbase + h_ * 64 + seg:
                                      colbase + h_ * 64 + seg + 32]
                        srcap = srcap.rearrange("(c p) m -> p c m", p=128)
                        dst = wf.ap().rearrange("p (c d m) -> p c d m", c=8, d=4)
                        for dd in range(2):
                            toks.append(sch.dma(
                                "sync", dst[:, :, h_ * 2 + dd, :], srcap,
                                waits=[wfree[wslot % 2]]))
                    return sch.emit("vector", lambda e, wf=wf, d=dst_bf:
                                    e.tensor_copy(d[:], wf[:]), waits=toks)

                wfree = [None, None]
                t1 = dup_load(wq_h, c0, 0, wqA, 0)
                wfree[0] = t1
                t2 = dup_load(wq_h, c0, 32, wqB, 1)
                wfree[1] = t2
                t3 = dup_load(wkv_h, c0, 0, wkA, 0)
                wfree[0] = t3
                t4 = dup_load(wkv_h, c0, 32, wkB, 1)
                wfree[1] = t4
                wfv = w_f[0]
                t5 = sch.dma("sync", wfv.ap().rearrange("p (c m) -> p c m", c=8),
                             wkv_h[:, D + c0:D + c0 + 128].rearrange(
                                 "(c p) m -> p c m", p=128), waits=[wfree[0]])
                t_wv = sch.emit("vector", lambda e: e.tensor_copy(wv_b[:], wfv[:]),
                                waits=[t5])
                tw = {"qA": t1, "qB": t2, "kA": t3, "kB": t4, "v": t_wv}
                sch.flush()

                bank_free = {id(psA): [], id(psB): [], id(psA2): [], id(psB2): []}

                def mmgroup(pb, wb, tc, extra_waits):
                    mmt = None
                    for c in range(8):
                        mmt = sch.emit("tensor", lambda e, c=c, tc=tc, wb=wb, pb=pb:
                                       e.matmul(
                            pb[:, 0:512], wb[:, c * 128:(c + 1) * 128],
                            ln_fm[:, c * S + tc * 512:c * S + (tc + 1) * 512],
                            start=(c == 0), stop=(c == 7)),
                            inc=(1 if c == 7 else 0),
                            waits=(bank_free[id(pb)] + extra_waits) if c == 0 else [])
                    return mmt

                for tc in range(4):
                    pa = [psA, psB][tc % 2]
                    pb2 = [psA2, psB2][tc % 2]
                    mA = mmgroup(pa, wqA, tc, [tw["qA"]])
                    mB = mmgroup(pb2, wqB, tc, [tw["qB"]])
                    tr1 = sch.emit("vector", lambda e, pa=pa, hp=hp, tc=tc:
                                   e.scalar_tensor_tensor(
                        ropeS[0][:], pa[:, 0:512], biaqA[:, hp:hp + 1],
                        csf[:, tc * 512:(tc + 1) * 512], ALU.add, ALU.mult),
                        waits=[mA, t_bias])
                    tr2 = sch.emit("vector", lambda e, pb2=pb2, hp=hp, tc=tc:
                                   e.scalar_tensor_tensor(
                        ropeS[1][:], pb2[:, 0:512], biaqB[:, hp:hp + 1],
                        scf[:, tc * 512:(tc + 1) * 512], ALU.add, ALU.mult),
                        waits=[mB, t_bias])
                    tq = sch.emit("vector", lambda e, tc=tc: e.tensor_tensor(
                        qr_b[:, tc * 512:(tc + 1) * 512], ropeS[0][:], ropeS[1][:],
                        ALU.add), waits=[tr1, tr2])
                    bank_free[id(pa)] = [tr1]
                    bank_free[id(pb2)] = [tr2]
                    mA = mmgroup(pa, wkA, tc, [tw["kA"]])
                    mB = mmgroup(pb2, wkB, tc, [tw["kB"]])
                    tr1 = sch.emit("vector", lambda e, pa=pa, hp=hp, tc=tc:
                                   e.scalar_tensor_tensor(
                        ropeS[0][:], pa[:, 0:512], biakA[:, hp:hp + 1],
                        csf[:, tc * 512:(tc + 1) * 512], ALU.add, ALU.mult),
                        waits=[mA])
                    tr2 = sch.emit("vector", lambda e, pb2=pb2, hp=hp, tc=tc:
                                   e.scalar_tensor_tensor(
                        ropeS[1][:], pb2[:, 0:512], biakB[:, hp:hp + 1],
                        scf[:, tc * 512:(tc + 1) * 512], ALU.add, ALU.mult),
                        waits=[mB])
                    tk = sch.emit("vector", lambda e, tc=tc: e.tensor_tensor(
                        kr_b[:, tc * 512:(tc + 1) * 512], ropeS[0][:], ropeS[1][:],
                        ALU.add), waits=[tr1, tr2])
                    bank_free[id(pa)] = [tr1]
                    bank_free[id(pb2)] = [tr2]
                    mV = mmgroup(pa, wv_b, tc, [tw["v"]])
                    tv = sch.emit("scalar", lambda e, pa=pa, hp=hp, tc=tc:
                                  e.activation(
                        v_b[:, tc * 512:(tc + 1) * 512], pa[:, 0:512], AF.Identity,
                        bias=biav[:, hp:hp + 1]), waits=[mV, t_bias])
                    bank_free[id(pa)] = [tv]
                t_v = sch.tok("scalar")
                sch.flush()

                for hh in range(2):
                    for kt in range(16):
                        pb = st["ptr"][kt % 2]
                        t_tr = sch.emit("tensor", lambda e, hh=hh, kt=kt, pb=pb:
                                        e.transpose(
                            pb[0:128, 0:64],
                            v_b[hh * 64:(hh + 1) * 64, kt * 128:(kt + 1) * 128],
                            st["id_bf"][hh * 64:(hh + 1) * 64,
                                        hh * 64:(hh + 1) * 64]),
                            waits=[st["ptr_free"][kt % 2], t_v])
                        t_cp = sch.emit("vector", lambda e, hh=hh, kt=kt, pb=pb:
                                        e.tensor_copy(
                            vhat[hh][:, kt * 66:kt * 66 + 64], pb[0:128, 0:64]),
                            waits=[t_tr])
                        st["ptr_free"][kt % 2] = t_cp
                sch.flush()

                for qc in range(4):
                    kt_hi = 4 * (qc + 1)
                    exp_done = [None, None]
                    pv_last = [None, None]
                    for kt in range(kt_hi):
                        sb = kt % 2
                        psx = [[psA, psA2][sb], [psB, psB2][sb]]
                        ebufs = [eA[sb], eB[sb]]
                        d = kt - 4 * qc
                        for hh in range(2):
                            tqk = sch.emit("tensor", lambda e, p=psx[hh], hh=hh,
                                           kt=kt, qc=qc: e.matmul(
                                p[:, 0:512],
                                kr_b[hh * 64:(hh + 1) * 64, kt * 128:(kt + 1) * 128],
                                qr_b[hh * 64:(hh + 1) * 64, qc * 512:(qc + 1) * 512],
                                start=True, stop=True, tile_position=(hh * 64, 0)),
                                waits=[exp_done[hh]])
                            texp = sch.emit("scalar", lambda e, p=psx[hh],
                                            d2=ebufs[hh]: e.activation(
                                d2[:], p[:, 0:512], AF.Exp, scale=0.125,
                                bias=bm4[:, 0:1]),
                                waits=[tqk, pv_last[hh]])
                            if d >= 0:
                                texp = sch.emit("vector", lambda e, d=d,
                                                d2=ebufs[hh]: e.tensor_tensor(
                                    d2[:], d2[:], dmask[:, d * 512:(d + 1) * 512],
                                    ALU.mult), waits=[texp, t_dm])
                            po = [poA, poB][hh]
                            tpv = sch.emit("tensor", lambda e, hh=hh, kt=kt, po=po,
                                           d2=ebufs[hh], last=(kt == kt_hi - 1):
                                           e.matmul(
                                po[0:65, 0:512], vhat[hh][:, kt * 66:kt * 66 + 65],
                                d2[:], start=(kt == 0), stop=last),
                                waits=[texp])
                            exp_done[hh] = texp
                            pv_last[hh] = tpv
                    sch.flush()
                    for hh in range(2):
                        po = [poA, poB][hh]
                        pn = [psA, psB][hh]
                        trc = sch.emit("vector", lambda e, po=po, hh=hh:
                                       e.reciprocal(rcp[0:1, hh * 512:(hh + 1) * 512],
                                                    po[64:65, 0:512]),
                                       waits=[pv_last[hh]])
                        tbc = sch.emit("tensor", lambda e, hh=hh, pn=pn: e.matmul(
                            pn[:, 0:512], onesf[:, 0:128],
                            rcp[0:1, hh * 512:(hh + 1) * 512], start=True, stop=True),
                            waits=[trc, exp_done[0], exp_done[1]])
                        tcb = sch.emit("vector", lambda e, hh=hh, pn=pn:
                                       e.tensor_copy(
                            sbc[:, hh * 512:(hh + 1) * 512], pn[:, 0:512]),
                            waits=[tbc])
                        sch.emit("vector", lambda e, po=po, hh=hh, qc=qc, hp=hp:
                                 e.tensor_tensor(
                            attn_fm[hh * 64:(hh + 1) * 64,
                                    hp * S + qc * 512:hp * S + (qc + 1) * 512],
                            po[0:64, 0:512], sbc[0:64, hh * 512:(hh + 1) * 512],
                            ALU.mult), waits=[tcb])
                    sch.flush()

            if not wo_loaded:
                wo_loaded = True
                prev = None
                for pc in range(8):
                    wf = w_f[pc % 2]
                    t_l = sch.dma("sync", wf[:],
                                  wo_h[pc * 128:(pc + 1) * 128, :],
                                  waits=[prev])
                    prev = sch.emit("vector", lambda e, pc=pc, wf=wf: e.tensor_copy(
                        wob[:, pc * 1024:(pc + 1) * 1024], wf[:]), waits=[t_l])
                sch.flush()

            obank_free = [None, None]
            for tt in range(16):
                p = tt % 2
                xb = xrt[p]
                yb = yt[p]
                t_x = sch.dma("sync", xb[:],
                              x_h[row0 + tt * 128:row0 + (tt + 1) * 128, :])
                mml = [None, None]
                for half in range(2):
                    pb = [psA, psB][half]
                    for pc in range(8):
                        mml[half] = sch.emit("tensor", lambda e, pc=pc, half=half,
                                             tt=tt, pb=pb: e.matmul(
                            pb[:, 0:512],
                            attn_fm[:, pc * S + tt * 128:pc * S + (tt + 1) * 128],
                            wob[:, pc * 1024 + half * 512:
                                pc * 1024 + half * 512 + 512],
                            start=(pc == 0), stop=(pc == 7)),
                            inc=(1 if pc == 7 else 0),
                            waits=[obank_free[half]] if pc == 0 else [])
                t_a1 = sch.emit("vector", lambda e, x=xb, y=yb: e.tensor_tensor(
                    y[:], x[:], bo_bc[:], ALU.add), waits=[t_x, t_bobc])
                t_y0 = sch.emit("vector", lambda e, y=yb: e.tensor_tensor(
                    y[:, 0:512], y[:, 0:512], psA[:, 0:512], ALU.add),
                    waits=[t_a1, mml[0]])
                t_y1 = sch.emit("vector", lambda e, y=yb: e.tensor_tensor(
                    y[:, 512:1024], y[:, 512:1024], psB[:, 0:512], ALU.add),
                    waits=[t_a1, mml[1]])
                obank_free[0] = t_y0
                obank_free[1] = t_y1
                sch.dma("sync", y_h[row0 + tt * 128:row0 + (tt + 1) * 128, :],
                        yb[:], waits=[t_y0, t_y1])
                if p == 1:
                    sch.flush()
            sch.flush()


def build_attn0():
    nc = bass.Bass("TRN2", target_bir_lowering=False)
    hs = {}
    hs["x_h"] = nc.dram_tensor("x", [SEQ_PER_CORE * S, D], F32,
                               kind="ExternalInput").ap()
    hs["wq_h"] = nc.dram_tensor("wq", [D, D], F32, kind="ExternalInput").ap()
    hs["wkv_h"] = nc.dram_tensor("wkv", [D, 2 * D], F32, kind="ExternalInput").ap()
    hs["wo_h"] = nc.dram_tensor("wo", [D, D], F32, kind="ExternalInput").ap()
    hs["bq_h"] = nc.dram_tensor("bq", [D, 1], F32, kind="ExternalInput").ap()
    hs["bkv_h"] = nc.dram_tensor("bkv", [2 * D, 1], F32, kind="ExternalInput").ap()
    hs["bo_h"] = nc.dram_tensor("bo", [1, D], F32, kind="ExternalInput").ap()
    hs["cs_h"] = nc.dram_tensor("csfull", [128, S], F32, kind="ExternalInput").ap()
    hs["sc_h"] = nc.dram_tensor("scfull", [128, S], F32, kind="ExternalInput").ap()
    hs["id_h"] = nc.dram_tensor("ident", [128, 128], F32, kind="ExternalInput").ap()
    hs["dm_h"] = nc.dram_tensor("diagmask", [128, 4 * 512], BF16,
                                kind="ExternalInput").ap()
    hs["y_h"] = nc.dram_tensor("y", [SEQ_PER_CORE * S, D], F32,
                               kind="ExternalOutput").ap()
    with ExitStack() as ctx:
        sch = Sch(nc, ctx)
        emit_attn0(nc, sch, ctx, **hs)
        t_fin = sch.tok("sync")
        sch.emit("gpsimd", lambda e: None, inc=1, waits=[t_fin])
        sch.flush()
    return nc


def emit_attn1(nc, sch, ctx, x_h, y_h, wq_h, wk_h, wv_h, wo_h, wf_h, id_h, dm_h):
    if True:
        st = make_ln_state(nc, ctx)
        ln_fm = sbuf(nc, ctx, "lnfm", [128, 8 * S], BF16)
        attn_fm = sbuf(nc, ctx, "attnfm", [128, 8 * S], BF16)
        w_f = [sbuf(nc, ctx, f"wf{i}", [128, 1024], F32) for i in range(2)]
        wq_b = sbuf(nc, ctx, "wqb", [128, 1024], BF16)
        wk_b = sbuf(nc, ctx, "wkb", [128, 1024], BF16)
        wv_b = sbuf(nc, ctx, "wvb", [128, 1024], BF16)
        wfb = sbuf(nc, ctx, "wfb", [128, 128], BF16)
        qhat = sbuf(nc, ctx, "qhat", [68, S], BF16)
        khat = sbuf(nc, ctx, "khat", [68, S], BF16)
        v_b = sbuf(nc, ctx, "vb", [128, S], BF16)
        vhat = [sbuf(nc, ctx, f"vhat{h}", [128, 16 * 66], BF16) for h in range(2)]
        U1 = sbuf(nc, ctx, "U1", [128, S], F32)
        U2 = sbuf(nc, ctx, "U2", [128, S], F32)
        Ubf = sbuf(nc, ctx, "Ubf", [128, S], BF16)
        lst = [sbuf(nc, ctx, f"lst{i}", [128, 16], F32) for i in range(4)]
        lsig = sbuf(nc, ctx, "lsig", [128, 16], F32)
        eA = [sbuf(nc, ctx, f"eA{i}", [128, 512], BF16) for i in range(2)]
        dmask = sbuf(nc, ctx, "dmask", [128, 4 * 512], F32)
        rcp = sbuf(nc, ctx, "rcp", [1, 1024], F32)
        sbc = sbuf(nc, ctx, "sbc", [128, 512], F32)
        onesf = sbuf(nc, ctx, "onesf", [1, 128], F32)
        bm4 = sbuf(nc, ctx, "bm4", [128, 1], F32)
        idf32 = sbuf(nc, ctx, "idf32C", [128, 128], F32)
        wob = sbuf(nc, ctx, "wob", [128, 8 * 1024], BF16)
        xrt = [sbuf(nc, ctx, f"xrt{i}", [128, D], F32) for i in range(2)]
        yt = [sbuf(nc, ctx, f"yt{i}", [128, D], F32) for i in range(2)]

        psA = psum(nc, ctx, "psA", [128, 512], F32)
        psB = psum(nc, ctx, "psB", [128, 512], F32)
        psA2 = psum(nc, ctx, "psA2", [128, 512], F32)
        psB2 = psum(nc, ctx, "psB2", [128, 512], F32)
        poA = psum(nc, ctx, "poA", [128, 512], F32)
        poB = psum(nc, ctx, "poB", [128, 512], F32)
        pT = psum(nc, ctx, "pT", [128, 1024], BF16)
        pT2 = psum(nc, ctx, "pT2", [128, 1024], BF16)
        st["ptr"] = [PHalf(pT, 0), PHalf(pT2, 0)]

        t_dm = sch.dma("sync", dmask[:], dm_h)
        emit_load_ident(sch, st, id_h)
        t_idf = sch.dma("sync", idf32[:], id_h)
        t_on = sch.emit("gpsimd", lambda e: e.memset(onesf[:], 1.0))
        sch.emit("gpsimd", lambda e: e.memset(bm4[:], -4.0))
        t_z = sch.emit("gpsimd", lambda e: e.memset(U2[0:16, :], 0.0))
        t_o1 = sch.emit("gpsimd", lambda e: e.memset(qhat[64:68, :], 1.0))
        t_o2 = sch.emit("gpsimd", lambda e: e.memset(khat[64:68, :], 1.0))
        for hh in range(2):
            for kt in range(16):
                sch.emit("gpsimd", lambda e, h=hh, k=kt: e.memset(
                    vhat[h][:, k * 66 + 64: k * 66 + 65], 1.0))
        t_wf = sch.dma("sync", w_f[0].ap().rearrange(
            "p (c m) -> p c m", c=8)[:, :, 0:16],
            wf_h.rearrange("(c p) m -> p c m", p=128))
        t_wfb = sch.emit("vector", lambda e: e.tensor_copy(
            wfb.ap().rearrange("p (c m) -> p c m", c=8),
            w_f[0].ap().rearrange("p (c m) -> p c m", c=8)[:, :, 0:16]),
            waits=[t_wf])
        st["xin_free"][0] = t_wfb
        wo_toks = []
        for pc in range(8):
            wfs = w_f[pc % 2]
            t_l = sch.dma("sync", wfs[:], wo_h[pc * 128:(pc + 1) * 128, :],
                          waits=[wo_toks[-1] if wo_toks else t_wfb])
            t_c = sch.emit("vector", lambda e, pc=pc, wfs=wfs: e.tensor_copy(
                wob[:, pc * 1024:(pc + 1) * 1024], wfs[:]), waits=[t_l])
            wo_toks.append(t_c)
        sch.flush()

        for s in range(SEQ_PER_CORE):
            row0 = s * S
            emit_ln_to_fm(sch, st, x_h, row0, 16, ln_fm, S, 0)
            sch.flush()

            for tt in range(16):
                mmt = None
                for c in range(8):
                    mmt = sch.emit("tensor", lambda e, c=c, tt=tt: e.matmul(
                        psB2[:, 0:16],
                        ln_fm[:, c * S + tt * 128:c * S + (tt + 1) * 128],
                        wfb[:, c * 16:(c + 1) * 16],
                        start=(c == 0), stop=(c == 7)),
                        inc=(1 if c == 7 else 0))
                t_a = sch.emit("scalar", lambda e: e.activation(
                    lst[0][:], psB2[:, 0:16], AF.Abs), waits=[mmt])
                t_e = sch.emit("scalar", lambda e: e.activation(
                    lst[1][:], lst[0][:], AF.Exp, scale=-1.0), waits=[t_a])
                t_l = sch.emit("scalar", lambda e: e.activation(
                    lst[2][:], lst[1][:], AF.Ln, bias=1.0), waits=[t_e])
                t_m = sch.emit("vector", lambda e: e.tensor_scalar(
                    lst[3][:], psB2[:, 0:16], 0.0, None, ALU.min), waits=[mmt])
                t_s = sch.emit("vector", lambda e: e.tensor_tensor(
                    lsig[:], lst[3][:], lst[2][:], ALU.subtract),
                    waits=[t_l, t_m])
                t_tr = sch.emit("tensor", lambda e: e.transpose(
                    psA2[0:16, 0:128], lsig[:], idf32[:]),
                    waits=[t_s, t_idf])
                t_cp = sch.emit("vector", lambda e, tt=tt: e.tensor_copy(
                    U1[0:16, tt * 128:(tt + 1) * 128], psA2[0:16, 0:128]),
                    waits=[t_tr])
                sch.flush()
            t_scan = sch.emit("vector", lambda e: e.tensor_tensor_scan(
                U1[32:48, :], U1[0:16, :], U2[0:16, :], 0.0, ALU.add, ALU.add),
                waits=[t_z, sch.tok("vector")])
            t_hi = sch.emit("vector", lambda e: e.tensor_copy(
                Ubf[0:16, :], U1[32:48, :]), waits=[t_scan])
            t_hf = sch.emit("vector", lambda e: e.tensor_copy(
                U2[32:48, :], Ubf[0:16, :]), waits=[t_hi])
            t_lo = sch.emit("vector", lambda e: e.tensor_tensor(
                U1[64:80, :], U1[32:48, :], U2[32:48, :], ALU.subtract),
                waits=[t_hf])
            t_lob = sch.emit("vector", lambda e: e.tensor_copy(
                Ubf[64:80, :], U1[64:80, :]), waits=[t_lo])
            t_nhi = sch.emit("vector", lambda e: e.tensor_scalar(
                Ubf[32:48, :], Ubf[0:16, :], -1.0, None, ALU.mult), waits=[t_hi])
            t_nlo = sch.emit("vector", lambda e: e.tensor_scalar(
                Ubf[96:112, :], Ubf[64:80, :], -1.0, None, ALU.mult),
                waits=[t_lob])
            sch.flush()

            for hp in range(8):
                c0 = hp * 128
                tws = []
                for (w_hx, dstb) in ((wq_h, wq_b), (wk_h, wk_b), (wv_h, wv_b)):
                    sl = len(tws) % 2
                    t_w = sch.dma("sync",
                                  w_f[sl].ap().rearrange("p (c m) -> p c m", c=8),
                                  w_hx[:, c0:c0 + 128].rearrange(
                                      "(c p) m -> p c m", p=128),
                                  waits=[st.get(f"wfree{sl}")])
                    t_c = sch.emit("vector", lambda e, sl=sl, d=dstb: e.tensor_copy(
                        d[:], w_f[sl][:]), waits=[t_w])
                    st[f"wfree{sl}"] = t_c
                    tws.append(t_c)
                sch.flush()

                bank_free = {id(psA): [], id(psB): []}

                def mmgroup(pb, wb, tc, extra):
                    mmt = None
                    for c in range(8):
                        mmt = sch.emit("tensor", lambda e, c=c, tc=tc, wb=wb,
                                       pb=pb: e.matmul(
                            pb[:, 0:512], wb[:, c * 128:(c + 1) * 128],
                            ln_fm[:, c * S + tc * 512:c * S + (tc + 1) * 512],
                            start=(c == 0), stop=(c == 7)),
                            inc=(1 if c == 7 else 0),
                            waits=(bank_free[id(pb)] + extra) if c == 0 else [])
                    return mmt

                for hh in range(2):
                    h_abs = 2 * hp + hh
                    r0 = hh * 64
                    for tc in range(4):
                        pb = [psA, psB][tc % 2]
                        mq = mmgroup(pb, wq_b, tc, [tws[0]])
                        tq = sch.emit("scalar", lambda e, pb=pb, tc=tc, r0=r0:
                                      e.activation(
                            qhat[0:64, tc * 512:(tc + 1) * 512],
                            pb[r0:r0 + 64, 0:512], AF.Identity, scale=0.125),
                            waits=[mq])
                        bank_free[id(pb)] = [tq]
                        mk = mmgroup(pb, wk_b, tc, [tws[1]])
                        tk = sch.emit("scalar", lambda e, pb=pb, tc=tc, r0=r0:
                                      e.activation(
                            khat[0:64, tc * 512:(tc + 1) * 512],
                            pb[r0:r0 + 64, 0:512], AF.Identity),
                            waits=[mk])
                        bank_free[id(pb)] = [tk]
                        if hh == 0:
                            mv = mmgroup(pb, wv_b, tc, [tws[2]])
                            tv = sch.emit("scalar", lambda e, pb=pb, tc=tc:
                                          e.activation(
                                v_b[:, tc * 512:(tc + 1) * 512], pb[:, 0:512],
                                AF.Identity), waits=[mv])
                            bank_free[id(pb)] = [tv]
                    t_v = sch.tok("scalar")
                    t_r1 = sch.dma("sync", qhat[64:65, :],
                                   Ubf[h_abs:h_abs + 1, :], waits=[t_hi])
                    t_r2 = sch.dma("sync", qhat[66:67, :],
                                   Ubf[64 + h_abs:64 + h_abs + 1, :],
                                   waits=[t_lob])
                    t_r3 = sch.dma("sync", khat[65:66, :],
                                   Ubf[32 + h_abs:32 + h_abs + 1, :],
                                   waits=[t_nhi])
                    t_r4 = sch.dma("sync", khat[67:68, :],
                                   Ubf[96 + h_abs:96 + h_abs + 1, :],
                                   waits=[t_nlo])
                    sch.flush()

                    if hh == 0:
                        for h2 in range(2):
                            for kt in range(16):
                                pb2 = st["ptr"][kt % 2]
                                t_tr = sch.emit("tensor", lambda e, h2=h2, kt=kt,
                                                pb2=pb2: e.transpose(
                                    pb2[0:128, 0:64],
                                    v_b[h2 * 64:(h2 + 1) * 64,
                                        kt * 128:(kt + 1) * 128],
                                    st["id_bf"][h2 * 64:(h2 + 1) * 64,
                                                h2 * 64:(h2 + 1) * 64]),
                                    waits=[st["ptr_free"][kt % 2], t_v])
                                t_cp = sch.emit("vector", lambda e, h2=h2, kt=kt,
                                                pb2=pb2: e.tensor_copy(
                                    vhat[h2][:, kt * 66:kt * 66 + 64],
                                    pb2[0:128, 0:64]), waits=[t_tr])
                                st["ptr_free"][kt % 2] = t_cp
                        sch.flush()

                    for qc in range(4):
                        kt_hi = 4 * (qc + 1)
                        exp_done = None
                        pv_last = None
                        for kt in range(kt_hi):
                            sb = kt % 2
                            px = [psA, psA2][sb]
                            ebuf = eA[sb]
                            d = kt - 4 * qc
                            tqk = sch.emit("tensor", lambda e, px=px, kt=kt,
                                           qc=qc: e.matmul(
                                px[:, 0:512],
                                khat[0:68, kt * 128:(kt + 1) * 128],
                                qhat[0:68, qc * 512:(qc + 1) * 512],
                                start=True, stop=True),
                                waits=[exp_done, t_r4 if kt == 0 else None])
                            tmask = None
                            if d >= 0:
                                tmask = sch.emit("vector", lambda e, px=px, d=d:
                                                 e.tensor_tensor(
                                    px[:, 0:512], px[:, 0:512],
                                    dmask[:, d * 512:(d + 1) * 512], ALU.add),
                                    waits=[tqk, t_dm])
                            texp = sch.emit("scalar", lambda e, px=px, d2=ebuf:
                                            e.activation(
                                d2[:], px[:, 0:512], AF.Exp, scale=1.0,
                                bias=bm4[:, 0:1]),
                                waits=[tmask or tqk, pv_last])
                            tpv = sch.emit("tensor", lambda e, hh=hh, kt=kt,
                                           d2=ebuf, last=(kt == kt_hi - 1):
                                           e.matmul(
                                poA[0:65, 0:512],
                                vhat[hh][:, kt * 66:kt * 66 + 65],
                                d2[:], start=(kt == 0), stop=last),
                                waits=[texp])
                            exp_done = texp
                            pv_last = tpv
                            sch.flush()
                        trc = sch.emit("vector", lambda e: e.reciprocal(
                            rcp[0:1, 0:512], poA[64:65, 0:512]), waits=[pv_last])
                        tbc = sch.emit("tensor", lambda e: e.matmul(
                            psB[:, 0:512], onesf[:, 0:128], rcp[0:1, 0:512],
                            start=True, stop=True), waits=[trc, exp_done])
                        tcb = sch.emit("vector", lambda e: e.tensor_copy(
                            sbc[:], psB[:, 0:512]), waits=[tbc])
                        sch.emit("vector", lambda e, hh=hh, qc=qc, hp=hp:
                                 e.tensor_tensor(
                            attn_fm[hh * 64:(hh + 1) * 64,
                                    hp * S + qc * 512:hp * S + (qc + 1) * 512],
                            poA[0:64, 0:512], sbc[0:64, :], ALU.mult),
                            waits=[tcb])
                        sch.flush()

            obank_free = [None, None]
            for tt in range(16):
                p = tt % 2
                xb = xrt[p]
                yb = yt[p]
                t_x = sch.dma("sync", xb[:],
                              x_h[row0 + tt * 128:row0 + (tt + 1) * 128, :])
                mml = [None, None]
                for half in range(2):
                    pb = [psA, psB][half]
                    for pc in range(8):
                        mml[half] = sch.emit("tensor", lambda e, pc=pc, half=half,
                                             tt=tt, pb=pb: e.matmul(
                            pb[:, 0:512],
                            attn_fm[:, pc * S + tt * 128:pc * S + (tt + 1) * 128],
                            wob[:, pc * 1024 + half * 512:
                                pc * 1024 + half * 512 + 512],
                            start=(pc == 0), stop=(pc == 7)),
                            inc=(1 if pc == 7 else 0),
                            waits=[obank_free[half]] if pc == 0 else [])
                t_y0 = sch.emit("vector", lambda e, x=xb, y=yb: e.tensor_tensor(
                    y[:, 0:512], x[:, 0:512], psA[:, 0:512], ALU.add),
                    waits=[t_x, mml[0]])
                t_y1 = sch.emit("vector", lambda e, x=xb, y=yb: e.tensor_tensor(
                    y[:, 512:1024], x[:, 512:1024], psB[:, 0:512], ALU.add),
                    waits=[t_x, mml[1]])
                obank_free[0] = t_y0
                obank_free[1] = t_y1
                sch.dma("sync", y_h[row0 + tt * 128:row0 + (tt + 1) * 128, :],
                        yb[:], waits=[t_y0, t_y1])
                if p == 1:
                    sch.flush()
            sch.flush()


def build_attn1():
    nc = bass.Bass("TRN2", target_bir_lowering=False)
    hs = {}
    hs["x_h"] = nc.dram_tensor("x", [SEQ_PER_CORE * S, D], F32,
                               kind="ExternalInput").ap()
    hs["wq_h"] = nc.dram_tensor("wq", [D, D], F32, kind="ExternalInput").ap()
    hs["wk_h"] = nc.dram_tensor("wk", [D, D], F32, kind="ExternalInput").ap()
    hs["wv_h"] = nc.dram_tensor("wv", [D, D], F32, kind="ExternalInput").ap()
    hs["wo_h"] = nc.dram_tensor("wo", [D, D], F32, kind="ExternalInput").ap()
    hs["wf_h"] = nc.dram_tensor("wf", [D, H], F32, kind="ExternalInput").ap()
    hs["id_h"] = nc.dram_tensor("ident", [128, 128], F32, kind="ExternalInput").ap()
    hs["dm_h"] = nc.dram_tensor("diagmaskA", [128, 4 * 512], F32,
                                kind="ExternalInput").ap()
    hs["y_h"] = nc.dram_tensor("y", [SEQ_PER_CORE * S, D], F32,
                               kind="ExternalOutput").ap()
    with ExitStack() as ctx:
        sch = Sch(nc, ctx)
        emit_attn1(nc, sch, ctx, **hs)
        t_fin = sch.tok("sync")
        sch.emit("gpsimd", lambda e: None, inc=1, waits=[t_fin])
        sch.flush()
    return nc


def _diag_masks_add():
    m = np.zeros((128, 4, 512), np.float32)
    p = np.arange(128)[:, None]
    o = np.arange(512)[None, :]
    for d in range(4):
        m[:, d, :] = np.where(p > o - 128 * d, NEG, 0.0)
    return m.reshape(128, 4 * 512)


def run_attn1(x, Wq, Wk, Wv, Wo, Wf):
    nc = build_attn1()
    in_maps = []
    Wq = np.ascontiguousarray(np.asarray(Wq, np.float32))
    Wk = np.ascontiguousarray(np.asarray(Wk, np.float32))
    Wv = np.ascontiguousarray(np.asarray(Wv, np.float32))
    Wo = np.ascontiguousarray(np.asarray(Wo, np.float32))
    Wf = np.ascontiguousarray(np.asarray(Wf, np.float32))
    dm = _diag_masks_add()
    for c in range(NCORES):
        xs = np.ascontiguousarray(
            x[c * SEQ_PER_CORE:(c + 1) * SEQ_PER_CORE].reshape(SEQ_PER_CORE * S, D))
        in_maps.append(dict(x=xs, wq=Wq, wk=Wk, wv=Wv, wo=Wo, wf=Wf,
                            diagmaskA=dm, ident=_ident()))
    res = run_bass_kernel_spmd(nc, in_maps, core_ids=list(range(NCORES)))
    return np.concatenate([r["y"].reshape(SEQ_PER_CORE, S, D)
                           for r in res.results], 0)


JW = 256
T_STEPS = 16


def build_ffn0():
    nc = bass.Bass("TRN2", target_bir_lowering=False)
    NROW = T_STEPS * JW
    x_h = nc.dram_tensor("x", [NROW, D], F32, kind="ExternalInput").ap()
    w_hs = {}
    for nm in ("wf", "uf", "wi", "ui", "wc", "uc", "wo", "uo"):
        w_hs[nm] = nc.dram_tensor(nm, [D, D], F32, kind="ExternalInput").ap()
    b_hs = {}
    for nm in ("bf", "bi", "bc", "bo", "inc"):
        b_hs[nm] = nc.dram_tensor(nm, [D, 1], F32, kind="ExternalInput").ap()
    id_h = nc.dram_tensor("ident", [128, 128], F32, kind="ExternalInput").ap()
    y_h = nc.dram_tensor("y", [NROW, D], F32, kind="ExternalOutput").ap()

    with ExitStack() as ctx:
        sch = Sch(nc, ctx)
        xg_fm = sbuf(nc, ctx, "xgfm", [128, 8 * NROW], BF16)
        wT = {nm: sbuf(nc, ctx, f"wT_{nm}", [128, 64 * 128], BF16)
              for nm in ("wf", "uf", "wi", "ui", "wc", "uc")}
        c_fm = sbuf(nc, ctx, "cfm", [128, 8 * JW], F32)
        c_prev = sbuf(nc, ctx, "cprev", [128, 8 * JW], BF16)
        biasp = sbuf(nc, ctx, "biasp", [128, 4 * 8], F32)
        idf32 = sbuf(nc, ctx, "idf32B", [128, 128], F32)
        pg = [psum(nc, ctx, f"pg{i}", [128, 512], F32) for i in range(6)]
        pT = psum(nc, ctx, "pTB", [128, 1024], BF16)
        pT2 = psum(nc, ctx, "pT2B", [128, 1024], BF16)

        with ExitStack() as c1:
            st = make_ln_state(nc, c1, "B")
            st["ptr"] = [PHalf(pT, 0), PHalf(pT2, 0)]
            wstg = st["xin"]
            wstgb = st["ln_tm"]
            bstg = sbuf(nc, c1, "bstgB", [32, 128], F32)

            t_idf = sch.dma("sync", idf32[:], id_h)
            emit_load_ident(sch, st, id_h)
            for i, nm in enumerate(("bf", "bi", "bc", "inc")):
                sch.dma("sync", bstg[i * 8:(i + 1) * 8, :],
                        b_hs[nm].rearrange("(c p) o -> c (p o)", c=8))
            t_bl = sch.tok("sync")
            t_btr = sch.emit("tensor", lambda e: e.transpose(
                pg[0][:, 0:32], bstg[:], idf32[0:32, 0:32]), waits=[t_bl, t_idf])
            t_bias = sch.emit("vector", lambda e: e.tensor_copy(
                biasp[:], pg[0][:, 0:32]), waits=[t_btr])
            sch.flush()

            ptr_free = [None, None]
            for wi_, nm in enumerate(("wf", "uf", "wi", "ui", "wc", "uc")):
                for mc in range(8):
                    sl = (wi_ * 8 + mc) % 2
                    t_l = sch.dma("sync", wstg[sl][:],
                                  w_hs[nm][mc * 128:(mc + 1) * 128, :],
                                  waits=[st.get(f"wstg_free{sl}")])
                    t_c = sch.emit("vector", lambda e, sl=sl: e.tensor_copy(
                        wstgb[sl][:], wstg[sl][:]),
                        waits=[t_l, st.get(f"wstgb_free{sl}")])
                    st[f"wstg_free{sl}"] = t_c
                    last = None
                    for kc in range(8):
                        pb = st["ptr"][kc % 2]
                        t_tr = sch.emit("tensor", lambda e, sl=sl, kc=kc, pb=pb:
                                        e.transpose(
                            pb[0:128, 0:128], wstgb[sl][:, kc * 128:(kc + 1) * 128],
                            st["id_bf"][:]), waits=[t_c, ptr_free[kc % 2]])
                        t_cp = sch.emit("vector", lambda e, nm=nm, kc=kc, mc=mc,
                                        pb=pb: e.tensor_copy(
                            wT[nm][:, (kc * 8 + mc) * 128:(kc * 8 + mc + 1) * 128],
                            pb[0:128, 0:128]), waits=[t_tr])
                        ptr_free[kc % 2] = t_cp
                        last = t_cp
                    st[f"wstgb_free{sl}"] = last
                    st["xin_free"][sl] = st[f"wstg_free{sl}"]
                    st["ln_free"][sl] = last
                sch.flush()

            st["ptr_free"] = ptr_free
            emit_ln_to_fm(sch, st, x_h, 0, 32, xg_fm, NROW, 0)
            sch.flush()

        with ExitStack() as c2:
            gsc = [sbuf(nc, c2, f"gsc{i}", [128, JW], F32) for i in range(3)]
            tmp1 = sbuf(nc, c2, "tmp1", [128, JW], F32)
            tmp2 = sbuf(nc, c2, "tmp2", [128, JW], F32)
            t_ms = sch.emit("gpsimd", lambda e: e.memset(c_fm[:], 0.0))
            for mc in range(8):
                sch.emit("vector", lambda e, mc=mc: e.tensor_scalar(
                    c_fm[:, mc * JW:(mc + 1) * JW], c_fm[:, mc * JW:(mc + 1) * JW],
                    biasp[:, 24 + mc:24 + mc + 1], None, ALU.add),
                    waits=[t_ms, t_bias])
            sch.flush()

            last_up = sch.tok("vector")
            last_mm_prev = None
            for t_ in range(T_STEPS):
                t_cpv = sch.emit("vector", lambda e: e.tensor_copy(
                    c_prev[:], c_fm[:]), waits=[last_up, last_mm_prev])
                evac_done = {}
                upd_done = {}
                for mc in range(8):
                    for gi_, (wnm, unm, act, bcol) in enumerate((
                            ("wf", "uf", AF.Sigmoid, 0),
                            ("wi", "ui", AF.Sigmoid, 1),
                            ("wc", "uc", AF.Tanh, 2))):
                        bank = pg[(mc % 2) * 3 + gi_]
                        mm = None
                        for kc in range(8):
                            mm = sch.emit("tensor", lambda e, wnm=wnm, kc=kc,
                                          mc=mc, t_=t_, bank=bank: e.matmul(
                                bank[:, 0:JW],
                                wT[wnm][:, (kc * 8 + mc) * 128:
                                        (kc * 8 + mc + 1) * 128],
                                xg_fm[:, kc * NROW + t_ * JW:
                                      kc * NROW + (t_ + 1) * JW],
                                start=(kc == 0), stop=False), inc=0,
                                waits=[evac_done.get(id(bank)), t_cpv]
                                if kc == 0 else [])
                        for kc in range(8):
                            mm = sch.emit("tensor", lambda e, unm=unm, kc=kc,
                                          mc=mc, bank=bank: e.matmul(
                                bank[:, 0:JW],
                                wT[unm][:, (kc * 8 + mc) * 128:
                                        (kc * 8 + mc + 1) * 128],
                                c_prev[:, kc * JW:(kc + 1) * JW],
                                start=False, stop=(kc == 7)),
                                inc=(1 if kc == 7 else 0))
                        evac_done[id(bank)] = sch.emit(
                            "scalar", lambda e, gi_=gi_, mc=mc, act=act,
                            bcol=bcol, bank=bank: e.activation(
                                gsc[gi_][:], bank[:, 0:JW], act,
                                bias=biasp[:, bcol * 8 + mc:bcol * 8 + mc + 1]),
                            waits=[mm, upd_done.get(gi_)])
                    tk1 = sch.emit("vector", lambda e, mc=mc: e.tensor_tensor(
                        tmp1[:], gsc[0][:], c_fm[:, mc * JW:(mc + 1) * JW],
                        ALU.mult), waits=[evac_done[id(pg[(mc % 2) * 3])], t_cpv])
                    tk2 = sch.emit("vector", lambda e: e.tensor_tensor(
                        tmp2[:], gsc[1][:], gsc[2][:], ALU.mult),
                        waits=[evac_done[id(pg[(mc % 2) * 3 + 1])],
                               evac_done[id(pg[(mc % 2) * 3 + 2])]])
                    t_up = sch.emit("vector", lambda e, mc=mc: e.tensor_tensor(
                        c_fm[:, mc * JW:(mc + 1) * JW], tmp1[:], tmp2[:], ALU.add),
                        waits=[tk1, tk2])
                    for gi_ in range(3):
                        upd_done[gi_] = t_up
                last_up = sch.tok("vector")
                last_mm_prev = sch.tok("tensor")
            sch.emit("vector", lambda e: e.tensor_copy(c_prev[:], c_fm[:]),
                     waits=[last_up, last_mm_prev])
            sch.flush()

        with ExitStack() as c3:
            wstg = [sbuf(nc, c3, f"wstgC{i}", [128, 1024], F32) for i in range(2)]
            wstgb = [sbuf(nc, c3, f"wstgbC{i}", [128, 1024], BF16) for i in range(2)]
            idb = sbuf(nc, c3, "idbC", [128, 128], BF16)
            borow = sbuf(nc, c3, "borowC", [1, D], F32)
            onesf = sbuf(nc, c3, "onesfC", [1, 128], F32)
            bo_bc = sbuf(nc, c3, "bobcC", [128, D], F32)
            cu_bo = [sbuf(nc, c3, f"cubo{i}", [128, D], F32) for i in range(2)]
            hsc = sbuf(nc, c3, "hsc", [128, D], F32)
            ptr_free = [None, None]
            t_idb = sch.emit("vector", lambda e: e.tensor_copy(idb[:], idf32[:]))
            woT, uoT = wT["wf"], wT["uf"]
            stg_free = [None, None]
            stgb_free = [None, None]
            for wi_, (nm, dst) in enumerate((("wo", woT), ("uo", uoT))):
                for mc in range(8):
                    sl = (wi_ * 8 + mc) % 2
                    t_l = sch.dma("sync", wstg[sl][:],
                                  w_hs[nm][mc * 128:(mc + 1) * 128, :],
                                  waits=[stg_free[sl]])
                    t_c = sch.emit("vector", lambda e, sl=sl: e.tensor_copy(
                        wstgb[sl][:], wstg[sl][:]), waits=[t_l, stgb_free[sl]])
                    stg_free[sl] = t_c
                    last = None
                    for kc in range(8):
                        pb = [PHalf(pT, 0), PHalf(pT2, 0)][kc % 2]
                        t_tr = sch.emit("tensor", lambda e, sl=sl, kc=kc, pb=pb:
                                        e.transpose(
                            pb[0:128, 0:128], wstgb[sl][:, kc * 128:(kc + 1) * 128],
                            idb[:]), waits=[t_c, ptr_free[kc % 2], t_idb])
                        t_cp = sch.emit("vector", lambda e, dst=dst, kc=kc, mc=mc,
                                        pb=pb: e.tensor_copy(
                            dst[:, (kc * 8 + mc) * 128:(kc * 8 + mc + 1) * 128],
                            pb[0:128, 0:128]), waits=[t_tr])
                        ptr_free[kc % 2] = t_cp
                        last = t_cp
                    stgb_free[sl] = last
            t_bo = sch.dma("sync", borow[:], b_hs["bo"].rearrange("p o -> o p"))
            t_on = sch.emit("gpsimd", lambda e: e.memset(onesf[:], 1.0))
            tb1 = sch.emit("tensor", lambda e: e.matmul(
                pg[0][:, 0:512], onesf[:, 0:128], borow[:, 0:512], start=True,
                stop=True), waits=[t_bo, t_on])
            tb2 = sch.emit("tensor", lambda e: e.matmul(
                pg[1][:, 0:512], onesf[:, 0:128], borow[:, 512:1024], start=True,
                stop=True), waits=[t_bo, t_on])
            sch.emit("vector", lambda e: e.tensor_copy(bo_bc[:, 0:512],
                                                       pg[0][:, 0:512]), waits=[tb1])
            t_bobc = sch.emit("vector", lambda e: e.tensor_copy(
                bo_bc[:, 512:1024], pg[1][:, 0:512]), waits=[tb2])
            sch.flush()

            for jt in range(2):
                mml = [None, None]
                for half in range(2):
                    bank = pg[2 + half]
                    for kc in range(8):
                        mml[half] = sch.emit("tensor", lambda e, kc=kc, jt=jt,
                                             half=half, bank=bank: e.matmul(
                            bank[:, 0:512],
                            c_prev[:, kc * JW + jt * 128:kc * JW + (jt + 1) * 128],
                            uoT[:, (kc * 8) * 128 + half * 512:
                                (kc * 8) * 128 + half * 512 + 512],
                            start=(kc == 0), stop=(kc == 7)),
                            inc=(1 if kc == 7 else 0))
                t_c1 = sch.emit("vector", lambda e, jt=jt: e.tensor_tensor(
                    cu_bo[jt][:, 0:512], pg[2][:, 0:512], bo_bc[:, 0:512], ALU.add),
                    waits=[mml[0], t_bobc])
                t_c2 = sch.emit("vector", lambda e, jt=jt: e.tensor_tensor(
                    cu_bo[jt][:, 512:1024], pg[3][:, 0:512], bo_bc[:, 512:1024],
                    ALU.add), waits=[mml[1], t_bobc])
            sch.flush()
            xrt = [wstg[0], bo_bc]
            yt = wstg[1]

            obank_free = [None, None]
            prev_th = None
            prev_store = None
            xrt_free = [sch.tok("vector"), sch.tok("vector")]
            for rt in range(32):
                p = rt % 2
                xb = xrt[p]
                yb = yt
                jt = rt % 2
                t_x = sch.dma("sync", xb[:], x_h[rt * 128:(rt + 1) * 128, :],
                              waits=[xrt_free[p]])
                mml = [None, None]
                for half in range(2):
                    bank = pg[4 + half]
                    for kc in range(8):
                        mml[half] = sch.emit("tensor", lambda e, kc=kc, rt=rt,
                                             half=half, bank=bank: e.matmul(
                            bank[:, 0:512],
                            xg_fm[:, kc * NROW + rt * 128:kc * NROW + (rt + 1) * 128],
                            woT[:, kc * 1024 + half * 512:kc * 1024 + half * 512
                                + 512],
                            start=(kc == 0), stop=(kc == 7)),
                            inc=(1 if kc == 7 else 0),
                            waits=[obank_free[half]] if kc == 0 else [])
                t_h0 = sch.emit("vector", lambda e, jt=jt: e.tensor_tensor(
                    hsc[:, 0:512], pg[4][:, 0:512], cu_bo[jt][:, 0:512], ALU.add),
                    waits=[mml[0], prev_th])
                t_h1 = sch.emit("vector", lambda e, jt=jt: e.tensor_tensor(
                    hsc[:, 512:1024], pg[5][:, 0:512], cu_bo[jt][:, 512:1024],
                    ALU.add), waits=[mml[1], t_h0])
                t_th = sch.emit("scalar", lambda e: e.activation(
                    hsc[:], hsc[:], AF.Tanh), waits=[t_h1])
                prev_th = t_th
                t_y = sch.emit("vector", lambda e, x=xb, y=yb: e.tensor_tensor(
                    y[:], x[:], hsc[:], ALU.add),
                    waits=[t_th, t_x, prev_store])
                xrt_free[p] = t_y
                obank_free[0] = t_y
                obank_free[1] = t_y
                prev_store = sch.dma("sync",
                                     y_h[rt * 128:(rt + 1) * 128, :], yb[:],
                                     waits=[t_y])
            sch.flush()

        t_fin = sch.tok("sync")
        sch.emit("gpsimd", lambda e: None, inc=1, waits=[t_fin])
        sch.flush()
    return nc


def emit_ffn0_fused(nc, sch, ctx, x1_d, x2_d, whs, bhs, id_h,
                    ln1loc, arin, arout, cuin, cuout,
                    sidx_h, gidx_h, cidx_h, cc1, cc2):
    NROW = T_STEPS * JW

    wT = {nm: sbuf(nc, ctx, f"wT_{nm}", [128, 64 * 128], BF16)
          for nm in ("wf", "uf")}
    c_fm = sbuf(nc, ctx, "cfm", [128, 8 * JW], F32)
    c_prev = sbuf(nc, ctx, "cprev", [128, 8 * JW], BF16)
    biasp = sbuf(nc, ctx, "biasp", [128, 4 * 8], F32)
    idf32 = sbuf(nc, ctx, "idf32F", [128, 128], F32)
    idb = sbuf(nc, ctx, "idbF", [128, 128], BF16)
    sidx = sbuf(nc, ctx, "sidx", [128, 32], mybir.dt.uint32)
    gidx = sbuf(nc, ctx, "gidx", [128, 32], mybir.dt.uint32)
    cidx = sbuf(nc, ctx, "cidx", [128, 2], mybir.dt.uint32)
    pg = [psum(nc, ctx, f"pg{i}", [128, 512], F32) for i in range(6)]
    pT = psum(nc, ctx, "pTF", [128, 1024], BF16)
    pT2 = psum(nc, ctx, "pT2F", [128, 1024], BF16)
    ptrs = [PHalf(pT, 0), PHalf(pT2, 0)]

    def idma(out, in_off, in_, out_off, waits=()):
        tok = sch.emit("gpsimd", lambda e, o=out, i=in_, io=in_off, oo=out_off:
                       e.indirect_dma_start(out=o, out_offset=oo, in_=i,
                                            in_offset=io), inc=16, waits=waits)
        sch.emit("gpsimd", lambda e: None, inc=0, waits=[tok])
        return tok

    with ExitStack() as c1:
        st = make_ln_state(nc, c1, "F")
        st["ptr"] = ptrs
        zt = sbuf(nc, c1, "zt", [128, 16384], BF16)
        emit_load_ident(sch, st, id_h)
        t_idf = sch.dma("sync", idf32[:], id_h)
        t_idb = sch.emit("vector", lambda e: e.tensor_copy(idb[:], idf32[:]),
                         waits=[t_idf])
        t_zm = sch.emit("gpsimd", lambda e: e.memset(zt[:], 0.0))
        sch.dma("sync", sidx[:], sidx_h)
        sch.dma("sync", gidx[:], gidx_h)
        sch.dma("sync", cidx[:], cidx_h)
        t_zero = None
        for kk in range(16):
            t_zero = sch.dma("sync",
                             arin[kk * 2048:(kk + 1) * 2048, :].rearrange(
                                 "(a p) d -> p a d", p=128),
                             zt.ap().rearrange("p (a d) -> p a d", a=16),
                             waits=[t_zm])
        t_zero2 = sch.dma("sync", cuin.rearrange("(a p) d -> p a d", p=128),
                          zt.ap().rearrange("p (a d) -> p a d", a=16)
                          [:, 0:16, :], waits=[t_zm])
        sch.flush()

        def post_tile(tt, p, t_dma):
            return idma(arin, None, st["ln_tm"][p][:],
                        IndirectOffsetOnAxis(sidx[:, tt:tt + 1], 0),
                        waits=[t_dma, t_zero, t_zero2])

        rows = [tt * 128 for tt in range(32)]
        emit_ln_to_fm(sch, st, x1_d, 0, 32, None, 0, 0,
                      dram_out=ln1loc, dram_rows=rows, post_tile=post_tile)
        sch.flush()

    t_lg = sch.tok("gpsimd")

    def _coll_ar1(e):
        ins_ = e.collective_compute(
            "AllReduce", ALU.add, replica_groups=[list(range(NCORES))],
            ins=[arin.opt()], outs=[arout.opt()])
        ins_.then_inc(cc1)
        return None
    sch.emit("gpsimd", _coll_ar1, inc=1, waits=[t_lg])
    sch.flush()

    with ExitStack() as cXG:
        xg_fm = sbuf(nc, cXG, "xgfm", [128, 8 * NROW], BF16)
        for nm in ("wi", "ui", "wc", "uc"):
            wT[nm] = sbuf(nc, cXG, f"wT_{nm}", [128, 64 * 128], BF16)

        with ExitStack() as c2:
            xgin = [sbuf(nc, c2, f"xgin{i}", [128, D], BF16) for i in range(2)]
            xg_free = [None, None]
            ptr_free = [None, None]
            first = [True]
            for tt in range(32):
                p = tt % 2
                xb = xgin[p]
                if first[0]:
                    first[0] = False
                    t_l = sch.emit("gpsimd", lambda e, xb=xb, tt=tt: (
                        e.wait_ge(cc1, 1),
                        e.indirect_dma_start(
                            out=xb[:], out_offset=None, in_=arout,
                            in_offset=IndirectOffsetOnAxis(
                                gidx[:, tt:tt + 1], 0)))[1],
                        inc=16, waits=[xg_free[p]])
                    sch.emit("gpsimd", lambda e: None, inc=0, waits=[t_l])
                else:
                    t_l = idma(xb[:], IndirectOffsetOnAxis(gidx[:, tt:tt + 1], 0),
                               arout, None, waits=[xg_free[p]])
                last = None
                for c in range(8):
                    pb = ptrs[c % 2]
                    t_tr = sch.emit("tensor", lambda e, xb=xb, c=c, pb=pb:
                                    e.transpose(
                        pb[0:128, 0:128], xb[:, c * 128:(c + 1) * 128], idb[:]),
                        waits=[t_l, ptr_free[c % 2], t_idb])
                    t_cp = sch.emit("vector", lambda e, c=c, pb=pb, tt=tt:
                                    e.tensor_copy(
                        xg_fm[:, c * NROW + tt * 128:c * NROW + (tt + 1) * 128],
                        pb[0:128, 0:128]), waits=[t_tr])
                    ptr_free[c % 2] = t_cp
                    last = t_cp
                xg_free[p] = last
            sch.flush()

            wstg = [sbuf(nc, c2, f"wstgF{i}", [128, 1024], F32) for i in range(2)]
            wstgb = [sbuf(nc, c2, f"wstgbF{i}", [128, 1024], BF16)
                     for i in range(2)]
            bstg = sbuf(nc, c2, "bstgF", [32, 128], F32)
            stf = {}
            for i, nm in enumerate(("bf", "bi", "bc", "inc")):
                sch.dma("sync", bstg[i * 8:(i + 1) * 8, :],
                        bhs[nm].rearrange("(c p) o -> c (p o)", c=8))
            t_bl = sch.tok("sync")
            t_btr = sch.emit("tensor", lambda e: e.transpose(
                pg[0][:, 0:32], bstg[:], idf32[0:32, 0:32]), waits=[t_bl, t_idf])
            t_bias = sch.emit("vector", lambda e: e.tensor_copy(
                biasp[:], pg[0][:, 0:32]), waits=[t_btr])
            sch.flush()
            for wi_, nm in enumerate(("wf", "uf", "wi", "ui", "wc", "uc")):
                for mc in range(8):
                    sl = (wi_ * 8 + mc) % 2
                    t_l = sch.dma("sync", wstg[sl][:],
                                  whs[nm][mc * 128:(mc + 1) * 128, :],
                                  waits=[stf.get(f"ws{sl}")])
                    t_c = sch.emit("vector", lambda e, sl=sl: e.tensor_copy(
                        wstgb[sl][:], wstg[sl][:]),
                        waits=[t_l, stf.get(f"wb{sl}")])
                    stf[f"ws{sl}"] = t_c
                    last = None
                    for kc in range(8):
                        pb = ptrs[kc % 2]
                        t_tr = sch.emit("tensor", lambda e, sl=sl, kc=kc, pb=pb:
                                        e.transpose(
                            pb[0:128, 0:128], wstgb[sl][:, kc * 128:(kc + 1)
                                                        * 128], idb[:]),
                            waits=[t_c, ptr_free[kc % 2], t_idb])
                        t_cp = sch.emit("vector", lambda e, nm=nm, kc=kc, mc=mc,
                                        pb=pb: e.tensor_copy(
                            wT[nm][:, (kc * 8 + mc) * 128:
                                   (kc * 8 + mc + 1) * 128],
                            pb[0:128, 0:128]), waits=[t_tr])
                        ptr_free[kc % 2] = t_cp
                        last = t_cp
                    stf[f"wb{sl}"] = last
                sch.flush()

        with ExitStack() as c3:
            gsc = [sbuf(nc, c3, f"gscF{i}", [128, JW], F32) for i in range(3)]
            tmp1 = sbuf(nc, c3, "tmp1F", [128, JW], F32)
            tmp2 = sbuf(nc, c3, "tmp2F", [128, JW], F32)
            t_ms = sch.emit("gpsimd", lambda e: e.memset(c_fm[:], 0.0))
            for mc in range(8):
                sch.emit("vector", lambda e, mc=mc: e.tensor_scalar(
                    c_fm[:, mc * JW:(mc + 1) * JW],
                    c_fm[:, mc * JW:(mc + 1) * JW],
                    biasp[:, 24 + mc:24 + mc + 1], None, ALU.add),
                    waits=[t_ms, t_bias])
            sch.flush()

            last_up = sch.tok("vector")
            last_mm_prev = None
            for t_ in range(T_STEPS):
                t_cpv = sch.emit("vector", lambda e: e.tensor_copy(
                    c_prev[:], c_fm[:]), waits=[last_up, last_mm_prev])
                evac_done = {}
                upd_done = {}
                for mc in range(8):
                    for gi_, (wnm, unm, act, bcol) in enumerate((
                            ("wf", "uf", AF.Sigmoid, 0),
                            ("wi", "ui", AF.Sigmoid, 1),
                            ("wc", "uc", AF.Tanh, 2))):
                        bank = pg[(mc % 2) * 3 + gi_]
                        mm = None
                        for kc in range(8):
                            mm = sch.emit("tensor", lambda e, wnm=wnm, kc=kc,
                                          mc=mc, t_=t_, bank=bank: e.matmul(
                                bank[:, 0:JW],
                                wT[wnm][:, (kc * 8 + mc) * 128:
                                        (kc * 8 + mc + 1) * 128],
                                xg_fm[:, kc * NROW + t_ * JW:
                                      kc * NROW + (t_ + 1) * JW],
                                start=(kc == 0), stop=False), inc=0,
                                waits=[evac_done.get(id(bank)), t_cpv]
                                if kc == 0 else [])
                        for kc in range(8):
                            mm = sch.emit("tensor", lambda e, unm=unm, kc=kc,
                                          mc=mc, bank=bank: e.matmul(
                                bank[:, 0:JW],
                                wT[unm][:, (kc * 8 + mc) * 128:
                                        (kc * 8 + mc + 1) * 128],
                                c_prev[:, kc * JW:(kc + 1) * JW],
                                start=False, stop=(kc == 7)),
                                inc=(1 if kc == 7 else 0))
                        evac_done[id(bank)] = sch.emit(
                            "scalar", lambda e, gi_=gi_, mc=mc, act=act,
                            bcol=bcol, bank=bank: e.activation(
                                gsc[gi_][:], bank[:, 0:JW], act,
                                bias=biasp[:, bcol * 8 + mc:bcol * 8 + mc + 1]),
                            waits=[mm, upd_done.get(gi_)])
                    tk1 = sch.emit("vector", lambda e, mc=mc: e.tensor_tensor(
                        tmp1[:], gsc[0][:], c_fm[:, mc * JW:(mc + 1) * JW],
                        ALU.mult), waits=[evac_done[id(pg[(mc % 2) * 3])],
                                          t_cpv])
                    tk2 = sch.emit("vector", lambda e: e.tensor_tensor(
                        tmp2[:], gsc[1][:], gsc[2][:], ALU.mult),
                        waits=[evac_done[id(pg[(mc % 2) * 3 + 1])],
                               evac_done[id(pg[(mc % 2) * 3 + 2])]])
                    t_up = sch.emit("vector", lambda e, mc=mc: e.tensor_tensor(
                        c_fm[:, mc * JW:(mc + 1) * JW], tmp1[:], tmp2[:],
                        ALU.add), waits=[tk1, tk2])
                    for gi_ in range(3):
                        upd_done[gi_] = t_up
                last_up = sch.tok("vector")
                last_mm_prev = sch.tok("tensor")
            t_cfin = sch.emit("vector", lambda e: e.tensor_copy(
                c_prev[:], c_fm[:]), waits=[last_up, last_mm_prev])
            sch.flush()

    with ExitStack() as c4:
        wstg = [sbuf(nc, c4, f"wstgG{i}", [128, 1024], F32) for i in range(2)]
        wstgb = [sbuf(nc, c4, f"wstgbG{i}", [128, 1024], BF16) for i in range(2)]
        woT, uoT = wT["wf"], wT["uf"]
        ptr_free = [None, None]
        stf = {}
        for wi_, (nm, dst) in enumerate((("wo", woT), ("uo", uoT))):
            for mc in range(8):
                sl = (wi_ * 8 + mc) % 2
                t_l = sch.dma("sync", wstg[sl][:],
                              whs[nm][mc * 128:(mc + 1) * 128, :],
                              waits=[stf.get(f"ws{sl}")])
                t_c = sch.emit("vector", lambda e, sl=sl: e.tensor_copy(
                    wstgb[sl][:], wstg[sl][:]), waits=[t_l, stf.get(f"wb{sl}")])
                stf[f"ws{sl}"] = t_c
                last = None
                for kc in range(8):
                    pb = [PHalf(pT, 0), PHalf(pT2, 0)][kc % 2]
                    t_tr = sch.emit("tensor", lambda e, sl=sl, kc=kc, pb=pb:
                                    e.transpose(
                        pb[0:128, 0:128], wstgb[sl][:, kc * 128:(kc + 1) * 128],
                        idb[:]), waits=[t_c, ptr_free[kc % 2]])
                    t_cp = sch.emit("vector", lambda e, dst=dst, kc=kc, mc=mc,
                                    pb=pb: e.tensor_copy(
                        dst[:, (kc * 8 + mc) * 128:(kc * 8 + mc + 1) * 128],
                        pb[0:128, 0:128]), waits=[t_tr])
                    ptr_free[kc % 2] = t_cp
                    last = t_cp
                stf[f"wb{sl}"] = last
        sch.flush()

        cu_bf = [sbuf(nc, c4, f"cubf{i}", [128, D], BF16) for i in range(2)]
        cu_evac = [None, None]
        for jt in range(2):
            mml = [None, None]
            for half in range(2):
                bank = pg[2 + half]
                for kc in range(8):
                    mml[half] = sch.emit("tensor", lambda e, kc=kc, jt=jt,
                                         half=half, bank=bank: e.matmul(
                        bank[:, 0:512],
                        c_prev[:, kc * JW + jt * 128:kc * JW + (jt + 1) * 128],
                        uoT[:, kc * 1024 + half * 512:kc * 1024 + half * 512
                            + 512],
                        start=(kc == 0), stop=(kc == 7)),
                        inc=(1 if kc == 7 else 0),
                        waits=[cu_evac[half]] if kc == 0 else [])
            t_c1 = sch.emit("vector", lambda e, jt=jt: e.tensor_copy(
                cu_bf[jt][:, 0:512], pg[2][:, 0:512]), waits=[mml[0]])
            t_c2 = sch.emit("vector", lambda e, jt=jt: e.tensor_copy(
                cu_bf[jt][:, 512:1024], pg[3][:, 0:512]), waits=[mml[1]])
            cu_evac[0], cu_evac[1] = t_c1, t_c2
            idma(cuin, None, cu_bf[jt][:],
                 IndirectOffsetOnAxis(cidx[:, jt:jt + 1], 0),
                 waits=[t_c1, t_c2])
        t_lg2 = sch.tok("gpsimd")

        def _coll_ar2(e):
            ins_ = e.collective_compute(
                "AllReduce", ALU.add, replica_groups=[list(range(NCORES))],
                ins=[cuin.opt()], outs=[cuout.opt()])
            ins_.then_inc(cc2)
            return None
        sch.emit("gpsimd", _coll_ar2, inc=1, waits=[t_lg2])
        sch.flush()

        own_fm = sbuf(nc, c4, "ownfm", [128, 8 * NROW], BF16)
        xgin = [sbuf(nc, c4, f"oxin{i}", [128, D], BF16) for i in range(2)]
        xg_free = [None, None]
        for tt in range(32):
            p = tt % 2
            xb = xgin[p]
            t_l = sch.dma("sync", xb[:], ln1loc[tt * 128:(tt + 1) * 128, :],
                          waits=[xg_free[p]])
            last = None
            for c in range(8):
                pb = [PHalf(pT, 0), PHalf(pT2, 0)][c % 2]
                t_tr = sch.emit("tensor", lambda e, xb=xb, c=c, pb=pb:
                                e.transpose(
                    pb[0:128, 0:128], xb[:, c * 128:(c + 1) * 128], idb[:]),
                    waits=[t_l, ptr_free[c % 2]])
                t_cp = sch.emit("vector", lambda e, c=c, pb=pb, tt=tt:
                                e.tensor_copy(
                    own_fm[:, c * NROW + tt * 128:c * NROW + (tt + 1) * 128],
                    pb[0:128, 0:128]), waits=[t_tr])
                ptr_free[c % 2] = t_cp
                last = t_cp
            xg_free[p] = last
        sch.flush()

        borow = sbuf(nc, c4, "borowG", [1, D], F32)
        onesf = sbuf(nc, c4, "onesfG", [1, 128], F32)
        bo_bc = sbuf(nc, c4, "bobcG", [128, D], F32)
        cu32 = sbuf(nc, c4, "cu32", [128, D], F32)
        cubo = sbuf(nc, c4, "cubo", [128, D], F32)
        hsc = sbuf(nc, c4, "hscG", [128, D], F32)
        xrt = [sbuf(nc, c4, f"xrtG{i}", [128, D], F32) for i in range(2)]
        yt = sbuf(nc, c4, "ytG", [128, D], F32)
        t_bo = sch.dma("sync", borow[:], bhs["bo"].rearrange("p o -> o p"))
        t_on = sch.emit("gpsimd", lambda e: e.memset(onesf[:], 1.0))
        tb1 = sch.emit("tensor", lambda e: e.matmul(
            pg[0][:, 0:512], onesf[:, 0:128], borow[:, 0:512], start=True,
            stop=True), waits=[t_bo, t_on])
        tb2 = sch.emit("tensor", lambda e: e.matmul(
            pg[1][:, 0:512], onesf[:, 0:128], borow[:, 512:1024], start=True,
            stop=True), waits=[t_bo, t_on])
        sch.emit("vector", lambda e: e.tensor_copy(bo_bc[:, 0:512],
                                                   pg[0][:, 0:512]), waits=[tb1])
        t_bobc = sch.emit("vector", lambda e: e.tensor_copy(
            bo_bc[:, 512:1024], pg[1][:, 0:512]), waits=[tb2])
        sch.flush()

        obank_free = [None, None]
        prev_th = None
        prev_store = None
        xrt_free = [None, None]
        cu_used = None
        first_cu = [True]
        for st_ in range(16):
            if first_cu[0]:
                first_cu[0] = False
                t_cl = sch.emit("sync", lambda e, st_=st_: (
                    e.wait_ge(cc2, 1),
                    e.dma_start(out=cu_bf[0][:],
                                in_=cuout[st_ * 128:(st_ + 1) * 128, :]))[1],
                    inc=16, waits=[])
                sch.emit("sync", lambda e: None, inc=0, waits=[t_cl])
            else:
                t_cl = sch.dma("sync", cu_bf[0][:],
                               cuout[st_ * 128:(st_ + 1) * 128, :],
                               waits=[cu_used])
            t_c32 = sch.emit("vector", lambda e: e.tensor_copy(
                cu32[:], cu_bf[0][:]), waits=[t_cl, prev_th])
            t_cub = sch.emit("vector", lambda e: e.tensor_tensor(
                cubo[:], cu32[:], bo_bc[:], ALU.add), waits=[t_c32, t_bobc])
            cu_used = t_c32
            for b_l in range(2):
                rt = b_l * 16 + st_
                p = rt % 2
                xb = xrt[p]
                t_x = sch.dma("sync", xb[:],
                              x1_d[rt * 128:(rt + 1) * 128, :],
                              waits=[xrt_free[p]])
                mml = [None, None]
                for half in range(2):
                    bank = pg[4 + half]
                    for kc in range(8):
                        mml[half] = sch.emit("tensor", lambda e, kc=kc, rt=rt,
                                             half=half, bank=bank: e.matmul(
                            bank[:, 0:512],
                            own_fm[:, kc * NROW + rt * 128:
                                   kc * NROW + (rt + 1) * 128],
                            woT[:, kc * 1024 + half * 512:kc * 1024
                                + half * 512 + 512],
                            start=(kc == 0), stop=(kc == 7)),
                            inc=(1 if kc == 7 else 0),
                            waits=[obank_free[half]] if kc == 0 else [])
                t_h0 = sch.emit("vector", lambda e: e.tensor_tensor(
                    hsc[:, 0:512], pg[4][:, 0:512], cubo[:, 0:512], ALU.add),
                    waits=[mml[0], prev_th, t_cub])
                t_h1 = sch.emit("vector", lambda e: e.tensor_tensor(
                    hsc[:, 512:1024], pg[5][:, 0:512], cubo[:, 512:1024],
                    ALU.add), waits=[mml[1], t_h0])
                t_th = sch.emit("scalar", lambda e: e.activation(
                    hsc[:], hsc[:], AF.Tanh), waits=[t_h1])
                prev_th = t_th
                t_y = sch.emit("vector", lambda e, x=xb: e.tensor_tensor(
                    yt[:], x[:], hsc[:], ALU.add),
                    waits=[t_th, t_x, prev_store])
                xrt_free[p] = t_y
                obank_free[0] = t_y
                obank_free[1] = t_y
                prev_store = sch.dma("sync",
                                     x2_d[rt * 128:(rt + 1) * 128, :], yt[:],
                                     waits=[t_y])
        sch.flush()


def build_fused(phases="ABCD"):
    nc = bass.Bass("TRN2", target_bir_lowering=False)
    NROW = SEQ_PER_CORE * S
    g = {}
    g["x"] = nc.dram_tensor("x", [NROW, D], F32, kind="ExternalInput").ap()
    for nm, shp in (("wq0", [D, D]), ("wkv0", [D, 2 * D]), ("wo0", [D, D]),
                    ("wq1", [D, D]), ("wk1", [D, D]), ("wv1", [D, D]),
                    ("wo1", [D, D]), ("wf1", [D, H]),
                    ("f_wf", [D, D]), ("f_uf", [D, D]), ("f_wi", [D, D]),
                    ("f_ui", [D, D]), ("f_wc", [D, D]), ("f_uc", [D, D]),
                    ("f_wo", [D, D]), ("f_uo", [D, D]),
                    ("wgate", [I_FF, D]), ("wup", [I_FF, D]),
                    ("wdown", [D, I_FF])):
        g[nm] = nc.dram_tensor(nm, shp, F32, kind="ExternalInput").ap()
    for nm in ("bq0", "bkv0", "f_bf", "f_bi", "f_bc", "f_bo", "f_inc"):
        shp = [2 * D, 1] if nm == "bkv0" else [D, 1]
        g[nm] = nc.dram_tensor(nm, shp, F32, kind="ExternalInput").ap()
    g["bo0"] = nc.dram_tensor("bo0", [1, D], F32, kind="ExternalInput").ap()
    g["csfull"] = nc.dram_tensor("csfull", [128, S], F32,
                                 kind="ExternalInput").ap()
    g["scfull"] = nc.dram_tensor("scfull", [128, S], F32,
                                 kind="ExternalInput").ap()
    g["ident"] = nc.dram_tensor("ident", [128, 128], F32,
                                kind="ExternalInput").ap()
    g["diagmask"] = nc.dram_tensor("diagmask", [128, 4 * 512], BF16,
                                   kind="ExternalInput").ap()
    g["diagmaskA"] = nc.dram_tensor("diagmaskA", [128, 4 * 512], F32,
                                    kind="ExternalInput").ap()
    y_h = nc.dram_tensor("y", [NROW, D], F32, kind="ExternalOutput").ap()
    x1_d = nc.dram_tensor("x1_d", [NROW, D], F32).ap()
    x2_d = nc.dram_tensor("x2_d", [NROW, D], F32).ap()
    x3_d = nc.dram_tensor("x3_d", [NROW, D], F32).ap()
    ln1loc = nc.dram_tensor("ln1loc", [NROW, D], BF16).ap()
    arin = nc.dram_tensor("arin", [B * S, D], BF16).ap()
    arout = nc.dram_tensor("arout", [B * S, D], BF16).ap()
    cuin = nc.dram_tensor("cuin", [S, D], BF16).ap()
    cuout = nc.dram_tensor("cuout", [S, D], BF16).ap()
    U32 = mybir.dt.uint32
    g["sidx"] = nc.dram_tensor("sidx", [128, 32], U32, kind="ExternalInput").ap()
    g["gidx"] = nc.dram_tensor("gidx", [128, 32], U32, kind="ExternalInput").ap()
    g["cidx"] = nc.dram_tensor("cidx", [128, 2], U32, kind="ExternalInput").ap()
    wgq_d = nc.dram_tensor("wgq_d", [D, I_FF], BF16).ap()
    wuq_d = nc.dram_tensor("wuq_d", [D, I_FF], BF16).ap()

    with ExitStack() as ctx0:
        sch = Sch(nc, ctx0)
        cc1 = ctx0.enter_context(nc.semaphore("cc1"))
        cc2 = ctx0.enter_context(nc.semaphore("cc2"))
        def _clr(e):
            e.sem_clear(cc1)
            e.sem_clear(cc2)
            return None
        sch.emit("gpsimd", _clr, inc=1)
        sch.flush()
        if "A" in phases:
            with ExitStack() as cA:
                emit_attn0(nc, sch, cA, x_h=g["x"], y_h=x1_d, wq_h=g["wq0"],
                           wkv_h=g["wkv0"], wo_h=g["wo0"], bq_h=g["bq0"],
                           bkv_h=g["bkv0"], bo_h=g["bo0"], cs_h=g["csfull"],
                           sc_h=g["scfull"], id_h=g["ident"], dm_h=g["diagmask"])
            sch.flush()
        whs = {k[2:]: g["f_" + k[2:]] for k in
               ("f_wf", "f_uf", "f_wi", "f_ui", "f_wc", "f_uc", "f_wo", "f_uo")}
        bhs = {k: g["f_" + k] for k in ("bf", "bi", "bc", "bo", "inc")}
        if "B" in phases:
            with ExitStack() as cB:
                emit_ffn0_fused(nc, sch, cB, x1_d, x2_d, whs, bhs, g["ident"],
                                ln1loc, arin, arout, cuin, cuout,
                                g["sidx"], g["gidx"], g["cidx"], cc1, cc2)
            sch.flush()
        if "C" in phases:
            with ExitStack() as cC:
                emit_attn1(nc, sch, cC, x_h=x2_d, y_h=x3_d, wq_h=g["wq1"],
                           wk_h=g["wk1"], wv_h=g["wv1"], wo_h=g["wo1"],
                           wf_h=g["wf1"], id_h=g["ident"], dm_h=g["diagmaskA"])
            sch.flush()
        if "D" in phases:
            with ExitStack() as cD:
                emit_ffn1(nc, sch, cD, x_h=x3_d, y_h=y_h, wg_h=g["wgate"],
                          wu_h=g["wup"], wd_h=g["wdown"], id_h=g["ident"],
                          wgq_d=wgq_d, wuq_d=wuq_d)
            sch.flush()
        if "D" not in phases:
            srcd = {"A": x1_d, "B": x2_d, "C": x3_d}[phases[-1]]
            with ExitStack() as cZ:
                tbuf = sbuf(nc, cZ, "cpz", [128, D], F32)
                for tt in range(NROW // 128):
                    tl = sch.dma("sync", tbuf[:], srcd[tt * 128:(tt + 1) * 128, :])
                    sch.dma("sync", y_h[tt * 128:(tt + 1) * 128, :], tbuf[:],
                            waits=[tl])
            sch.flush()
        t_fin = sch.tok("sync")
        sch.emit("gpsimd", lambda e: None, inc=1, waits=[t_fin])
        sch.flush()
    return nc


def kernel_fused(phases="ABCD", **inp):
    nc = build_fused(phases)
    cs, sc = _rope_cs_sc()
    x = np.asarray(inp["L0_attn_x"], np.float32)

    def f32(k):
        return np.ascontiguousarray(np.asarray(inp[k], np.float32))
    base = dict(
        wq0=f32("L0_attn_Wq"), wkv0=f32("L0_attn_Wkv"), wo0=f32("L0_attn_Wo"),
        bq0=f32("L0_attn_bq").reshape(D, 1),
        bkv0=f32("L0_attn_bkv").reshape(2 * D, 1),
        bo0=f32("L0_attn_bo").reshape(1, D),
        f_wf=f32("L0_ffn_wf"), f_uf=f32("L0_ffn_uf"), f_wi=f32("L0_ffn_wi"),
        f_ui=f32("L0_ffn_ui"), f_wc=f32("L0_ffn_wc"), f_uc=f32("L0_ffn_uc"),
        f_wo=f32("L0_ffn_wo"), f_uo=f32("L0_ffn_uo"),
        f_bf=f32("L0_ffn_bf").reshape(D, 1), f_bi=f32("L0_ffn_bi").reshape(D, 1),
        f_bc=f32("L0_ffn_bc").reshape(D, 1), f_bo=f32("L0_ffn_bo").reshape(D, 1),
        f_inc=f32("L0_ffn_in_c").reshape(D, 1),
        wq1=f32("L1_attn_Wq"), wk1=f32("L1_attn_Wk"), wv1=f32("L1_attn_Wv"),
        wo1=f32("L1_attn_Wo"), wf1=f32("L1_attn_Wf"),
        wgate=f32("L1_ffn_w_gate"), wup=f32("L1_ffn_w_up"),
        wdown=f32("L1_ffn_w_down"),
        csfull=cs, scfull=sc, ident=_ident(), diagmask=_diag_masks(),
        diagmaskA=_diag_masks_add(),
    )
    in_maps = []
    for c in range(NCORES):
        m = dict(base)
        m["x"] = np.ascontiguousarray(
            x[c * SEQ_PER_CORE:(c + 1) * SEQ_PER_CORE].reshape(
                SEQ_PER_CORE * S, D))
        k = np.arange(128, dtype=np.uint32)
        tt = np.arange(32, dtype=np.uint32)
        m["sidx"] = np.ascontiguousarray(
            ((2 * c + tt // 16) * 2048 + (tt % 16) * 128)[None, :] + k[:, None]
        ).astype(np.uint32)
        r = tt[None, :] * 128 + k[:, None]
        m["gidx"] = np.ascontiguousarray(
            (r // 256) * 2048 + c * 256 + (r % 256)).astype(np.uint32)
        jt = np.arange(2, dtype=np.uint32)
        m["cidx"] = np.ascontiguousarray(
            c * 256 + jt[None, :] * 128 + k[:, None]).astype(np.uint32)
        in_maps.append(m)
    res = run_bass_kernel_spmd(nc, in_maps, core_ids=list(range(NCORES)))
    return np.concatenate([r["y"].reshape(SEQ_PER_CORE, S, D)
                           for r in res.results], 0)


def run_ffn0(x1, wf, uf, bf, wi, ui, bi, wc, uc, bc, wo, uo, bo, in_c):
    nc = build_ffn0()
    in_maps = []
    args = dict(
        wf=np.ascontiguousarray(np.asarray(wf, np.float32)),
        uf=np.ascontiguousarray(np.asarray(uf, np.float32)),
        wi=np.ascontiguousarray(np.asarray(wi, np.float32)),
        ui=np.ascontiguousarray(np.asarray(ui, np.float32)),
        wc=np.ascontiguousarray(np.asarray(wc, np.float32)),
        uc=np.ascontiguousarray(np.asarray(uc, np.float32)),
        wo=np.ascontiguousarray(np.asarray(wo, np.float32)),
        uo=np.ascontiguousarray(np.asarray(uo, np.float32)),
        bf=np.asarray(bf, np.float32).reshape(D, 1),
        bi=np.asarray(bi, np.float32).reshape(D, 1),
        bc=np.asarray(bc, np.float32).reshape(D, 1),
        bo=np.asarray(bo, np.float32).reshape(D, 1),
        inc=np.asarray(in_c, np.float32).reshape(D, 1),
        ident=_ident(),
    )
    for c in range(NCORES):
        xs = np.ascontiguousarray(x1[:, c * JW:(c + 1) * JW, :].reshape(
            T_STEPS * JW, D))
        m = dict(args)
        m["x"] = xs
        in_maps.append(m)
    res = run_bass_kernel_spmd(nc, in_maps, core_ids=list(range(NCORES)))
    out = np.empty((T_STEPS, S, D), np.float32)
    for c in range(NCORES):
        out[:, c * JW:(c + 1) * JW, :] = res.results[c]["y"].reshape(
            T_STEPS, JW, D)
    return out


I_FF = 2816
ISL = [(0, 512), (512, 512), (1024, 512), (1536, 512), (2048, 512), (2560, 256)]


def emit_quant(sch, src, scr2, out, bm, scale, rs, F, waits, out_guard=None):
    B16 = F // 16
    t_bm = sch.emit("vector", lambda e: e.tensor_reduce(
        bm[:, 0:B16], src.rearrange("p (g x) -> p g x", x=16), AX.X, ALU.max,
        apply_absolute_value=True), waits=waits)
    t_sc = sch.emit("vector", lambda e: e.tensor_scalar(
        scale[:, 0:B16], bm[:, 0:B16], 1.0 / 6.0, 1e-12, ALU.mult, ALU.max),
        waits=[t_bm])
    t_rs = sch.emit("vector", lambda e: e.reciprocal(
        rs[:, 0:B16], scale[:, 0:B16]), waits=[t_sc])
    rsb = rs[:, 0:B16].unsqueeze(2).to_broadcast((128, B16, 16))
    scb = scale[:, 0:B16].unsqueeze(2).to_broadcast((128, B16, 16))
    s3 = src.rearrange("p (g x) -> p g x", x=16)
    u3 = scr2.rearrange("p (g x) -> p g x", x=16)
    t_u = sch.emit("vector", lambda e: e.tensor_tensor(u3, s3, rsb, ALU.mult),
                   waits=[t_rs] + list(waits))
    t_v = sch.emit("vector", lambda e: e.tensor_scalar(
        src.bitcast(I32), scr2.bitcast(I32), 0x7F800000, None, ALU.bitwise_and),
        waits=[t_u])
    t_k = sch.emit("vector", lambda e: e.tensor_scalar(
        src, src, 1.0, 6291456.0, ALU.max, ALU.mult), waits=[t_v])
    t_w = sch.emit("vector", lambda e: e.tensor_tensor(
        scr2, scr2, src, ALU.add), waits=[t_k])
    t_q = sch.emit("vector", lambda e: e.tensor_tensor(
        scr2, scr2, src, ALU.subtract), waits=[t_w])
    o3 = out.rearrange("p (g x) -> p g x", x=16)
    t_o = sch.emit("vector", lambda e: e.tensor_tensor(o3, u3, scb, ALU.mult),
                   waits=[t_q, out_guard])
    return t_o


def emit_ffn1(nc, sch, ctx, x_h, y_h, wg_h, wu_h, wd_h, id_h, wgq_d, wuq_d):
    NROW = SEQ_PER_CORE * S
    if True:
        wdq = sbuf(nc, ctx, "wdq", [128, 22 * 1024], BF16)
        scr1 = sbuf(nc, ctx, "scr1", [128, I_FF], F32)
        scr2 = sbuf(nc, ctx, "scr2", [128, I_FF], F32)
        bm = sbuf(nc, ctx, "bm", [128, 176], F32)
        scale = sbuf(nc, ctx, "scale", [128, 176], F32)
        rs = sbuf(nc, ctx, "rs", [128, 176], F32)
        st = make_ln_state(nc, ctx, "D")
        ln32 = [sbuf(nc, ctx, f"ln32{i}", [128, D], F32) for i in range(2)]
        pA = psum(nc, ctx, "pA", [128, 512], F32)
        pB = psum(nc, ctx, "pB", [128, 512], F32)
        pC = psum(nc, ctx, "pC", [128, 512], F32)
        pD = psum(nc, ctx, "pD", [128, 512], F32)
        pE = psum(nc, ctx, "pE", [128, 512], F32)
        pF = psum(nc, ctx, "pF", [128, 512], F32)
        pT = psum(nc, ctx, "pT", [128, 1024], BF16)
        pT2 = psum(nc, ctx, "pT2", [128, 1024], BF16)
        st["ptr"] = [PHalf(pT, 0), PHalf(pT2, 0)]
        emit_load_ident(sch, st, id_h)
        sch.flush()

        with ExitStack() as c1:
            wq_tm = sbuf(nc, c1, "wqtm", [128, 22 * 1024], BF16)
            strip = sbuf(nc, c1, "strip", [128, I_FF], BF16)
            for w_hx, wq_dx in ((wg_h, wgq_d), (wu_h, wuq_d)):
                q_toks = []
                for it in range(22):
                    t_l = sch.dma("sync", scr1[:, 0:1024],
                                  w_hx[it * 128:(it + 1) * 128, :],
                                  waits=[st.get("q_prev")])
                    t_q = emit_quant(sch, scr1[:, 0:1024], scr2[:, 0:1024],
                                     wq_tm[:, it * 1024:(it + 1) * 1024],
                                     bm, scale, rs, 1024, [t_l])
                    st["q_prev"] = t_q
                    q_toks.append(t_q)
                sch.flush()
                for kc in range(8):
                    cp_toks = []
                    for it in range(22):
                        pb = st["ptr"][it % 2]
                        t_tr = sch.emit("tensor", lambda e, it=it, kc=kc, pb=pb:
                                        e.transpose(
                            pb[0:128, 0:128],
                            wq_tm[:, it * 1024 + kc * 128:it * 1024 + (kc + 1)
                                  * 128], st["id_bf"][:]),
                            waits=[st["ptr_free"][it % 2]])
                        t_cp = sch.emit("vector", lambda e, it=it, pb=pb:
                                        e.tensor_copy(
                            strip[:, it * 128:(it + 1) * 128], pb[0:128, 0:128]),
                            waits=[t_tr, st.get("strip_spill")])
                        st["ptr_free"][it % 2] = t_cp
                        cp_toks.append(t_cp)
                    st["spill_done"] = sch.dma(
                        "sync", wq_dx[kc * 128:(kc + 1) * 128, :],
                        strip[:], waits=[cp_toks[-1]])
                    st["strip_spill"] = st["spill_done"]
                    sch.flush()
            for et in range(8):
                t_l = sch.dma("sync", scr1[:], wd_h[et * 128:(et + 1) * 128, :],
                              waits=[st.get("q_prev")])
                t_q = emit_quant(sch, scr1[:], scr2[:],
                                 strip[:], bm, scale, rs, I_FF, [t_l],
                                 out_guard=st.get("strip_guard"))
                st["q_prev"] = t_q
                for ic in range(22):
                    pb = st["ptr"][ic % 2]
                    t_tr = sch.emit("tensor", lambda e, ic=ic, pb=pb: e.transpose(
                        pb[0:128, 0:128], strip[:, ic * 128:(ic + 1) * 128],
                        st["id_bf"][:]), waits=[t_q, st["ptr_free"][ic % 2]])
                    t_cp = sch.emit("vector", lambda e, ic=ic, et=et, pb=pb:
                                    e.tensor_copy(
                        wdq[:, ic * 1024 + et * 128:ic * 1024 + (et + 1) * 128],
                        pb[0:128, 0:128]), waits=[t_tr])
                    st["ptr_free"][ic % 2] = t_cp
                st["q_prev"] = sch.tok("vector")
                st["strip_guard"] = sch.tok("tensor")
                sch.flush()

        with ExitStack() as c2:
            xq_fm = sbuf(nc, c2, "xqfm", [128, 8 * 512], BF16)
            strip2 = sbuf(nc, c2, "strip2", [128, I_FF], BF16)
            hq_fm = sbuf(nc, c2, "hqfm", [128, 22 * 512], BF16)
            wg_sl = [sbuf(nc, c2, f"wgsl{i}", [128, 8 * 512], BF16) for i in range(2)]
            wu_sl = [sbuf(nc, c2, f"wusl{i}", [128, 8 * 512], BF16) for i in range(2)]
            silu_s = sbuf(nc, c2, "silus", [128, 512], F32)
            xq_tm = sbuf(nc, c2, "xqtm", [128, D], BF16)
            xrt = [sbuf(nc, c2, f"xrtD{i}", [128, D], F32) for i in range(2)]
            yt = [sbuf(nc, c2, f"ytD{i}", [128, D], F32) for i in range(2)]
            wsl_free = [None, None]
            for tg in range(8):
                row0 = tg * 512
                sums = st["stat"]
                for tt4 in range(4):
                    p = tt4 % 2
                    co = 8 * p
                    xb = st["xin"][p]
                    sq_scr = st["sq_scr"][p]
                    lnb = ln32[p]
                    t_x = sch.dma("sync", xb[:],
                                  x_h[row0 + tt4 * 128:row0 + (tt4 + 1) * 128, :],
                                  waits=[st["xin_free"][p]])
                    t_sum = sch.emit("vector", lambda e, x=xb, co=co:
                                     e.tensor_reduce(
                        sums[:, co:co + 1], x[:], AX.X, ALU.add),
                        waits=[t_x, st["stat_free"][p]])
                    t_sq0 = sch.emit("scalar", lambda e, x=xb, sq=sq_scr:
                                     e.activation(sq[:], x[:], AF.Square),
                                     waits=[t_x, st["sqw_free"][p]])
                    t_sq = sch.emit("vector", lambda e, sq=sq_scr, co=co:
                                    e.tensor_reduce(
                        sums[:, co + 4:co + 5], sq[:], AX.X, ALU.add),
                        waits=[t_sq0, st["stat_free"][p]])
                    st["sqw_free"][p] = t_sq
                    t_mean = sch.emit("vector", lambda e, co=co: e.tensor_scalar(
                        sums[:, co + 1:co + 2], sums[:, co:co + 1], 1.0 / D, None,
                        ALU.mult), waits=[t_sum])
                    t_msq = sch.emit("vector", lambda e, co=co: e.tensor_scalar(
                        sums[:, co + 2:co + 3], sums[:, co + 4:co + 5], 1.0 / D,
                        None, ALU.mult), waits=[t_sq])
                    t_mm = sch.emit("vector", lambda e, co=co: e.tensor_tensor(
                        sums[:, co + 3:co + 4], sums[:, co + 1:co + 2],
                        sums[:, co + 1:co + 2], ALU.mult), waits=[t_mean])
                    t_var = sch.emit("vector", lambda e, co=co: e.tensor_tensor(
                        sums[:, co + 5:co + 6], sums[:, co + 2:co + 3],
                        sums[:, co + 3:co + 4], ALU.subtract),
                        waits=[t_msq, t_mm])
                    t_sd = sch.emit("scalar", lambda e, co=co: e.activation(
                        sums[:, co + 6:co + 7], sums[:, co + 5:co + 6], AF.Sqrt,
                        bias=st["epsb"][:, 0:1]), waits=[t_var, st["t_eps"]])
                    t_rstd = sch.emit("vector", lambda e, co=co: e.reciprocal(
                        sums[:, co + 7:co + 8], sums[:, co + 6:co + 7]),
                        waits=[t_sd])
                    t_ap = sch.emit("vector", lambda e, x=xb, l=lnb, co=co:
                                    e.tensor_scalar(
                        l[:], x[:], sums[:, co + 1:co + 2],
                        sums[:, co + 7:co + 8], ALU.subtract, ALU.mult),
                        waits=[t_rstd, st["ln_free"][p]])
                    st["xin_free"][p] = t_ap
                    st["stat_free"][p] = t_ap
                    t_q = emit_quant(sch, lnb[:], scr2[:, 0:1024], xq_tm[:],
                                     bm, scale, rs, 1024, [t_ap],
                                     out_guard=st.get("q_xqtm"))
                    st["ln_free"][p] = t_q
                    last_cp = None
                    for c in range(8):
                        pb = st["ptr"][c % 2]
                        t_tr = sch.emit("tensor", lambda e, c=c, pb=pb: e.transpose(
                            pb[0:128, 0:128], xq_tm[:, c * 128:(c + 1) * 128],
                            st["id_bf"][:]),
                            waits=[t_q, st["ptr_free"][c % 2]])
                        t_cp = sch.emit("vector", lambda e, c=c, pb=pb, tt4=tt4:
                                        e.tensor_copy(
                            xq_fm[:, c * 512 + tt4 * 128:c * 512 + (tt4 + 1)
                                  * 128], pb[0:128, 0:128]), waits=[t_tr])
                        st["ptr_free"][c % 2] = t_cp
                        last_cp = t_cp
                    st["q_xqtm"] = last_cp
                sch.flush()

                for tt4 in range(4):
                    db = tt4 % 2
                    for isl, (i0, w) in enumerate(ISL):
                        st[f"wg_ld{isl % 2}"] = sch.dma(
                            "sync",
                            wg_sl[isl % 2].ap().rearrange(
                                "p (c m) -> p c m", c=8)[:, :, 0:w],
                            wgq_d[:, i0:i0 + w].rearrange(
                                "(c p) m -> p c m", p=128),
                            waits=[st.get(f"wg_used{isl % 2}"),
                                   st.get("spill_done")])
                        st[f"wu_ld{isl % 2}"] = sch.dma(
                            "sync",
                            wu_sl[isl % 2].ap().rearrange(
                                "p (c m) -> p c m", c=8)[:, :, 0:w],
                            wuq_d[:, i0:i0 + w].rearrange(
                                "(c p) m -> p c m", p=128),
                            waits=[st.get(f"wu_used{isl % 2}")])
                        mg = mu = None
                        for kc in range(8):
                            mg = sch.emit("tensor", lambda e, kc=kc, tt4=tt4,
                                          isl=isl, w=w: e.matmul(
                                pA[:, 0:w],
                                xq_fm[:, kc * 512 + tt4 * 128:kc * 512
                                      + (tt4 + 1) * 128],
                                wg_sl[isl % 2][:, kc * 512:kc * 512 + w],
                                start=(kc == 0), stop=(kc == 7)),
                                inc=(1 if kc == 7 else 0),
                                waits=[st.get(f"wg_ld{isl % 2}"),
                                       st.get("pA_free")] if kc == 0 else [])
                            mu = sch.emit("tensor", lambda e, kc=kc, tt4=tt4,
                                          isl=isl, w=w: e.matmul(
                                pB[:, 0:w],
                                xq_fm[:, kc * 512 + tt4 * 128:kc * 512
                                      + (tt4 + 1) * 128],
                                wu_sl[isl % 2][:, kc * 512:kc * 512 + w],
                                start=(kc == 0), stop=(kc == 7)),
                                inc=(1 if kc == 7 else 0),
                                waits=[st.get(f"wu_ld{isl % 2}"),
                                       st.get("pB_free")] if kc == 0 else [])
                        t_sil = sch.emit("scalar", lambda e, w=w: e.activation(
                            silu_s[:, 0:w], pA[:, 0:w], AF.Silu),
                            waits=[mg, st.get("pB_free")])
                        st["pA_free"] = t_sil
                        t_h = sch.emit("vector", lambda e, i0=i0, w=w:
                                       e.tensor_tensor(
                            scr1[:, i0:i0 + w], silu_s[:, 0:w], pB[:, 0:w],
                            ALU.mult), waits=[t_sil, mu, st.get("h_guard")])
                        st["pB_free"] = t_h
                        st[f"wg_used{isl % 2}"] = mg
                        st[f"wu_used{isl % 2}"] = mu
                    t_hq = emit_quant(sch, scr1[:], scr2[:], strip2[:],
                                      bm, scale, rs, I_FF, [sch.tok("vector")],
                                      out_guard=st.get("hq_guard"))
                    st["h_guard"] = t_hq
                    for ic in range(22):
                        pb = st["ptr"][ic % 2]
                        t_tr = sch.emit("tensor", lambda e, ic=ic, pb=pb:
                                        e.transpose(
                            pb[0:128, 0:128], strip2[:, ic * 128:(ic + 1) * 128],
                            st["id_bf"][:]), waits=[t_hq, st["ptr_free"][ic % 2]])
                        t_cp = sch.emit("vector", lambda e, ic=ic, tt4=tt4, pb=pb:
                                        e.tensor_copy(
                            hq_fm[:, ic * 512 + tt4 * 128:ic * 512 + (tt4 + 1)
                                  * 128], pb[0:128, 0:128]), waits=[t_tr])
                        st["ptr_free"][ic % 2] = t_cp
                        if ic == 21:
                            st["hq_guard"] = sch.tok("tensor")
                sch.flush()

                obank_free = [None, None]
                for tt4 in range(4):
                    p = tt4 % 2
                    xb = xrt[p]
                    yb = yt[p]
                    t_x = sch.dma("sync", xb[:],
                                  x_h[row0 + tt4 * 128:row0 + (tt4 + 1) * 128, :])
                    mml = [None, None]
                    for half in range(2):
                        bank = [pC, pD][half]
                        for ic in range(22):
                            mml[half] = sch.emit("tensor", lambda e, ic=ic,
                                                 tt4=tt4, half=half, bank=bank:
                                                 e.matmul(
                                bank[:, 0:512],
                                hq_fm[:, ic * 512 + tt4 * 128:ic * 512
                                      + (tt4 + 1) * 128],
                                wdq[:, ic * 1024 + half * 512:ic * 1024
                                    + half * 512 + 512],
                                start=(ic == 0), stop=(ic == 21)),
                                inc=(1 if ic == 21 else 0),
                                waits=[obank_free[half]] if ic == 0 else [])
                    t_y0 = sch.emit("vector", lambda e, x=xb, y=yb:
                                    e.tensor_tensor(
                        y[:, 0:512], x[:, 0:512], pC[:, 0:512], ALU.add),
                        waits=[t_x, mml[0]])
                    t_y1 = sch.emit("vector", lambda e, x=xb, y=yb:
                                    e.tensor_tensor(
                        y[:, 512:1024], x[:, 512:1024], pD[:, 0:512], ALU.add),
                        waits=[t_x, mml[1]])
                    obank_free[0] = t_y0
                    obank_free[1] = t_y1
                    sch.dma("sync",
                            y_h[row0 + tt4 * 128:row0 + (tt4 + 1) * 128, :],
                            yb[:], waits=[t_y0, t_y1])
                sch.flush()


def build_ffn1():
    nc = bass.Bass("TRN2", target_bir_lowering=False)
    NROW = SEQ_PER_CORE * S
    hs = {}
    hs["x_h"] = nc.dram_tensor("x", [NROW, D], F32, kind="ExternalInput").ap()
    hs["wg_h"] = nc.dram_tensor("wgate", [I_FF, D], F32, kind="ExternalInput").ap()
    hs["wu_h"] = nc.dram_tensor("wup", [I_FF, D], F32, kind="ExternalInput").ap()
    hs["wd_h"] = nc.dram_tensor("wdown", [D, I_FF], F32, kind="ExternalInput").ap()
    hs["id_h"] = nc.dram_tensor("ident", [128, 128], F32, kind="ExternalInput").ap()
    hs["y_h"] = nc.dram_tensor("y", [NROW, D], F32, kind="ExternalOutput").ap()
    hs["wgq_d"] = nc.dram_tensor("wgq_d", [D, I_FF], BF16)
    hs["wuq_d"] = nc.dram_tensor("wuq_d", [D, I_FF], BF16)
    with ExitStack() as ctx:
        sch = Sch(nc, ctx)
        emit_ffn1(nc, sch, ctx, **hs)
        t_fin = sch.tok("sync")
        sch.emit("gpsimd", lambda e: None, inc=1, waits=[t_fin])
        sch.flush()
    return nc


def run_ffn1(x3, w_gate, w_up, w_down):
    nc = build_ffn1()
    in_maps = []
    wg = np.ascontiguousarray(np.asarray(w_gate, np.float32))
    wu = np.ascontiguousarray(np.asarray(w_up, np.float32))
    wd = np.ascontiguousarray(np.asarray(w_down, np.float32))
    for c in range(NCORES):
        xs = np.ascontiguousarray(
            x3[c * SEQ_PER_CORE:(c + 1) * SEQ_PER_CORE].reshape(
                SEQ_PER_CORE * S, D))
        in_maps.append(dict(x=xs, wgate=wg, wup=wu, wdown=wd, ident=_ident()))
    res = run_bass_kernel_spmd(nc, in_maps, core_ids=list(range(NCORES)))
    return np.concatenate([r["y"].reshape(SEQ_PER_CORE, S, D)
                           for r in res.results], 0)


def _rope_cs_sc():
    inv = 1.0 / (10000.0 ** (np.arange(0, 64, 2, dtype=np.float64) / 64.0))
    pos = np.arange(S, dtype=np.float64)
    ang = inv[:, None] * pos[None, :]
    cos = np.cos(ang)
    sin = np.sin(ang)
    cs = np.concatenate([cos, sin, cos, sin], 0).astype(np.float32)
    sc = np.concatenate([-sin, cos, -sin, cos], 0).astype(np.float32)
    return cs, sc


def _diag_masks():
    from ml_dtypes import bfloat16
    m = np.zeros((128, 4, 512), np.float32)
    p = np.arange(128)[:, None]
    o = np.arange(512)[None, :]
    for d in range(4):
        m[:, d, :] = np.where(p > o - 128 * d, 0.0, 1.0)
    return m.reshape(128, 4 * 512).astype(bfloat16)


def _ident():
    return np.eye(128, dtype=np.float32)


def run_attn0(x, Wq, bq, Wkv, bkv, Wo, bo):
    nc = build_attn0()
    cs, sc = _rope_cs_sc()
    dm = _diag_masks()
    in_maps = []
    Wq = np.ascontiguousarray(np.asarray(Wq, np.float32))
    Wkv = np.ascontiguousarray(np.asarray(Wkv, np.float32))
    Wo = np.ascontiguousarray(np.asarray(Wo, np.float32))
    for c in range(NCORES):
        xs = np.ascontiguousarray(
            x[c * SEQ_PER_CORE:(c + 1) * SEQ_PER_CORE].reshape(SEQ_PER_CORE * S, D))
        in_maps.append(dict(x=xs, wq=Wq, wkv=Wkv, wo=Wo,
                            bq=np.asarray(bq, np.float32).reshape(D, 1),
                            bkv=np.asarray(bkv, np.float32).reshape(2 * D, 1),
                            bo=np.asarray(bo, np.float32).reshape(1, D),
                            csfull=cs, scfull=sc, diagmask=dm, ident=_ident()))
    res = run_bass_kernel_spmd(nc, in_maps, core_ids=list(range(NCORES)))
    return np.concatenate([r["y"].reshape(SEQ_PER_CORE, S, D)
                           for r in res.results], 0)


def kernel(**inp):
    return kernel_fused(phases="ABCD", **inp)


def kernel_multilaunch(**inp):
    x = np.asarray(inp["L0_attn_x"], np.float32)
    x1 = run_attn0(x, inp["L0_attn_Wq"], inp["L0_attn_bq"], inp["L0_attn_Wkv"],
                   inp["L0_attn_bkv"], inp["L0_attn_Wo"], inp["L0_attn_bo"])
    x2 = run_ffn0(x1, inp["L0_ffn_wf"], inp["L0_ffn_uf"], inp["L0_ffn_bf"],
                  inp["L0_ffn_wi"], inp["L0_ffn_ui"], inp["L0_ffn_bi"],
                  inp["L0_ffn_wc"], inp["L0_ffn_uc"], inp["L0_ffn_bc"],
                  inp["L0_ffn_wo"], inp["L0_ffn_uo"], inp["L0_ffn_bo"],
                  inp["L0_ffn_in_c"])
    x3 = run_attn1(x2, inp["L1_attn_Wq"], inp["L1_attn_Wk"], inp["L1_attn_Wv"],
                   inp["L1_attn_Wo"], inp["L1_attn_Wf"])
    x4 = run_ffn1(x3, inp["L1_ffn_w_gate"], inp["L1_ffn_w_up"],
                  inp["L1_ffn_w_down"])
    return x4


# version 06
# speedup vs baseline: 1.2010x; 1.2010x over previous
import numpy as np
from contextlib import ExitStack

import concourse.bass as bass
import concourse.mybir as mybir
from concourse.bass_utils import run_bass_kernel_spmd
from concourse.bass import IndirectOffsetOnAxis

F32 = mybir.dt.float32
BF16 = mybir.dt.bfloat16
I32 = mybir.dt.int32
AF = mybir.ActivationFunctionType
ALU = mybir.AluOpType
AX = mybir.AxisListType

NCORES = 8
B, S, D = 16, 2048, 1024
H, DH = 16, 64
SEQ_PER_CORE = B // NCORES
NEG = -1.0e30


class Sch:
    ENGINES = ("sync", "tensor", "vector", "scalar", "gpsimd")

    def __init__(self, nc, ctx):
        self.nc = nc
        self.ops = {e: [] for e in self.ENGINES}
        self.cnt = {e: 0 for e in self.ENGINES}
        self.sems = {e: ctx.enter_context(nc.semaphore(f"s_{e}")) for e in self.ENGINES}
        self.nblk = 0
        sems = list(self.sems.values())

        def clear_all(e, _s=tuple(sems)):
            for s_ in _s:
                e.sem_clear(s_)
        self.ops["gpsimd"].append(lambda e: clear_all(e))
        self.flush()

    def emit(self, eng, fn, inc=1, waits=()):
        best = {}
        for w in waits:
            if w is None:
                continue
            we, wv = w
            if we == eng and wv > self.cnt[eng]:
                raise AssertionError("same-engine wait on future op")
            if wv > best.get(we, -1):
                best[we] = wv
        waits = tuple(best.items())
        sem_map = self.sems
        if inc:
            self.cnt[eng] += inc
        tok = (eng, self.cnt[eng])

        def run(e, _fn=fn, _w=waits, _inc=inc, _sem=sem_map[eng]):
            for (we, wv) in _w:
                e.wait_ge(sem_map[we], wv)
            ins = _fn(e)
            if _inc:
                if ins is not None:
                    ins.then_inc(_sem, _inc)
                else:
                    e.nop().then_inc(_sem, _inc)
        self.ops[eng].append(run)
        return tok

    def dma(self, eng, out, in_, waits=()):
        tok = self.emit(eng, lambda e, o=out, i=in_: e.dma_start(out=o, in_=i),
                        inc=16, waits=waits)
        self.emit(eng, lambda e: None, inc=0, waits=[tok])
        return tok

    def dma_slow(self, eng, out, in_, waits=()):
        return self.emit(eng, lambda e, o=out, i=in_: e.dma_start(
            out=o, in_=i, allow_slow_non_contiguous=True), inc=16, waits=waits)

    def tok(self, eng):
        return (eng, self.cnt[eng])

    def flush(self):
        self.nblk += 1
        live = [(e, self.ops[e]) for e in self.ENGINES if self.ops[e]]
        if not live:
            return
        with self.nc.Block(f"blk{self.nblk}") as blk:
            for ename, oplist in live:
                dec = getattr(blk, ename)

                def body(engobj, _ops=tuple(oplist)):
                    for f in _ops:
                        f(engobj)
                dec(body)
        for e in self.ENGINES:
            self.ops[e] = []


_NAMEC = [0]


def sbuf(nc, ctx, name, shape, dt):
    _NAMEC[0] += 1
    return ctx.enter_context(nc.sbuf_tensor(f"{name}_{_NAMEC[0]}", shape, dt))


def psum(nc, ctx, name, shape, dt):
    _NAMEC[0] += 1
    return ctx.enter_context(nc.psum_tensor(f"{name}_{_NAMEC[0]}", shape, dt))


class PHalf:

    def __init__(self, t, off):
        self.t, self.off = t, off

    def __getitem__(self, idx):
        ps, fs = idx
        lo = fs.start or 0
        hi = fs.stop if fs.stop is not None else 512
        return self.t[ps, self.off + lo: self.off + hi]


def make_ln_state(nc, ctx, tag=""):
    st = {}
    st["xin"] = [sbuf(nc, ctx, f"lnx{i}{tag}", [128, D], F32) for i in range(2)]
    st["sq_scr"] = [sbuf(nc, ctx, f"lnsq{i}{tag}", [128, D], F32) for i in range(2)]
    st["ln_tm"] = [sbuf(nc, ctx, f"lnt{i}{tag}", [128, D], BF16) for i in range(2)]
    st["stat"] = sbuf(nc, ctx, f"lnstat{tag}", [128, 16], F32)
    st["id_bf"] = sbuf(nc, ctx, f"idbf{tag}", [128, 128], BF16)
    st["epsb"] = sbuf(nc, ctx, f"epsb{tag}", [128, 1], F32)
    st["xin_free"] = [None, None]
    st["sqw_free"] = [None, None]
    st["stat_free"] = [None, None]
    st["ln_free"] = [None, None]
    st["ptr_free"] = [None, None]
    return st


def emit_load_ident(sch, st, id_h):
    idf = st["xin"][0]
    t = sch.dma("sync", idf[:, 0:128], id_h)
    t2 = sch.emit("vector", lambda e: e.tensor_copy(st["id_bf"][:], idf[:, 0:128]),
                  waits=[t])
    st["xin_free"][0] = t2
    st["t_eps"] = sch.emit("gpsimd", lambda e: e.memset(st["epsb"][:], 1e-5))


def emit_ln_to_fm(sch, st, x_h, row0, ntiles, ln_fm, fm_stride, fm_col0,
                  dram_out=None, dram_rows=None, post_tile=None):
    sums = st["stat"]
    for tt in range(ntiles):
        p = tt % 2
        co = 8 * p
        xb = st["xin"][p]
        sq_scr = st["sq_scr"][p]
        lnb = st["ln_tm"][p]
        t_x = sch.dma("sync", xb[:], x_h[row0 + tt * 128: row0 + (tt + 1) * 128, :],
                      waits=[st["xin_free"][p]])
        t_sum = sch.emit("vector", lambda e, x=xb, co=co: e.tensor_reduce(
            sums[:, co + 0:co + 1], x[:], AX.X, ALU.add),
            waits=[t_x, st["stat_free"][p]])
        t_sq0 = sch.emit("vector", lambda e, x=xb, sq=sq_scr: e.tensor_tensor(
            sq[:], x[:], x[:], ALU.mult), waits=[t_x, st["sqw_free"][p]])
        t_sq = sch.emit("vector", lambda e, sq=sq_scr, co=co: e.tensor_reduce(
            sums[:, co + 4:co + 5], sq[:], AX.X, ALU.add),
            waits=[t_sq0, st["stat_free"][p]])
        st["sqw_free"][p] = t_sq
        t_mean = sch.emit("vector", lambda e, co=co: e.tensor_scalar(
            sums[:, co + 1:co + 2], sums[:, co + 0:co + 1], 1.0 / D, None, ALU.mult),
            waits=[t_sum])
        t_msq = sch.emit("vector", lambda e, co=co: e.tensor_scalar(
            sums[:, co + 2:co + 3], sums[:, co + 4:co + 5], 1.0 / D, None, ALU.mult),
            waits=[t_sq])
        t_mm = sch.emit("vector", lambda e, co=co: e.tensor_tensor(
            sums[:, co + 3:co + 4], sums[:, co + 1:co + 2], sums[:, co + 1:co + 2],
            ALU.mult), waits=[t_mean])
        t_var = sch.emit("vector", lambda e, co=co: e.tensor_tensor(
            sums[:, co + 5:co + 6], sums[:, co + 2:co + 3], sums[:, co + 3:co + 4],
            ALU.subtract), waits=[t_msq, t_mm])
        t_sd = sch.emit("scalar", lambda e, co=co: e.activation(
            sums[:, co + 6:co + 7], sums[:, co + 5:co + 6], AF.Sqrt,
            bias=st["epsb"][:, 0:1]), waits=[t_var, st["t_eps"]])
        t_sd2 = sch.emit("scalar", lambda e, co=co: e.activation(
            sums[:, co + 4:co + 5], sums[:, co + 5:co + 6], AF.Sqrt,
            bias=st["epsb"][:, 0:1]), waits=[t_sd])
        t_rstd = sch.emit("vector", lambda e, co=co: e.reciprocal(
            sums[:, co + 7:co + 8], sums[:, co + 6:co + 7]), waits=[t_sd2])
        t_ap = sch.emit("vector", lambda e, x=xb, l=lnb, co=co: e.tensor_scalar(
            l[:], x[:], sums[:, co + 1:co + 2], sums[:, co + 7:co + 8],
            ALU.subtract, ALU.mult), waits=[t_rstd, st["ln_free"][p]])
        st["xin_free"][p] = t_ap
        st["stat_free"][p] = t_ap
        if dram_out is not None:
            r0d = dram_rows[tt]
            t_dma = sch.dma("sync", dram_out[r0d:r0d + 128, :],
                            st["ln_tm"][p][:], waits=[t_ap])
            st["ln_free"][p] = t_dma
            if post_tile is not None:
                st["ln_free"][p] = post_tile(tt, p, t_dma)
            sch.flush()
            continue
        last_cp = None
        for c in range(8):
            pb = st["ptr"][c % 2]
            t_tr = sch.emit("tensor", lambda e, l=lnb, c=c, pb=pb: e.transpose(
                pb[0:128, 0:128], l[:, c * 128:(c + 1) * 128], st["id_bf"][:]),
                waits=[t_ap, st["ptr_free"][c % 2]])
            t_cp = sch.emit("vector", lambda e, c=c, pb=pb, tt=tt: e.tensor_copy(
                ln_fm[:, c * fm_stride + fm_col0 + tt * 128:
                      c * fm_stride + fm_col0 + (tt + 1) * 128], pb[0:128, 0:128]),
                waits=[t_tr])
            st["ptr_free"][c % 2] = t_cp
            last_cp = t_cp
        st["ln_free"][p] = last_cp
        sch.flush()


def emit_attn0(nc, sch, ctx, x_h, y_h, wq_h, wkv_h, wo_h, bq_h, bkv_h, bo_h,
               cs_h, sc_h, id_h, dm_h):
    if True:
        st = make_ln_state(nc, ctx)
        ln_fm = sbuf(nc, ctx, "lnfm", [128, 8 * S], BF16)
        attn_fm = sbuf(nc, ctx, "attnfm", [128, 8 * S], BF16)
        w_f = [sbuf(nc, ctx, f"wf{i}", [128, 1024], F32) for i in range(2)]
        wqA = sbuf(nc, ctx, "wqA", [128, 1024], BF16)
        wqB = sbuf(nc, ctx, "wqB", [128, 1024], BF16)
        wkA = sbuf(nc, ctx, "wkA", [128, 1024], BF16)
        wkB = sbuf(nc, ctx, "wkB", [128, 1024], BF16)
        wv_b = sbuf(nc, ctx, "wvb", [128, 1024], BF16)
        csf = sbuf(nc, ctx, "csf", [128, S], F32)
        scf = sbuf(nc, ctx, "scf", [128, S], F32)
        ropeS = [sbuf(nc, ctx, f"ropeS{i}", [128, 512], F32) for i in range(2)]
        qr_b = sbuf(nc, ctx, "qrb", [128, S], BF16)
        kr_b = sbuf(nc, ctx, "krb", [128, S], BF16)
        v_b = sbuf(nc, ctx, "vb", [128, S], BF16)
        vhat = [sbuf(nc, ctx, f"vhat{h}", [128, 16 * 66], BF16) for h in range(2)]
        idf32 = sbuf(nc, ctx, "idf32", [128, 128], F32)
        bstg = sbuf(nc, ctx, "bstg", [16, 128], F32)
        bqpair = sbuf(nc, ctx, "bqpair", [128, 8], F32)
        bkpair = sbuf(nc, ctx, "bkpair", [128, 8], F32)
        biaqA = sbuf(nc, ctx, "biaqA", [128, 8], F32)
        biaqB = sbuf(nc, ctx, "biaqB", [128, 8], F32)
        biakA = sbuf(nc, ctx, "biakA", [128, 8], F32)
        biakB = sbuf(nc, ctx, "biakB", [128, 8], F32)
        biav = sbuf(nc, ctx, "biav", [128, 8], F32)
        eA = [sbuf(nc, ctx, f"eA{i}", [128, 512], BF16) for i in range(2)]
        eB = [sbuf(nc, ctx, f"eB{i}", [128, 512], BF16) for i in range(2)]
        dmask = sbuf(nc, ctx, "dmask", [128, 4 * 512], BF16)
        rcp = sbuf(nc, ctx, "rcp", [1, 1024], F32)
        sbc = sbuf(nc, ctx, "sbc", [128, 1024], F32)
        onesf = sbuf(nc, ctx, "onesf", [1, 128], F32)
        bm4 = sbuf(nc, ctx, "bm4", [128, 1], F32)
        bo_bc = sbuf(nc, ctx, "bobc", [128, D], F32)
        borow = sbuf(nc, ctx, "borow", [1, D], F32)
        wob = sbuf(nc, ctx, "wob", [128, 8 * 1024], BF16)
        xrt = [sbuf(nc, ctx, f"xrt{i}", [128, D], F32) for i in range(2)]
        yt = [sbuf(nc, ctx, f"yt{i}", [128, D], F32) for i in range(2)]

        psA = psum(nc, ctx, "psA", [128, 512], F32)
        psB = psum(nc, ctx, "psB", [128, 512], F32)
        psA2 = psum(nc, ctx, "psA2", [128, 512], F32)
        psB2 = psum(nc, ctx, "psB2", [128, 512], F32)
        poA = psum(nc, ctx, "poA", [128, 512], F32)
        poB = psum(nc, ctx, "poB", [128, 512], F32)
        pT = psum(nc, ctx, "pT", [128, 1024], BF16)
        pT2 = psum(nc, ctx, "pT2", [128, 1024], BF16)
        st["ptr"] = [PHalf(pT, 0), PHalf(pT2, 0)]

        t_dm = sch.dma("sync", dmask[:], dm_h)
        emit_load_ident(sch, st, id_h)
        sch.dma("sync", csf[:], cs_h)
        sch.dma("sync", scf[:], sc_h)
        t_bo = sch.dma("sync", borow[:], bo_h)
        t_idf = sch.dma("sync", idf32[:], id_h)
        t_l1 = sch.dma("sync", bstg[0:8, :],
                       bq_h.rearrange("(c p) o -> c (p o)", c=8))
        tt1 = sch.emit("tensor", lambda e: e.transpose(
            psA2[:, 0:8], bstg[0:8, :], idf32[0:8, 0:8]), waits=[t_l1, t_idf])
        t_bq = sch.emit("vector", lambda e: e.tensor_copy(
            bqpair[:], psA2[:, 0:8]), waits=[tt1])
        t_l2 = sch.dma("sync", bstg[:],
                       bkv_h.rearrange("(c p) o -> c (p o)", c=16), waits=[t_bq])
        tt2 = sch.emit("tensor", lambda e: e.transpose(
            psB2[:, 0:16], bstg[:], idf32[0:16, 0:16]), waits=[t_l2, t_idf, t_bq])
        t_bk = sch.emit("vector", lambda e: e.tensor_copy(
            bkpair[:], psB2[:, 0:8]), waits=[tt2])
        t_bv = sch.emit("vector", lambda e: e.tensor_copy(
            biav[:], psB2[:, 8:16]), waits=[tt2])
        for (dst, srcb, seg0, twt) in ((biaqA, bqpair, 0, t_bq),
                                       (biaqB, bqpair, 32, t_bq),
                                       (biakA, bkpair, 0, t_bk),
                                       (biakB, bkpair, 32, t_bk)):
            for h_ in range(2):
                for dd in range(2):
                    sch.emit("vector", lambda e, dst=dst, srcb=srcb, seg0=seg0,
                             h_=h_, dd=dd: e.tensor_copy(
                        dst[h_ * 64 + dd * 32:h_ * 64 + dd * 32 + 32, :],
                        srcb[h_ * 64 + seg0:h_ * 64 + seg0 + 32, :]), waits=[twt])
        t_bias = sch.tok("vector")
        t_on = sch.emit("gpsimd", lambda e: e.memset(onesf[:], 1.0))
        sch.emit("gpsimd", lambda e: e.memset(bm4[:], -4.0))
        for hh in range(2):
            for kt in range(16):
                sch.emit("gpsimd", lambda e, h=hh, k=kt: e.memset(
                    vhat[h][:, k * 66 + 64: k * 66 + 65], 1.0))
        tb1 = sch.emit("tensor", lambda e: e.matmul(
            psA[:, 0:512], onesf[:, 0:128], borow[:, 0:512], start=True, stop=True),
            waits=[t_bo, t_on])
        tb2 = sch.emit("tensor", lambda e: e.matmul(
            psB[:, 0:512], onesf[:, 0:128], borow[:, 512:1024], start=True,
            stop=True), waits=[t_bo, t_on])
        sch.emit("vector", lambda e: e.tensor_copy(bo_bc[:, 0:512], psA[:, 0:512]),
                 waits=[tb1])
        t_bobc = sch.emit("vector", lambda e: e.tensor_copy(
            bo_bc[:, 512:1024], psB[:, 0:512]), waits=[tb2])
        sch.flush()

        wo_loaded = False
        a0_exp = [[None, None], [None, None]]
        a0_pv = [[None, None], [None, None]]
        a0_norm = [None, None]
        a0_attnw = [None, None]
        for s in range(SEQ_PER_CORE):
            row0 = s * S
            emit_ln_to_fm(sch, st, x_h, row0, 16, ln_fm, S, 0)
            sch.flush()

            for hp in range(8):
                c0 = hp * 128

                def dup_load(col_h, colbase, seg, dst_bf, wslot):
                    wf = w_f[wslot % 2]
                    toks = []
                    for h_ in range(2):
                        srcap = col_h[:, colbase + h_ * 64 + seg:
                                      colbase + h_ * 64 + seg + 32]
                        srcap = srcap.rearrange("(c p) m -> p c m", p=128)
                        dst = wf.ap().rearrange("p (c d m) -> p c d m", c=8, d=4)
                        for dd in range(2):
                            toks.append(sch.dma(
                                "sync", dst[:, :, h_ * 2 + dd, :], srcap,
                                waits=[wfree[wslot % 2]]))
                    return sch.emit("vector", lambda e, wf=wf, d=dst_bf:
                                    e.tensor_copy(d[:], wf[:]), waits=toks)

                wfree = [None, None]
                t1 = dup_load(wq_h, c0, 0, wqA, 0)
                wfree[0] = t1
                t2 = dup_load(wq_h, c0, 32, wqB, 1)
                wfree[1] = t2
                t3 = dup_load(wkv_h, c0, 0, wkA, 0)
                wfree[0] = t3
                t4 = dup_load(wkv_h, c0, 32, wkB, 1)
                wfree[1] = t4
                wfv = w_f[0]
                t5 = sch.dma("sync", wfv.ap().rearrange("p (c m) -> p c m", c=8),
                             wkv_h[:, D + c0:D + c0 + 128].rearrange(
                                 "(c p) m -> p c m", p=128), waits=[wfree[0]])
                t_wv = sch.emit("vector", lambda e: e.tensor_copy(wv_b[:], wfv[:]),
                                waits=[t5])
                tw = {"qA": t1, "qB": t2, "kA": t3, "kB": t4, "v": t_wv}
                sch.flush()

                bank_free = {id(psA): [], id(psB): [], id(psA2): [], id(psB2): []}

                def mmgroup(pb, wb, tc, extra_waits):
                    mmt = None
                    for c in range(8):
                        mmt = sch.emit("tensor", lambda e, c=c, tc=tc, wb=wb, pb=pb:
                                       e.matmul(
                            pb[:, 0:512], wb[:, c * 128:(c + 1) * 128],
                            ln_fm[:, c * S + tc * 512:c * S + (tc + 1) * 512],
                            start=(c == 0), stop=(c == 7)),
                            inc=(1 if c == 7 else 0),
                            waits=(bank_free[id(pb)] + extra_waits) if c == 0 else [])
                    return mmt

                for tc in range(4):
                    pa = [psA, psB][tc % 2]
                    pb2 = [psA2, psB2][tc % 2]
                    mA = mmgroup(pa, wqA, tc, [tw["qA"]])
                    mB = mmgroup(pb2, wqB, tc, [tw["qB"]])
                    tr1 = sch.emit("vector", lambda e, pa=pa, hp=hp, tc=tc:
                                   e.scalar_tensor_tensor(
                        ropeS[0][:], pa[:, 0:512], biaqA[:, hp:hp + 1],
                        csf[:, tc * 512:(tc + 1) * 512], ALU.add, ALU.mult),
                        waits=[mA, t_bias])
                    tr2 = sch.emit("vector", lambda e, pb2=pb2, hp=hp, tc=tc:
                                   e.scalar_tensor_tensor(
                        ropeS[1][:], pb2[:, 0:512], biaqB[:, hp:hp + 1],
                        scf[:, tc * 512:(tc + 1) * 512], ALU.add, ALU.mult),
                        waits=[mB, t_bias])
                    tq = sch.emit("vector", lambda e, tc=tc: e.tensor_tensor(
                        qr_b[:, tc * 512:(tc + 1) * 512], ropeS[0][:], ropeS[1][:],
                        ALU.add), waits=[tr1, tr2])
                    bank_free[id(pa)] = [tr1]
                    bank_free[id(pb2)] = [tr2]
                    mA = mmgroup(pa, wkA, tc, [tw["kA"]])
                    mB = mmgroup(pb2, wkB, tc, [tw["kB"]])
                    tr1 = sch.emit("vector", lambda e, pa=pa, hp=hp, tc=tc:
                                   e.scalar_tensor_tensor(
                        ropeS[0][:], pa[:, 0:512], biakA[:, hp:hp + 1],
                        csf[:, tc * 512:(tc + 1) * 512], ALU.add, ALU.mult),
                        waits=[mA])
                    tr2 = sch.emit("vector", lambda e, pb2=pb2, hp=hp, tc=tc:
                                   e.scalar_tensor_tensor(
                        ropeS[1][:], pb2[:, 0:512], biakB[:, hp:hp + 1],
                        scf[:, tc * 512:(tc + 1) * 512], ALU.add, ALU.mult),
                        waits=[mB])
                    tk = sch.emit("vector", lambda e, tc=tc: e.tensor_tensor(
                        kr_b[:, tc * 512:(tc + 1) * 512], ropeS[0][:], ropeS[1][:],
                        ALU.add), waits=[tr1, tr2])
                    bank_free[id(pa)] = [tr1]
                    bank_free[id(pb2)] = [tr2]
                    mV = mmgroup(pa, wv_b, tc, [tw["v"]])
                    tv = sch.emit("scalar", lambda e, pa=pa, hp=hp, tc=tc:
                                  e.activation(
                        v_b[:, tc * 512:(tc + 1) * 512], pa[:, 0:512], AF.Identity,
                        bias=biav[:, hp:hp + 1]), waits=[mV, t_bias])
                    bank_free[id(pa)] = [tv]
                t_v = sch.tok("scalar")
                sch.flush()

                for hh in range(2):
                    for kt in range(16):
                        pb = st["ptr"][kt % 2]
                        t_tr = sch.emit("tensor", lambda e, hh=hh, kt=kt, pb=pb:
                                        e.transpose(
                            pb[0:128, 0:64],
                            v_b[hh * 64:(hh + 1) * 64, kt * 128:(kt + 1) * 128],
                            st["id_bf"][hh * 64:(hh + 1) * 64,
                                        hh * 64:(hh + 1) * 64]),
                            waits=[st["ptr_free"][kt % 2], t_v])
                        t_cp = sch.emit("vector", lambda e, hh=hh, kt=kt, pb=pb:
                                        e.tensor_copy(
                            vhat[hh][:, kt * 66:kt * 66 + 64], pb[0:128, 0:64]),
                            waits=[t_tr])
                        st["ptr_free"][kt % 2] = t_cp
                sch.flush()

                for qc in range(4):
                    kt_hi = 4 * (qc + 1)
                    for kt in range(kt_hi):
                        sb = kt % 2
                        psx = [[psA, psA2][sb], [psB, psB2][sb]]
                        ebufs = [eA[sb], eB[sb]]
                        d = kt - 4 * qc
                        for hh in range(2):
                            tqk = sch.emit("tensor", lambda e, p=psx[hh], hh=hh,
                                           kt=kt, qc=qc: e.matmul(
                                p[:, 0:512],
                                kr_b[hh * 64:(hh + 1) * 64, kt * 128:(kt + 1) * 128],
                                qr_b[hh * 64:(hh + 1) * 64, qc * 512:(qc + 1) * 512],
                                start=True, stop=True, tile_position=(hh * 64, 0)),
                                waits=[a0_exp[hh][sb], a0_norm[hh]])
                            texp = sch.emit("scalar", lambda e, p=psx[hh],
                                            d2=ebufs[hh]: e.activation(
                                d2[:], p[:, 0:512], AF.Exp, scale=0.125,
                                bias=bm4[:, 0:1]),
                                waits=[tqk, a0_pv[hh][sb]])
                            if d >= 0:
                                texp = sch.emit("vector", lambda e, d=d,
                                                d2=ebufs[hh]: e.tensor_tensor(
                                    d2[:], d2[:], dmask[:, d * 512:(d + 1) * 512],
                                    ALU.mult), waits=[texp, t_dm])
                            po = [poA, poB][hh]
                            tpv = sch.emit("tensor", lambda e, hh=hh, kt=kt, po=po,
                                           d2=ebufs[hh], last=(kt == kt_hi - 1):
                                           e.matmul(
                                po[0:65, 0:512], vhat[hh][:, kt * 66:kt * 66 + 65],
                                d2[:], start=(kt == 0), stop=last),
                                waits=[texp] + ([a0_attnw[hh]] if kt == 0 else []))
                            a0_exp[hh][sb] = texp
                            a0_pv[hh][sb] = tpv
                    for hh in range(2):
                        po = [poA, poB][hh]
                        pn = [psA, psB][hh]
                        trc = sch.emit("vector", lambda e, po=po, hh=hh:
                                       e.reciprocal(rcp[0:1, hh * 512:(hh + 1) * 512],
                                                    po[64:65, 0:512]),
                                       waits=[a0_pv[hh][(kt_hi - 1) % 2]])
                        tbc = sch.emit("tensor", lambda e, hh=hh, pn=pn: e.matmul(
                            pn[:, 0:512], onesf[:, 0:128],
                            rcp[0:1, hh * 512:(hh + 1) * 512], start=True, stop=True),
                            waits=[trc, a0_exp[hh][0], a0_exp[hh][1]])
                        tcb = sch.emit("vector", lambda e, hh=hh, pn=pn:
                                       e.tensor_copy(
                            sbc[:, hh * 512:(hh + 1) * 512], pn[:, 0:512]),
                            waits=[tbc])
                        a0_norm[hh] = tcb
                        a0_attnw[hh] = sch.emit(
                            "vector", lambda e, po=po, hh=hh, qc=qc, hp=hp:
                            e.tensor_tensor(
                                attn_fm[hh * 64:(hh + 1) * 64,
                                        hp * S + qc * 512:hp * S + (qc + 1) * 512],
                                po[0:64, 0:512],
                                sbc[0:64, hh * 512:(hh + 1) * 512],
                                ALU.mult), waits=[tcb])

            if not wo_loaded:
                wo_loaded = True
                prev = None
                for pc in range(8):
                    wf = w_f[pc % 2]
                    t_l = sch.dma("sync", wf[:],
                                  wo_h[pc * 128:(pc + 1) * 128, :],
                                  waits=[prev])
                    prev = sch.emit("vector", lambda e, pc=pc, wf=wf: e.tensor_copy(
                        wob[:, pc * 1024:(pc + 1) * 1024], wf[:]), waits=[t_l])
                sch.flush()

            obank_free = [[a0_norm[0]], [a0_norm[1]]]
            prev_y = [None, None]
            prev_st = [None, None]
            for tt in range(16):
                p = tt % 2
                xb = xrt[p]
                yb = yt[p]
                t_x = sch.dma("sync", xb[:],
                              x_h[row0 + tt * 128:row0 + (tt + 1) * 128, :],
                              waits=[prev_y[p]])
                mml = [None, None]
                for half in range(2):
                    pb = [psA, psB][half]
                    for pc in range(8):
                        mml[half] = sch.emit("tensor", lambda e, pc=pc, half=half,
                                             tt=tt, pb=pb: e.matmul(
                            pb[:, 0:512],
                            attn_fm[:, pc * S + tt * 128:pc * S + (tt + 1) * 128],
                            wob[:, pc * 1024 + half * 512:
                                pc * 1024 + half * 512 + 512],
                            start=(pc == 0), stop=(pc == 7)),
                            inc=(1 if pc == 7 else 0),
                            waits=obank_free[half] if pc == 0 else [])
                t_a1 = sch.emit("vector", lambda e, x=xb, y=yb: e.tensor_tensor(
                    y[:], x[:], bo_bc[:], ALU.add),
                    waits=[t_x, t_bobc, prev_st[p]])
                t_y0 = sch.emit("vector", lambda e, y=yb: e.tensor_tensor(
                    y[:, 0:512], y[:, 0:512], psA[:, 0:512], ALU.add),
                    waits=[t_a1, mml[0]])
                t_y1 = sch.emit("vector", lambda e, y=yb: e.tensor_tensor(
                    y[:, 512:1024], y[:, 512:1024], psB[:, 0:512], ALU.add),
                    waits=[t_a1, mml[1]])
                obank_free[0] = [t_y0]
                obank_free[1] = [t_y1]
                prev_y[p] = t_y1
                prev_st[p] = sch.dma(
                    "sync", y_h[row0 + tt * 128:row0 + (tt + 1) * 128, :],
                    yb[:], waits=[t_y0, t_y1])
            sch.flush()


def build_attn0():
    nc = bass.Bass("TRN2", target_bir_lowering=False)
    hs = {}
    hs["x_h"] = nc.dram_tensor("x", [SEQ_PER_CORE * S, D], F32,
                               kind="ExternalInput").ap()
    hs["wq_h"] = nc.dram_tensor("wq", [D, D], F32, kind="ExternalInput").ap()
    hs["wkv_h"] = nc.dram_tensor("wkv", [D, 2 * D], F32, kind="ExternalInput").ap()
    hs["wo_h"] = nc.dram_tensor("wo", [D, D], F32, kind="ExternalInput").ap()
    hs["bq_h"] = nc.dram_tensor("bq", [D, 1], F32, kind="ExternalInput").ap()
    hs["bkv_h"] = nc.dram_tensor("bkv", [2 * D, 1], F32, kind="ExternalInput").ap()
    hs["bo_h"] = nc.dram_tensor("bo", [1, D], F32, kind="ExternalInput").ap()
    hs["cs_h"] = nc.dram_tensor("csfull", [128, S], F32, kind="ExternalInput").ap()
    hs["sc_h"] = nc.dram_tensor("scfull", [128, S], F32, kind="ExternalInput").ap()
    hs["id_h"] = nc.dram_tensor("ident", [128, 128], F32, kind="ExternalInput").ap()
    hs["dm_h"] = nc.dram_tensor("diagmask", [128, 4 * 512], BF16,
                                kind="ExternalInput").ap()
    hs["y_h"] = nc.dram_tensor("y", [SEQ_PER_CORE * S, D], F32,
                               kind="ExternalOutput").ap()
    with ExitStack() as ctx:
        sch = Sch(nc, ctx)
        emit_attn0(nc, sch, ctx, **hs)
        t_fin = sch.tok("sync")
        sch.emit("gpsimd", lambda e: None, inc=1, waits=[t_fin])
        sch.flush()
    return nc


def emit_attn1(nc, sch, ctx, x_h, y_h, wq_h, wk_h, wv_h, wo_h, wf_h, id_h, dm_h):
    if True:
        st = make_ln_state(nc, ctx)
        ln_fm = sbuf(nc, ctx, "lnfm", [128, 8 * S], BF16)
        attn_fm = sbuf(nc, ctx, "attnfm", [128, 8 * S], BF16)
        w_f = [sbuf(nc, ctx, f"wf{i}", [128, 1024], F32) for i in range(2)]
        wq_b = sbuf(nc, ctx, "wqb", [128, 1024], BF16)
        wk_b = sbuf(nc, ctx, "wkb", [128, 1024], BF16)
        wv_b = sbuf(nc, ctx, "wvb", [128, 1024], BF16)
        wfb = sbuf(nc, ctx, "wfb", [128, 128], BF16)
        qhA = sbuf(nc, ctx, "qhA", [68, S], BF16)
        qhB = sbuf(nc, ctx, "qhB", [68, S], BF16)
        khA = sbuf(nc, ctx, "khA", [68, S], BF16)
        khB = sbuf(nc, ctx, "khB", [68, S], BF16)
        v_b = sbuf(nc, ctx, "vb", [128, S], BF16)
        vhat = [sbuf(nc, ctx, f"vhat{h}", [128, 16 * 66], BF16) for h in range(2)]
        U1 = sbuf(nc, ctx, "U1", [128, S], F32)
        U2 = sbuf(nc, ctx, "U2", [128, S], F32)
        Ubf = sbuf(nc, ctx, "Ubf", [128, S], BF16)
        lst = [sbuf(nc, ctx, f"lst{i}", [128, 16], F32) for i in range(4)]
        lsig = sbuf(nc, ctx, "lsig", [128, 16], F32)
        eA = [sbuf(nc, ctx, f"eA{i}", [128, 512], BF16) for i in range(2)]
        dmask = sbuf(nc, ctx, "dmask", [128, 4 * 512], F32)
        rcp = sbuf(nc, ctx, "rcp", [1, 1024], F32)
        sbc = sbuf(nc, ctx, "sbc", [128, 512], F32)
        onesf = sbuf(nc, ctx, "onesf", [1, 128], F32)
        bm4 = sbuf(nc, ctx, "bm4", [128, 1], F32)
        idf32 = sbuf(nc, ctx, "idf32C", [128, 128], F32)
        wob = sbuf(nc, ctx, "wob", [128, 8 * 1024], BF16)
        xrt = [sbuf(nc, ctx, f"xrt{i}", [128, D], F32) for i in range(2)]
        yt = [sbuf(nc, ctx, f"yt{i}", [128, D], F32) for i in range(2)]

        psA = psum(nc, ctx, "psA", [128, 512], F32)
        psB = psum(nc, ctx, "psB", [128, 512], F32)
        psA2 = psum(nc, ctx, "psA2", [128, 512], F32)
        psB2 = psum(nc, ctx, "psB2", [128, 512], F32)
        poA = psum(nc, ctx, "poA", [128, 512], F32)
        poB = psum(nc, ctx, "poB", [128, 512], F32)
        pT = psum(nc, ctx, "pT", [128, 1024], BF16)
        pT2 = psum(nc, ctx, "pT2", [128, 1024], BF16)
        st["ptr"] = [PHalf(pT, 0), PHalf(pT2, 0)]

        t_dm = sch.dma("sync", dmask[:], dm_h)
        emit_load_ident(sch, st, id_h)
        t_idf = sch.dma("sync", idf32[:], id_h)
        t_on = sch.emit("gpsimd", lambda e: e.memset(onesf[:], 1.0))
        sch.emit("gpsimd", lambda e: e.memset(bm4[:], -4.0))
        t_z = sch.emit("gpsimd", lambda e: e.memset(U2[0:16, :], 0.0))
        for qk in (qhA, qhB, khA, khB):
            sch.emit("gpsimd", lambda e, qk=qk: e.memset(qk[64:68, :], 1.0))
        for hh in range(2):
            for kt in range(16):
                sch.emit("gpsimd", lambda e, h=hh, k=kt: e.memset(
                    vhat[h][:, k * 66 + 64: k * 66 + 65], 1.0))
        t_wf = sch.dma("sync", w_f[0].ap().rearrange(
            "p (c m) -> p c m", c=8)[:, :, 0:16],
            wf_h.rearrange("(c p) m -> p c m", p=128))
        t_wfb = sch.emit("vector", lambda e: e.tensor_copy(
            wfb.ap().rearrange("p (c m) -> p c m", c=8),
            w_f[0].ap().rearrange("p (c m) -> p c m", c=8)[:, :, 0:16]),
            waits=[t_wf])
        st["xin_free"][0] = t_wfb
        wo_toks = []
        for pc in range(8):
            wfs = w_f[pc % 2]
            t_l = sch.dma("sync", wfs[:], wo_h[pc * 128:(pc + 1) * 128, :],
                          waits=[wo_toks[-1] if wo_toks else t_wfb])
            t_c = sch.emit("vector", lambda e, pc=pc, wfs=wfs: e.tensor_copy(
                wob[:, pc * 1024:(pc + 1) * 1024], wfs[:]), waits=[t_l])
            wo_toks.append(t_c)
        sch.flush()

        g_exp = [None, None]
        g_pv = [None, None]
        g_attn = None
        for s in range(SEQ_PER_CORE):
            row0 = s * S
            emit_ln_to_fm(sch, st, x_h, row0, 16, ln_fm, S, 0)
            sch.flush()

            pv_a = pv_m = pv_tr = pv_cp = None
            for tt in range(16):
                mmt = None
                for c in range(8):
                    mmt = sch.emit("tensor", lambda e, c=c, tt=tt: e.matmul(
                        psB2[:, 0:16],
                        ln_fm[:, c * S + tt * 128:c * S + (tt + 1) * 128],
                        wfb[:, c * 16:(c + 1) * 16],
                        start=(c == 0), stop=(c == 7)),
                        inc=(1 if c == 7 else 0),
                        waits=[pv_a, pv_m] if c == 0 else [])
                t_a = sch.emit("scalar", lambda e: e.activation(
                    lst[0][:], psB2[:, 0:16], AF.Abs), waits=[mmt])
                t_e = sch.emit("scalar", lambda e: e.activation(
                    lst[1][:], lst[0][:], AF.Exp, scale=-1.0), waits=[t_a])
                t_l = sch.emit("scalar", lambda e: e.activation(
                    lst[2][:], lst[1][:], AF.Ln, bias=1.0), waits=[t_e])
                t_m = sch.emit("vector", lambda e: e.tensor_scalar(
                    lst[3][:], psB2[:, 0:16], 0.0, None, ALU.min), waits=[mmt])
                t_s = sch.emit("vector", lambda e: e.tensor_tensor(
                    lsig[:], lst[3][:], lst[2][:], ALU.subtract),
                    waits=[t_l, t_m, pv_tr])
                t_tr = sch.emit("tensor", lambda e: e.transpose(
                    psA2[0:16, 0:128], lsig[:], idf32[:]),
                    waits=[t_s, t_idf, pv_cp])
                t_cp = sch.emit("vector", lambda e, tt=tt: e.tensor_copy(
                    U1[0:16, tt * 128:(tt + 1) * 128], psA2[0:16, 0:128]),
                    waits=[t_tr])
                pv_a, pv_m, pv_tr, pv_cp = t_a, t_m, t_tr, t_cp
            t_scan = sch.emit("vector", lambda e: e.tensor_tensor_scan(
                U1[32:48, :], U1[0:16, :], U2[0:16, :], 0.0, ALU.add, ALU.add),
                waits=[t_z, sch.tok("vector")])
            t_hi = sch.emit("vector", lambda e: e.tensor_copy(
                Ubf[0:16, :], U1[32:48, :]), waits=[t_scan])
            t_hf = sch.emit("vector", lambda e: e.tensor_copy(
                U2[32:48, :], Ubf[0:16, :]), waits=[t_hi])
            t_lo = sch.emit("vector", lambda e: e.tensor_tensor(
                U1[64:80, :], U1[32:48, :], U2[32:48, :], ALU.subtract),
                waits=[t_hf])
            t_lob = sch.emit("vector", lambda e: e.tensor_copy(
                Ubf[64:80, :], U1[64:80, :]), waits=[t_lo])
            t_nhi = sch.emit("vector", lambda e: e.tensor_scalar(
                Ubf[32:48, :], Ubf[0:16, :], -1.0, None, ALU.mult), waits=[t_hi])
            t_nlo = sch.emit("vector", lambda e: e.tensor_scalar(
                Ubf[96:112, :], Ubf[64:80, :], -1.0, None, ALU.mult),
                waits=[t_lob])
            sch.flush()

            for hp in range(8):
                c0 = hp * 128
                tws = []
                for (w_hx, dstb) in ((wq_h, wq_b), (wk_h, wk_b), (wv_h, wv_b)):
                    sl = len(tws) % 2
                    t_w = sch.dma("sync",
                                  w_f[sl].ap().rearrange("p (c m) -> p c m", c=8),
                                  w_hx[:, c0:c0 + 128].rearrange(
                                      "(c p) m -> p c m", p=128),
                                  waits=[st.get(f"wfree{sl}")])
                    t_c = sch.emit("vector", lambda e, sl=sl, d=dstb: e.tensor_copy(
                        d[:], w_f[sl][:]), waits=[t_w, g_exp[0], g_exp[1]])
                    st[f"wfree{sl}"] = t_c
                    tws.append(t_c)

                bank_free = {id(psA): [g_exp[0], g_exp[1]], id(psB): [g_attn]}

                def mmgroup(pb, wb, tc, extra):
                    mmt = None
                    for c in range(8):
                        mmt = sch.emit("tensor", lambda e, c=c, tc=tc, wb=wb,
                                       pb=pb: e.matmul(
                            pb[:, 0:512], wb[:, c * 128:(c + 1) * 128],
                            ln_fm[:, c * S + tc * 512:c * S + (tc + 1) * 512],
                            start=(c == 0), stop=(c == 7)),
                            inc=(1 if c == 7 else 0),
                            waits=(bank_free[id(pb)] + extra) if c == 0 else [])
                    return mmt

                for tc in range(4):
                    pb = [psA, psB][tc % 2]
                    mq = mmgroup(pb, wq_b, tc, [tws[0]])
                    tq0 = sch.emit("scalar", lambda e, pb=pb, tc=tc: e.activation(
                        qhA[0:64, tc * 512:(tc + 1) * 512],
                        pb[0:64, 0:512], AF.Identity, scale=0.125), waits=[mq])
                    tq1 = sch.emit("scalar", lambda e, pb=pb, tc=tc: e.activation(
                        qhB[0:64, tc * 512:(tc + 1) * 512],
                        pb[64:128, 0:512], AF.Identity, scale=0.125), waits=[mq])
                    bank_free[id(pb)] = [tq0, tq1]
                    mk = mmgroup(pb, wk_b, tc, [tws[1]])
                    tk0 = sch.emit("scalar", lambda e, pb=pb, tc=tc: e.activation(
                        khA[0:64, tc * 512:(tc + 1) * 512],
                        pb[0:64, 0:512], AF.Identity), waits=[mk])
                    tk1 = sch.emit("scalar", lambda e, pb=pb, tc=tc: e.activation(
                        khB[0:64, tc * 512:(tc + 1) * 512],
                        pb[64:128, 0:512], AF.Identity), waits=[mk])
                    bank_free[id(pb)] = [tk0, tk1]
                    mv = mmgroup(pb, wv_b, tc, [tws[2]])
                    tv = sch.emit("scalar", lambda e, pb=pb, tc=tc: e.activation(
                        v_b[:, tc * 512:(tc + 1) * 512], pb[:, 0:512],
                        AF.Identity), waits=[mv])
                    bank_free[id(pb)] = [tv]
                t_v = sch.tok("scalar")

                t_rs = [[], []]
                for hh2, (qh, kh) in enumerate(((qhA, khA), (qhB, khB))):
                    h_abs = 2 * hp + hh2
                    t_rs[hh2].append(sch.dma(
                        "sync", qh[64:65, :], Ubf[h_abs:h_abs + 1, :],
                        waits=[t_hi, g_exp[0], g_exp[1]]))
                    t_rs[hh2].append(sch.dma(
                        "sync", qh[66:67, :], Ubf[64 + h_abs:64 + h_abs + 1, :],
                        waits=[t_lob]))
                    t_rs[hh2].append(sch.dma(
                        "sync", kh[65:66, :], Ubf[32 + h_abs:32 + h_abs + 1, :],
                        waits=[t_nhi]))
                    t_rs[hh2].append(sch.dma(
                        "sync", kh[67:68, :], Ubf[96 + h_abs:96 + h_abs + 1, :],
                        waits=[t_nlo]))

                for h2 in range(2):
                    for kt in range(16):
                        pb2 = st["ptr"][kt % 2]
                        t_tr = sch.emit("tensor", lambda e, h2=h2, kt=kt,
                                        pb2=pb2: e.transpose(
                            pb2[0:128, 0:64],
                            v_b[h2 * 64:(h2 + 1) * 64,
                                kt * 128:(kt + 1) * 128],
                            st["id_bf"][h2 * 64:(h2 + 1) * 64,
                                        h2 * 64:(h2 + 1) * 64]),
                            waits=[st["ptr_free"][kt % 2], t_v])
                        t_cp = sch.emit("vector", lambda e, h2=h2, kt=kt,
                                        pb2=pb2: e.tensor_copy(
                            vhat[h2][:, kt * 66:kt * 66 + 64],
                            pb2[0:128, 0:64]), waits=[t_tr])
                        st["ptr_free"][kt % 2] = t_cp

                for hh in range(2):
                    qh, kh = ((qhA, khA), (qhB, khB))[hh]
                    vh = vhat[hh]
                    for qc in range(4):
                        kt_hi = 4 * (qc + 1)
                        for kt in range(kt_hi):
                            sb = kt % 2
                            px = [psA2, psB2][sb]
                            ebuf = eA[sb]
                            d = kt - 4 * qc
                            w0 = [g_exp[sb]]
                            if qc == 0 and kt == 0:
                                w0 += t_rs[hh]
                            tqk = sch.emit("tensor", lambda e, px=px, kt=kt,
                                           qc=qc, qh=qh, kh=kh: e.matmul(
                                px[:, 0:512],
                                kh[0:68, kt * 128:(kt + 1) * 128],
                                qh[0:68, qc * 512:(qc + 1) * 512],
                                start=True, stop=True), waits=w0)
                            tmask = None
                            if d >= 0:
                                tmask = sch.emit("vector", lambda e, px=px, d=d:
                                                 e.tensor_tensor(
                                    px[:, 0:512], px[:, 0:512],
                                    dmask[:, d * 512:(d + 1) * 512], ALU.add),
                                    waits=[tqk, t_dm])
                            texp = sch.emit("scalar", lambda e, px=px, d2=ebuf:
                                            e.activation(
                                d2[:], px[:, 0:512], AF.Exp, scale=1.0,
                                bias=bm4[:, 0:1]),
                                waits=[tmask or tqk, g_pv[sb]])
                            tpv = sch.emit("tensor", lambda e, kt=kt, vh=vh,
                                           d2=ebuf, last=(kt == kt_hi - 1):
                                           e.matmul(
                                poA[0:65, 0:512],
                                vh[:, kt * 66:kt * 66 + 65],
                                d2[:], start=(kt == 0), stop=last),
                                waits=[texp] + ([g_attn] if kt == 0 else []))
                            g_exp[sb] = texp
                            g_pv[sb] = tpv
                        trc = sch.emit("vector", lambda e: e.reciprocal(
                            rcp[0:1, 0:512], poA[64:65, 0:512]),
                            waits=[g_pv[(kt_hi - 1) % 2]])
                        tbc = sch.emit("tensor", lambda e: e.matmul(
                            psB[:, 0:512], onesf[:, 0:128], rcp[0:1, 0:512],
                            start=True, stop=True), waits=[trc])
                        tcb = sch.emit("vector", lambda e: e.tensor_copy(
                            sbc[:], psB[:, 0:512]), waits=[tbc])
                        g_attn = sch.emit("vector", lambda e, hh=hh, qc=qc,
                                          hp=hp: e.tensor_tensor(
                            attn_fm[hh * 64:(hh + 1) * 64,
                                    hp * S + qc * 512:hp * S + (qc + 1) * 512],
                            poA[0:64, 0:512], sbc[0:64, :], ALU.mult),
                            waits=[tcb])
            sch.flush()

            obank_free = [None, None]
            prev_y = [None, None]
            prev_st = [None, None]
            for tt in range(16):
                p = tt % 2
                xb = xrt[p]
                yb = yt[p]
                t_x = sch.dma("sync", xb[:],
                              x_h[row0 + tt * 128:row0 + (tt + 1) * 128, :],
                              waits=[prev_y[p]])
                mml = [None, None]
                for half in range(2):
                    pb = [psA, psB][half]
                    for pc in range(8):
                        mml[half] = sch.emit("tensor", lambda e, pc=pc, half=half,
                                             tt=tt, pb=pb: e.matmul(
                            pb[:, 0:512],
                            attn_fm[:, pc * S + tt * 128:pc * S + (tt + 1) * 128],
                            wob[:, pc * 1024 + half * 512:
                                pc * 1024 + half * 512 + 512],
                            start=(pc == 0), stop=(pc == 7)),
                            inc=(1 if pc == 7 else 0),
                            waits=[obank_free[half]] if pc == 0 else [])
                t_y0 = sch.emit("vector", lambda e, x=xb, y=yb: e.tensor_tensor(
                    y[:, 0:512], x[:, 0:512], psA[:, 0:512], ALU.add),
                    waits=[t_x, mml[0], prev_st[p]])
                t_y1 = sch.emit("vector", lambda e, x=xb, y=yb: e.tensor_tensor(
                    y[:, 512:1024], x[:, 512:1024], psB[:, 0:512], ALU.add),
                    waits=[t_x, mml[1], prev_st[p]])
                obank_free[0] = t_y0
                obank_free[1] = t_y1
                prev_y[p] = t_y1
                prev_st[p] = sch.dma(
                    "sync", y_h[row0 + tt * 128:row0 + (tt + 1) * 128, :],
                    yb[:], waits=[t_y0, t_y1])
            sch.flush()


def build_attn1():
    nc = bass.Bass("TRN2", target_bir_lowering=False)
    hs = {}
    hs["x_h"] = nc.dram_tensor("x", [SEQ_PER_CORE * S, D], F32,
                               kind="ExternalInput").ap()
    hs["wq_h"] = nc.dram_tensor("wq", [D, D], F32, kind="ExternalInput").ap()
    hs["wk_h"] = nc.dram_tensor("wk", [D, D], F32, kind="ExternalInput").ap()
    hs["wv_h"] = nc.dram_tensor("wv", [D, D], F32, kind="ExternalInput").ap()
    hs["wo_h"] = nc.dram_tensor("wo", [D, D], F32, kind="ExternalInput").ap()
    hs["wf_h"] = nc.dram_tensor("wf", [D, H], F32, kind="ExternalInput").ap()
    hs["id_h"] = nc.dram_tensor("ident", [128, 128], F32, kind="ExternalInput").ap()
    hs["dm_h"] = nc.dram_tensor("diagmaskA", [128, 4 * 512], F32,
                                kind="ExternalInput").ap()
    hs["y_h"] = nc.dram_tensor("y", [SEQ_PER_CORE * S, D], F32,
                               kind="ExternalOutput").ap()
    with ExitStack() as ctx:
        sch = Sch(nc, ctx)
        emit_attn1(nc, sch, ctx, **hs)
        t_fin = sch.tok("sync")
        sch.emit("gpsimd", lambda e: None, inc=1, waits=[t_fin])
        sch.flush()
    return nc


def _diag_masks_add():
    m = np.zeros((128, 4, 512), np.float32)
    p = np.arange(128)[:, None]
    o = np.arange(512)[None, :]
    for d in range(4):
        m[:, d, :] = np.where(p > o - 128 * d, NEG, 0.0)
    return m.reshape(128, 4 * 512)


def run_attn1(x, Wq, Wk, Wv, Wo, Wf):
    nc = build_attn1()
    in_maps = []
    Wq = np.ascontiguousarray(np.asarray(Wq, np.float32))
    Wk = np.ascontiguousarray(np.asarray(Wk, np.float32))
    Wv = np.ascontiguousarray(np.asarray(Wv, np.float32))
    Wo = np.ascontiguousarray(np.asarray(Wo, np.float32))
    Wf = np.ascontiguousarray(np.asarray(Wf, np.float32))
    dm = _diag_masks_add()
    for c in range(NCORES):
        xs = np.ascontiguousarray(
            x[c * SEQ_PER_CORE:(c + 1) * SEQ_PER_CORE].reshape(SEQ_PER_CORE * S, D))
        in_maps.append(dict(x=xs, wq=Wq, wk=Wk, wv=Wv, wo=Wo, wf=Wf,
                            diagmaskA=dm, ident=_ident()))
    res = run_bass_kernel_spmd(nc, in_maps, core_ids=list(range(NCORES)))
    return np.concatenate([r["y"].reshape(SEQ_PER_CORE, S, D)
                           for r in res.results], 0)


JW = 256
T_STEPS = 16


def build_ffn0():
    nc = bass.Bass("TRN2", target_bir_lowering=False)
    NROW = T_STEPS * JW
    x_h = nc.dram_tensor("x", [NROW, D], F32, kind="ExternalInput").ap()
    w_hs = {}
    for nm in ("wf", "uf", "wi", "ui", "wc", "uc", "wo", "uo"):
        w_hs[nm] = nc.dram_tensor(nm, [D, D], F32, kind="ExternalInput").ap()
    b_hs = {}
    for nm in ("bf", "bi", "bc", "bo", "inc"):
        b_hs[nm] = nc.dram_tensor(nm, [D, 1], F32, kind="ExternalInput").ap()
    id_h = nc.dram_tensor("ident", [128, 128], F32, kind="ExternalInput").ap()
    y_h = nc.dram_tensor("y", [NROW, D], F32, kind="ExternalOutput").ap()

    with ExitStack() as ctx:
        sch = Sch(nc, ctx)
        xg_fm = sbuf(nc, ctx, "xgfm", [128, 8 * NROW], BF16)
        wT = {nm: sbuf(nc, ctx, f"wT_{nm}", [128, 64 * 128], BF16)
              for nm in ("wf", "uf", "wi", "ui", "wc", "uc")}
        c_fm = sbuf(nc, ctx, "cfm", [128, 8 * JW], F32)
        c_prev = sbuf(nc, ctx, "cprev", [128, 8 * JW], BF16)
        biasp = sbuf(nc, ctx, "biasp", [128, 4 * 8], F32)
        idf32 = sbuf(nc, ctx, "idf32B", [128, 128], F32)
        pg = [psum(nc, ctx, f"pg{i}", [128, 512], F32) for i in range(6)]
        pT = psum(nc, ctx, "pTB", [128, 1024], BF16)
        pT2 = psum(nc, ctx, "pT2B", [128, 1024], BF16)

        with ExitStack() as c1:
            st = make_ln_state(nc, c1, "B")
            st["ptr"] = [PHalf(pT, 0), PHalf(pT2, 0)]
            wstg = st["xin"]
            wstgb = st["ln_tm"]
            bstg = sbuf(nc, c1, "bstgB", [32, 128], F32)

            t_idf = sch.dma("sync", idf32[:], id_h)
            emit_load_ident(sch, st, id_h)
            for i, nm in enumerate(("bf", "bi", "bc", "inc")):
                sch.dma("sync", bstg[i * 8:(i + 1) * 8, :],
                        b_hs[nm].rearrange("(c p) o -> c (p o)", c=8))
            t_bl = sch.tok("sync")
            t_btr = sch.emit("tensor", lambda e: e.transpose(
                pg[0][:, 0:32], bstg[:], idf32[0:32, 0:32]), waits=[t_bl, t_idf])
            t_bias = sch.emit("vector", lambda e: e.tensor_copy(
                biasp[:], pg[0][:, 0:32]), waits=[t_btr])
            sch.flush()

            ptr_free = [None, None]
            for wi_, nm in enumerate(("wf", "uf", "wi", "ui", "wc", "uc")):
                for mc in range(8):
                    sl = (wi_ * 8 + mc) % 2
                    t_l = sch.dma("sync", wstg[sl][:],
                                  w_hs[nm][mc * 128:(mc + 1) * 128, :],
                                  waits=[st.get(f"wstg_free{sl}")])
                    t_c = sch.emit("vector", lambda e, sl=sl: e.tensor_copy(
                        wstgb[sl][:], wstg[sl][:]),
                        waits=[t_l, st.get(f"wstgb_free{sl}")])
                    st[f"wstg_free{sl}"] = t_c
                    last = None
                    for kc in range(8):
                        pb = st["ptr"][kc % 2]
                        t_tr = sch.emit("tensor", lambda e, sl=sl, kc=kc, pb=pb:
                                        e.transpose(
                            pb[0:128, 0:128], wstgb[sl][:, kc * 128:(kc + 1) * 128],
                            st["id_bf"][:]), waits=[t_c, ptr_free[kc % 2]])
                        t_cp = sch.emit("vector", lambda e, nm=nm, kc=kc, mc=mc,
                                        pb=pb: e.tensor_copy(
                            wT[nm][:, (kc * 8 + mc) * 128:(kc * 8 + mc + 1) * 128],
                            pb[0:128, 0:128]), waits=[t_tr])
                        ptr_free[kc % 2] = t_cp
                        last = t_cp
                    st[f"wstgb_free{sl}"] = last
                    st["xin_free"][sl] = st[f"wstg_free{sl}"]
                    st["ln_free"][sl] = last
                sch.flush()

            st["ptr_free"] = ptr_free
            emit_ln_to_fm(sch, st, x_h, 0, 32, xg_fm, NROW, 0)
            sch.flush()

        with ExitStack() as c2:
            gsc = [sbuf(nc, c2, f"gsc{i}", [128, JW], F32) for i in range(3)]
            tmp1 = sbuf(nc, c2, "tmp1", [128, JW], F32)
            tmp2 = sbuf(nc, c2, "tmp2", [128, JW], F32)
            t_ms = sch.emit("gpsimd", lambda e: e.memset(c_fm[:], 0.0))
            for mc in range(8):
                sch.emit("vector", lambda e, mc=mc: e.tensor_scalar(
                    c_fm[:, mc * JW:(mc + 1) * JW], c_fm[:, mc * JW:(mc + 1) * JW],
                    biasp[:, 24 + mc:24 + mc + 1], None, ALU.add),
                    waits=[t_ms, t_bias])
            sch.flush()

            last_up = sch.tok("vector")
            last_mm_prev = None
            for t_ in range(T_STEPS):
                t_cpv = sch.emit("vector", lambda e: e.tensor_copy(
                    c_prev[:], c_fm[:]), waits=[last_up, last_mm_prev])
                evac_done = {}
                upd_done = {}
                for mc in range(8):
                    for gi_, (wnm, unm, act, bcol) in enumerate((
                            ("wf", "uf", AF.Sigmoid, 0),
                            ("wi", "ui", AF.Sigmoid, 1),
                            ("wc", "uc", AF.Tanh, 2))):
                        bank = pg[(mc % 2) * 3 + gi_]
                        mm = None
                        for kc in range(8):
                            mm = sch.emit("tensor", lambda e, wnm=wnm, kc=kc,
                                          mc=mc, t_=t_, bank=bank: e.matmul(
                                bank[:, 0:JW],
                                wT[wnm][:, (kc * 8 + mc) * 128:
                                        (kc * 8 + mc + 1) * 128],
                                xg_fm[:, kc * NROW + t_ * JW:
                                      kc * NROW + (t_ + 1) * JW],
                                start=(kc == 0), stop=False), inc=0,
                                waits=[evac_done.get(id(bank)), t_cpv]
                                if kc == 0 else [])
                        for kc in range(8):
                            mm = sch.emit("tensor", lambda e, unm=unm, kc=kc,
                                          mc=mc, bank=bank: e.matmul(
                                bank[:, 0:JW],
                                wT[unm][:, (kc * 8 + mc) * 128:
                                        (kc * 8 + mc + 1) * 128],
                                c_prev[:, kc * JW:(kc + 1) * JW],
                                start=False, stop=(kc == 7)),
                                inc=(1 if kc == 7 else 0))
                        evac_done[id(bank)] = sch.emit(
                            "scalar", lambda e, gi_=gi_, mc=mc, act=act,
                            bcol=bcol, bank=bank: e.activation(
                                gsc[gi_][:], bank[:, 0:JW], act,
                                bias=biasp[:, bcol * 8 + mc:bcol * 8 + mc + 1]),
                            waits=[mm, upd_done.get(gi_)])
                    tk1 = sch.emit("vector", lambda e, mc=mc: e.tensor_tensor(
                        tmp1[:], gsc[0][:], c_fm[:, mc * JW:(mc + 1) * JW],
                        ALU.mult), waits=[evac_done[id(pg[(mc % 2) * 3])], t_cpv])
                    tk2 = sch.emit("vector", lambda e: e.tensor_tensor(
                        tmp2[:], gsc[1][:], gsc[2][:], ALU.mult),
                        waits=[evac_done[id(pg[(mc % 2) * 3 + 1])],
                               evac_done[id(pg[(mc % 2) * 3 + 2])]])
                    t_up = sch.emit("vector", lambda e, mc=mc: e.tensor_tensor(
                        c_fm[:, mc * JW:(mc + 1) * JW], tmp1[:], tmp2[:], ALU.add),
                        waits=[tk1, tk2])
                    for gi_ in range(3):
                        upd_done[gi_] = t_up
                last_up = sch.tok("vector")
                last_mm_prev = sch.tok("tensor")
            sch.emit("vector", lambda e: e.tensor_copy(c_prev[:], c_fm[:]),
                     waits=[last_up, last_mm_prev])
            sch.flush()

        with ExitStack() as c3:
            wstg = [sbuf(nc, c3, f"wstgC{i}", [128, 1024], F32) for i in range(2)]
            wstgb = [sbuf(nc, c3, f"wstgbC{i}", [128, 1024], BF16) for i in range(2)]
            idb = sbuf(nc, c3, "idbC", [128, 128], BF16)
            borow = sbuf(nc, c3, "borowC", [1, D], F32)
            onesf = sbuf(nc, c3, "onesfC", [1, 128], F32)
            bo_bc = sbuf(nc, c3, "bobcC", [128, D], F32)
            cu_bo = [sbuf(nc, c3, f"cubo{i}", [128, D], F32) for i in range(2)]
            hsc = sbuf(nc, c3, "hsc", [128, D], F32)
            ptr_free = [None, None]
            t_idb = sch.emit("vector", lambda e: e.tensor_copy(idb[:], idf32[:]))
            woT, uoT = wT["wf"], wT["uf"]
            stg_free = [None, None]
            stgb_free = [None, None]
            for wi_, (nm, dst) in enumerate((("wo", woT), ("uo", uoT))):
                for mc in range(8):
                    sl = (wi_ * 8 + mc) % 2
                    t_l = sch.dma("sync", wstg[sl][:],
                                  w_hs[nm][mc * 128:(mc + 1) * 128, :],
                                  waits=[stg_free[sl]])
                    t_c = sch.emit("vector", lambda e, sl=sl: e.tensor_copy(
                        wstgb[sl][:], wstg[sl][:]), waits=[t_l, stgb_free[sl]])
                    stg_free[sl] = t_c
                    last = None
                    for kc in range(8):
                        pb = [PHalf(pT, 0), PHalf(pT2, 0)][kc % 2]
                        t_tr = sch.emit("tensor", lambda e, sl=sl, kc=kc, pb=pb:
                                        e.transpose(
                            pb[0:128, 0:128], wstgb[sl][:, kc * 128:(kc + 1) * 128],
                            idb[:]), waits=[t_c, ptr_free[kc % 2], t_idb])
                        t_cp = sch.emit("vector", lambda e, dst=dst, kc=kc, mc=mc,
                                        pb=pb: e.tensor_copy(
                            dst[:, (kc * 8 + mc) * 128:(kc * 8 + mc + 1) * 128],
                            pb[0:128, 0:128]), waits=[t_tr])
                        ptr_free[kc % 2] = t_cp
                        last = t_cp
                    stgb_free[sl] = last
            t_bo = sch.dma("sync", borow[:], b_hs["bo"].rearrange("p o -> o p"))
            t_on = sch.emit("gpsimd", lambda e: e.memset(onesf[:], 1.0))
            tb1 = sch.emit("tensor", lambda e: e.matmul(
                pg[0][:, 0:512], onesf[:, 0:128], borow[:, 0:512], start=True,
                stop=True), waits=[t_bo, t_on])
            tb2 = sch.emit("tensor", lambda e: e.matmul(
                pg[1][:, 0:512], onesf[:, 0:128], borow[:, 512:1024], start=True,
                stop=True), waits=[t_bo, t_on])
            sch.emit("vector", lambda e: e.tensor_copy(bo_bc[:, 0:512],
                                                       pg[0][:, 0:512]), waits=[tb1])
            t_bobc = sch.emit("vector", lambda e: e.tensor_copy(
                bo_bc[:, 512:1024], pg[1][:, 0:512]), waits=[tb2])
            sch.flush()

            for jt in range(2):
                mml = [None, None]
                for half in range(2):
                    bank = pg[2 + half]
                    for kc in range(8):
                        mml[half] = sch.emit("tensor", lambda e, kc=kc, jt=jt,
                                             half=half, bank=bank: e.matmul(
                            bank[:, 0:512],
                            c_prev[:, kc * JW + jt * 128:kc * JW + (jt + 1) * 128],
                            uoT[:, (kc * 8) * 128 + half * 512:
                                (kc * 8) * 128 + half * 512 + 512],
                            start=(kc == 0), stop=(kc == 7)),
                            inc=(1 if kc == 7 else 0))
                t_c1 = sch.emit("vector", lambda e, jt=jt: e.tensor_tensor(
                    cu_bo[jt][:, 0:512], pg[2][:, 0:512], bo_bc[:, 0:512], ALU.add),
                    waits=[mml[0], t_bobc])
                t_c2 = sch.emit("vector", lambda e, jt=jt: e.tensor_tensor(
                    cu_bo[jt][:, 512:1024], pg[3][:, 0:512], bo_bc[:, 512:1024],
                    ALU.add), waits=[mml[1], t_bobc])
            sch.flush()
            xrt = [wstg[0], bo_bc]
            yt = wstg[1]

            obank_free = [None, None]
            prev_th = None
            prev_store = None
            xrt_free = [sch.tok("vector"), sch.tok("vector")]
            for rt in range(32):
                p = rt % 2
                xb = xrt[p]
                yb = yt
                jt = rt % 2
                t_x = sch.dma("sync", xb[:], x_h[rt * 128:(rt + 1) * 128, :],
                              waits=[xrt_free[p]])
                mml = [None, None]
                for half in range(2):
                    bank = pg[4 + half]
                    for kc in range(8):
                        mml[half] = sch.emit("tensor", lambda e, kc=kc, rt=rt,
                                             half=half, bank=bank: e.matmul(
                            bank[:, 0:512],
                            xg_fm[:, kc * NROW + rt * 128:kc * NROW + (rt + 1) * 128],
                            woT[:, kc * 1024 + half * 512:kc * 1024 + half * 512
                                + 512],
                            start=(kc == 0), stop=(kc == 7)),
                            inc=(1 if kc == 7 else 0),
                            waits=[obank_free[half]] if kc == 0 else [])
                t_h0 = sch.emit("vector", lambda e, jt=jt: e.tensor_tensor(
                    hsc[:, 0:512], pg[4][:, 0:512], cu_bo[jt][:, 0:512], ALU.add),
                    waits=[mml[0], prev_th])
                t_h1 = sch.emit("vector", lambda e, jt=jt: e.tensor_tensor(
                    hsc[:, 512:1024], pg[5][:, 0:512], cu_bo[jt][:, 512:1024],
                    ALU.add), waits=[mml[1], t_h0])
                t_th = sch.emit("scalar", lambda e: e.activation(
                    hsc[:], hsc[:], AF.Tanh), waits=[t_h1])
                prev_th = t_th
                t_y = sch.emit("vector", lambda e, x=xb, y=yb: e.tensor_tensor(
                    y[:], x[:], hsc[:], ALU.add),
                    waits=[t_th, t_x, prev_store])
                xrt_free[p] = t_y
                obank_free[0] = t_y
                obank_free[1] = t_y
                prev_store = sch.dma("sync",
                                     y_h[rt * 128:(rt + 1) * 128, :], yb[:],
                                     waits=[t_y])
            sch.flush()

        t_fin = sch.tok("sync")
        sch.emit("gpsimd", lambda e: None, inc=1, waits=[t_fin])
        sch.flush()
    return nc


def emit_ffn0_fused(nc, sch, ctx, x1_d, x2_d, whs, bhs, id_h,
                    ln1loc, arin, arout, cuin, cuout,
                    sidx_h, gidx_h, cidx_h, cc1, cc2):
    NROW = T_STEPS * JW

    wT = {nm: sbuf(nc, ctx, f"wT_{nm}", [128, 64 * 128], BF16)
          for nm in ("wf", "uf")}
    c_fm = sbuf(nc, ctx, "cfm", [128, 8 * JW], F32)
    c_prev = sbuf(nc, ctx, "cprev", [128, 8 * JW], BF16)
    biasp = sbuf(nc, ctx, "biasp", [128, 4 * 8], F32)
    idf32 = sbuf(nc, ctx, "idf32F", [128, 128], F32)
    idb = sbuf(nc, ctx, "idbF", [128, 128], BF16)
    sidx = sbuf(nc, ctx, "sidx", [128, 32], mybir.dt.uint32)
    gidx = sbuf(nc, ctx, "gidx", [128, 32], mybir.dt.uint32)
    cidx = sbuf(nc, ctx, "cidx", [128, 2], mybir.dt.uint32)
    pg = [psum(nc, ctx, f"pg{i}", [128, 512], F32) for i in range(6)]
    pT = psum(nc, ctx, "pTF", [128, 1024], BF16)
    pT2 = psum(nc, ctx, "pT2F", [128, 1024], BF16)
    ptrs = [PHalf(pT, 0), PHalf(pT2, 0)]

    def idma(out, in_off, in_, out_off, waits=()):
        tok = sch.emit("gpsimd", lambda e, o=out, i=in_, io=in_off, oo=out_off:
                       e.indirect_dma_start(out=o, out_offset=oo, in_=i,
                                            in_offset=io), inc=16, waits=waits)
        sch.emit("gpsimd", lambda e: None, inc=0, waits=[tok])
        return tok

    with ExitStack() as c1:
        st = make_ln_state(nc, c1, "F")
        st["ptr"] = ptrs
        zt = sbuf(nc, c1, "zt", [128, 16384], BF16)
        emit_load_ident(sch, st, id_h)
        t_idf = sch.dma("sync", idf32[:], id_h)
        t_idb = sch.emit("vector", lambda e: e.tensor_copy(idb[:], idf32[:]),
                         waits=[t_idf])
        t_zm = sch.emit("gpsimd", lambda e: e.memset(zt[:], 0.0))
        sch.dma("sync", sidx[:], sidx_h)
        sch.dma("sync", gidx[:], gidx_h)
        sch.dma("sync", cidx[:], cidx_h)
        t_zero = None
        for kk in range(16):
            t_zero = sch.dma("sync",
                             arin[kk * 2048:(kk + 1) * 2048, :].rearrange(
                                 "(a p) d -> p a d", p=128),
                             zt.ap().rearrange("p (a d) -> p a d", a=16),
                             waits=[t_zm])
        t_zero2 = sch.dma("sync", cuin.rearrange("(a p) d -> p a d", p=128),
                          zt.ap().rearrange("p (a d) -> p a d", a=16)
                          [:, 0:16, :], waits=[t_zm])
        sch.flush()

        def post_tile(tt, p, t_dma):
            return idma(arin, None, st["ln_tm"][p][:],
                        IndirectOffsetOnAxis(sidx[:, tt:tt + 1], 0),
                        waits=[t_dma, t_zero, t_zero2])

        rows = [tt * 128 for tt in range(32)]
        emit_ln_to_fm(sch, st, x1_d, 0, 32, None, 0, 0,
                      dram_out=ln1loc, dram_rows=rows, post_tile=post_tile)
        sch.flush()

    t_lg = sch.tok("gpsimd")

    def _coll_ar1(e):
        ins_ = e.collective_compute(
            "AllReduce", ALU.add, replica_groups=[list(range(NCORES))],
            ins=[arin.opt()], outs=[arout.opt()])
        ins_.then_inc(cc1)
        return None
    sch.emit("gpsimd", _coll_ar1, inc=1, waits=[t_lg])
    sch.flush()

    with ExitStack() as cXG:
        xg_fm = sbuf(nc, cXG, "xgfm", [128, 8 * NROW], BF16)
        for nm in ("wi", "ui", "wc", "uc"):
            wT[nm] = sbuf(nc, cXG, f"wT_{nm}", [128, 64 * 128], BF16)

        with ExitStack() as c2:
            xgin = [sbuf(nc, c2, f"xgin{i}", [128, D], BF16) for i in range(2)]
            xg_free = [None, None]
            ptr_free = [None, None]
            first = [True]
            for tt in range(32):
                p = tt % 2
                xb = xgin[p]
                if first[0]:
                    first[0] = False
                    t_l = sch.emit("gpsimd", lambda e, xb=xb, tt=tt: (
                        e.wait_ge(cc1, 1),
                        e.indirect_dma_start(
                            out=xb[:], out_offset=None, in_=arout,
                            in_offset=IndirectOffsetOnAxis(
                                gidx[:, tt:tt + 1], 0)))[1],
                        inc=16, waits=[xg_free[p]])
                    sch.emit("gpsimd", lambda e: None, inc=0, waits=[t_l])
                else:
                    t_l = idma(xb[:], IndirectOffsetOnAxis(gidx[:, tt:tt + 1], 0),
                               arout, None, waits=[xg_free[p]])
                last = None
                for c in range(8):
                    pb = ptrs[c % 2]
                    t_tr = sch.emit("tensor", lambda e, xb=xb, c=c, pb=pb:
                                    e.transpose(
                        pb[0:128, 0:128], xb[:, c * 128:(c + 1) * 128], idb[:]),
                        waits=[t_l, ptr_free[c % 2], t_idb])
                    t_cp = sch.emit("vector", lambda e, c=c, pb=pb, tt=tt:
                                    e.tensor_copy(
                        xg_fm[:, c * NROW + tt * 128:c * NROW + (tt + 1) * 128],
                        pb[0:128, 0:128]), waits=[t_tr])
                    ptr_free[c % 2] = t_cp
                    last = t_cp
                xg_free[p] = last
            sch.flush()

            wstg = [sbuf(nc, c2, f"wstgF{i}", [128, 1024], F32) for i in range(2)]
            wstgb = [sbuf(nc, c2, f"wstgbF{i}", [128, 1024], BF16)
                     for i in range(2)]
            bstg = sbuf(nc, c2, "bstgF", [32, 128], F32)
            stf = {}
            for i, nm in enumerate(("bf", "bi", "bc", "inc")):
                sch.dma("sync", bstg[i * 8:(i + 1) * 8, :],
                        bhs[nm].rearrange("(c p) o -> c (p o)", c=8))
            t_bl = sch.tok("sync")
            t_btr = sch.emit("tensor", lambda e: e.transpose(
                pg[0][:, 0:32], bstg[:], idf32[0:32, 0:32]), waits=[t_bl, t_idf])
            t_bias = sch.emit("vector", lambda e: e.tensor_copy(
                biasp[:], pg[0][:, 0:32]), waits=[t_btr])
            sch.flush()
            for wi_, nm in enumerate(("wf", "uf", "wi", "ui", "wc", "uc")):
                for mc in range(8):
                    sl = (wi_ * 8 + mc) % 2
                    t_l = sch.dma("sync", wstg[sl][:],
                                  whs[nm][mc * 128:(mc + 1) * 128, :],
                                  waits=[stf.get(f"ws{sl}")])
                    t_c = sch.emit("vector", lambda e, sl=sl: e.tensor_copy(
                        wstgb[sl][:], wstg[sl][:]),
                        waits=[t_l, stf.get(f"wb{sl}")])
                    stf[f"ws{sl}"] = t_c
                    last = None
                    for kc in range(8):
                        pb = ptrs[kc % 2]
                        t_tr = sch.emit("tensor", lambda e, sl=sl, kc=kc, pb=pb:
                                        e.transpose(
                            pb[0:128, 0:128], wstgb[sl][:, kc * 128:(kc + 1)
                                                        * 128], idb[:]),
                            waits=[t_c, ptr_free[kc % 2], t_idb])
                        t_cp = sch.emit("vector", lambda e, nm=nm, kc=kc, mc=mc,
                                        pb=pb: e.tensor_copy(
                            wT[nm][:, (kc * 8 + mc) * 128:
                                   (kc * 8 + mc + 1) * 128],
                            pb[0:128, 0:128]), waits=[t_tr])
                        ptr_free[kc % 2] = t_cp
                        last = t_cp
                    stf[f"wb{sl}"] = last
                sch.flush()

        with ExitStack() as c3:
            gsc = [sbuf(nc, c3, f"gscF{i}", [128, JW], F32) for i in range(3)]
            tmp1 = sbuf(nc, c3, "tmp1F", [128, JW], F32)
            tmp2 = sbuf(nc, c3, "tmp2F", [128, JW], F32)
            t_ms = sch.emit("gpsimd", lambda e: e.memset(c_fm[:], 0.0))
            for mc in range(8):
                sch.emit("vector", lambda e, mc=mc: e.tensor_scalar(
                    c_fm[:, mc * JW:(mc + 1) * JW],
                    c_fm[:, mc * JW:(mc + 1) * JW],
                    biasp[:, 24 + mc:24 + mc + 1], None, ALU.add),
                    waits=[t_ms, t_bias])
            sch.flush()

            last_up = sch.tok("vector")
            last_mm_prev = None
            for t_ in range(T_STEPS):
                t_cpv = sch.emit("vector", lambda e: e.tensor_copy(
                    c_prev[:], c_fm[:]), waits=[last_up, last_mm_prev])
                evac_done = {}
                upd_done = {}
                for mc in range(8):
                    for gi_, (wnm, unm, act, bcol) in enumerate((
                            ("wf", "uf", AF.Sigmoid, 0),
                            ("wi", "ui", AF.Sigmoid, 1),
                            ("wc", "uc", AF.Tanh, 2))):
                        bank = pg[(mc % 2) * 3 + gi_]
                        mm = None
                        for kc in range(8):
                            mm = sch.emit("tensor", lambda e, wnm=wnm, kc=kc,
                                          mc=mc, t_=t_, bank=bank: e.matmul(
                                bank[:, 0:JW],
                                wT[wnm][:, (kc * 8 + mc) * 128:
                                        (kc * 8 + mc + 1) * 128],
                                xg_fm[:, kc * NROW + t_ * JW:
                                      kc * NROW + (t_ + 1) * JW],
                                start=(kc == 0), stop=False), inc=0,
                                waits=[evac_done.get(id(bank)), t_cpv]
                                if kc == 0 else [])
                        for kc in range(8):
                            mm = sch.emit("tensor", lambda e, unm=unm, kc=kc,
                                          mc=mc, bank=bank: e.matmul(
                                bank[:, 0:JW],
                                wT[unm][:, (kc * 8 + mc) * 128:
                                        (kc * 8 + mc + 1) * 128],
                                c_prev[:, kc * JW:(kc + 1) * JW],
                                start=False, stop=(kc == 7)),
                                inc=(1 if kc == 7 else 0))
                        evac_done[id(bank)] = sch.emit(
                            "scalar", lambda e, gi_=gi_, mc=mc, act=act,
                            bcol=bcol, bank=bank: e.activation(
                                gsc[gi_][:], bank[:, 0:JW], act,
                                bias=biasp[:, bcol * 8 + mc:bcol * 8 + mc + 1]),
                            waits=[mm, upd_done.get(gi_)])
                    tk1 = sch.emit("vector", lambda e, mc=mc: e.tensor_tensor(
                        tmp1[:], gsc[0][:], c_fm[:, mc * JW:(mc + 1) * JW],
                        ALU.mult), waits=[evac_done[id(pg[(mc % 2) * 3])],
                                          t_cpv])
                    tk2 = sch.emit("vector", lambda e: e.tensor_tensor(
                        tmp2[:], gsc[1][:], gsc[2][:], ALU.mult),
                        waits=[evac_done[id(pg[(mc % 2) * 3 + 1])],
                               evac_done[id(pg[(mc % 2) * 3 + 2])]])
                    t_up = sch.emit("vector", lambda e, mc=mc: e.tensor_tensor(
                        c_fm[:, mc * JW:(mc + 1) * JW], tmp1[:], tmp2[:],
                        ALU.add), waits=[tk1, tk2])
                    for gi_ in range(3):
                        upd_done[gi_] = t_up
                last_up = sch.tok("vector")
                last_mm_prev = sch.tok("tensor")
            t_cfin = sch.emit("vector", lambda e: e.tensor_copy(
                c_prev[:], c_fm[:]), waits=[last_up, last_mm_prev])
            sch.flush()

    with ExitStack() as c4:
        wstg = [sbuf(nc, c4, f"wstgG{i}", [128, 1024], F32) for i in range(2)]
        wstgb = [sbuf(nc, c4, f"wstgbG{i}", [128, 1024], BF16) for i in range(2)]
        woT, uoT = wT["wf"], wT["uf"]
        ptr_free = [None, None]
        stf = {}
        for wi_, (nm, dst) in enumerate((("wo", woT), ("uo", uoT))):
            for mc in range(8):
                sl = (wi_ * 8 + mc) % 2
                t_l = sch.dma("sync", wstg[sl][:],
                              whs[nm][mc * 128:(mc + 1) * 128, :],
                              waits=[stf.get(f"ws{sl}")])
                t_c = sch.emit("vector", lambda e, sl=sl: e.tensor_copy(
                    wstgb[sl][:], wstg[sl][:]), waits=[t_l, stf.get(f"wb{sl}")])
                stf[f"ws{sl}"] = t_c
                last = None
                for kc in range(8):
                    pb = [PHalf(pT, 0), PHalf(pT2, 0)][kc % 2]
                    t_tr = sch.emit("tensor", lambda e, sl=sl, kc=kc, pb=pb:
                                    e.transpose(
                        pb[0:128, 0:128], wstgb[sl][:, kc * 128:(kc + 1) * 128],
                        idb[:]), waits=[t_c, ptr_free[kc % 2]])
                    t_cp = sch.emit("vector", lambda e, dst=dst, kc=kc, mc=mc,
                                    pb=pb: e.tensor_copy(
                        dst[:, (kc * 8 + mc) * 128:(kc * 8 + mc + 1) * 128],
                        pb[0:128, 0:128]), waits=[t_tr])
                    ptr_free[kc % 2] = t_cp
                    last = t_cp
                stf[f"wb{sl}"] = last
        sch.flush()

        cu_bf = [sbuf(nc, c4, f"cubf{i}", [128, D], BF16) for i in range(2)]
        cu_evac = [None, None]
        for jt in range(2):
            mml = [None, None]
            for half in range(2):
                bank = pg[2 + half]
                for kc in range(8):
                    mml[half] = sch.emit("tensor", lambda e, kc=kc, jt=jt,
                                         half=half, bank=bank: e.matmul(
                        bank[:, 0:512],
                        c_prev[:, kc * JW + jt * 128:kc * JW + (jt + 1) * 128],
                        uoT[:, kc * 1024 + half * 512:kc * 1024 + half * 512
                            + 512],
                        start=(kc == 0), stop=(kc == 7)),
                        inc=(1 if kc == 7 else 0),
                        waits=[cu_evac[half]] if kc == 0 else [])
            t_c1 = sch.emit("vector", lambda e, jt=jt: e.tensor_copy(
                cu_bf[jt][:, 0:512], pg[2][:, 0:512]), waits=[mml[0]])
            t_c2 = sch.emit("vector", lambda e, jt=jt: e.tensor_copy(
                cu_bf[jt][:, 512:1024], pg[3][:, 0:512]), waits=[mml[1]])
            cu_evac[0], cu_evac[1] = t_c1, t_c2
            idma(cuin, None, cu_bf[jt][:],
                 IndirectOffsetOnAxis(cidx[:, jt:jt + 1], 0),
                 waits=[t_c1, t_c2])
        t_lg2 = sch.tok("gpsimd")

        def _coll_ar2(e):
            ins_ = e.collective_compute(
                "AllReduce", ALU.add, replica_groups=[list(range(NCORES))],
                ins=[cuin.opt()], outs=[cuout.opt()])
            ins_.then_inc(cc2)
            return None
        sch.emit("gpsimd", _coll_ar2, inc=1, waits=[t_lg2])
        sch.flush()

        own_fm = sbuf(nc, c4, "ownfm", [128, 8 * NROW], BF16)
        xgin = [sbuf(nc, c4, f"oxin{i}", [128, D], BF16) for i in range(2)]
        xg_free = [None, None]
        for tt in range(32):
            p = tt % 2
            xb = xgin[p]
            t_l = sch.dma("sync", xb[:], ln1loc[tt * 128:(tt + 1) * 128, :],
                          waits=[xg_free[p]])
            last = None
            for c in range(8):
                pb = [PHalf(pT, 0), PHalf(pT2, 0)][c % 2]
                t_tr = sch.emit("tensor", lambda e, xb=xb, c=c, pb=pb:
                                e.transpose(
                    pb[0:128, 0:128], xb[:, c * 128:(c + 1) * 128], idb[:]),
                    waits=[t_l, ptr_free[c % 2]])
                t_cp = sch.emit("vector", lambda e, c=c, pb=pb, tt=tt:
                                e.tensor_copy(
                    own_fm[:, c * NROW + tt * 128:c * NROW + (tt + 1) * 128],
                    pb[0:128, 0:128]), waits=[t_tr])
                ptr_free[c % 2] = t_cp
                last = t_cp
            xg_free[p] = last
        sch.flush()

        borow = sbuf(nc, c4, "borowG", [1, D], F32)
        onesf = sbuf(nc, c4, "onesfG", [1, 128], F32)
        bo_bc = sbuf(nc, c4, "bobcG", [128, D], F32)
        cu32 = sbuf(nc, c4, "cu32", [128, D], F32)
        cubo = sbuf(nc, c4, "cubo", [128, D], F32)
        hsc = sbuf(nc, c4, "hscG", [128, D], F32)
        xrt = [sbuf(nc, c4, f"xrtG{i}", [128, D], F32) for i in range(2)]
        yt = sbuf(nc, c4, "ytG", [128, D], F32)
        t_bo = sch.dma("sync", borow[:], bhs["bo"].rearrange("p o -> o p"))
        t_on = sch.emit("gpsimd", lambda e: e.memset(onesf[:], 1.0))
        tb1 = sch.emit("tensor", lambda e: e.matmul(
            pg[0][:, 0:512], onesf[:, 0:128], borow[:, 0:512], start=True,
            stop=True), waits=[t_bo, t_on])
        tb2 = sch.emit("tensor", lambda e: e.matmul(
            pg[1][:, 0:512], onesf[:, 0:128], borow[:, 512:1024], start=True,
            stop=True), waits=[t_bo, t_on])
        sch.emit("vector", lambda e: e.tensor_copy(bo_bc[:, 0:512],
                                                   pg[0][:, 0:512]), waits=[tb1])
        t_bobc = sch.emit("vector", lambda e: e.tensor_copy(
            bo_bc[:, 512:1024], pg[1][:, 0:512]), waits=[tb2])
        sch.flush()

        obank_free = [None, None]
        prev_th = None
        prev_store = None
        xrt_free = [None, None]
        cu_used = None
        first_cu = [True]
        for st_ in range(16):
            if first_cu[0]:
                first_cu[0] = False
                t_cl = sch.emit("sync", lambda e, st_=st_: (
                    e.wait_ge(cc2, 1),
                    e.dma_start(out=cu_bf[0][:],
                                in_=cuout[st_ * 128:(st_ + 1) * 128, :]))[1],
                    inc=16, waits=[])
                sch.emit("sync", lambda e: None, inc=0, waits=[t_cl])
            else:
                t_cl = sch.dma("sync", cu_bf[0][:],
                               cuout[st_ * 128:(st_ + 1) * 128, :],
                               waits=[cu_used])
            t_c32 = sch.emit("vector", lambda e: e.tensor_copy(
                cu32[:], cu_bf[0][:]), waits=[t_cl, prev_th])
            t_cub = sch.emit("vector", lambda e: e.tensor_tensor(
                cubo[:], cu32[:], bo_bc[:], ALU.add), waits=[t_c32, t_bobc])
            cu_used = t_c32
            for b_l in range(2):
                rt = b_l * 16 + st_
                p = rt % 2
                xb = xrt[p]
                t_x = sch.dma("sync", xb[:],
                              x1_d[rt * 128:(rt + 1) * 128, :],
                              waits=[xrt_free[p]])
                mml = [None, None]
                for half in range(2):
                    bank = pg[4 + half]
                    for kc in range(8):
                        mml[half] = sch.emit("tensor", lambda e, kc=kc, rt=rt,
                                             half=half, bank=bank: e.matmul(
                            bank[:, 0:512],
                            own_fm[:, kc * NROW + rt * 128:
                                   kc * NROW + (rt + 1) * 128],
                            woT[:, kc * 1024 + half * 512:kc * 1024
                                + half * 512 + 512],
                            start=(kc == 0), stop=(kc == 7)),
                            inc=(1 if kc == 7 else 0),
                            waits=[obank_free[half]] if kc == 0 else [])
                t_h0 = sch.emit("vector", lambda e: e.tensor_tensor(
                    hsc[:, 0:512], pg[4][:, 0:512], cubo[:, 0:512], ALU.add),
                    waits=[mml[0], prev_th, t_cub])
                t_h1 = sch.emit("vector", lambda e: e.tensor_tensor(
                    hsc[:, 512:1024], pg[5][:, 0:512], cubo[:, 512:1024],
                    ALU.add), waits=[mml[1], t_h0])
                t_th = sch.emit("scalar", lambda e: e.activation(
                    hsc[:], hsc[:], AF.Tanh), waits=[t_h1])
                prev_th = t_th
                t_y = sch.emit("vector", lambda e, x=xb: e.tensor_tensor(
                    yt[:], x[:], hsc[:], ALU.add),
                    waits=[t_th, t_x, prev_store])
                xrt_free[p] = t_y
                obank_free[0] = t_y
                obank_free[1] = t_y
                prev_store = sch.dma("sync",
                                     x2_d[rt * 128:(rt + 1) * 128, :], yt[:],
                                     waits=[t_y])
        sch.flush()


def build_fused(phases="ABCD"):
    nc = bass.Bass("TRN2", target_bir_lowering=False)
    NROW = SEQ_PER_CORE * S
    g = {}
    g["x"] = nc.dram_tensor("x", [NROW, D], F32, kind="ExternalInput").ap()
    for nm, shp in (("wq0", [D, D]), ("wkv0", [D, 2 * D]), ("wo0", [D, D]),
                    ("wq1", [D, D]), ("wk1", [D, D]), ("wv1", [D, D]),
                    ("wo1", [D, D]), ("wf1", [D, H]),
                    ("f_wf", [D, D]), ("f_uf", [D, D]), ("f_wi", [D, D]),
                    ("f_ui", [D, D]), ("f_wc", [D, D]), ("f_uc", [D, D]),
                    ("f_wo", [D, D]), ("f_uo", [D, D]),
                    ("wgate", [I_FF, D]), ("wup", [I_FF, D]),
                    ("wdown", [D, I_FF])):
        g[nm] = nc.dram_tensor(nm, shp, F32, kind="ExternalInput").ap()
    for nm in ("bq0", "bkv0", "f_bf", "f_bi", "f_bc", "f_bo", "f_inc"):
        shp = [2 * D, 1] if nm == "bkv0" else [D, 1]
        g[nm] = nc.dram_tensor(nm, shp, F32, kind="ExternalInput").ap()
    g["bo0"] = nc.dram_tensor("bo0", [1, D], F32, kind="ExternalInput").ap()
    g["csfull"] = nc.dram_tensor("csfull", [128, S], F32,
                                 kind="ExternalInput").ap()
    g["scfull"] = nc.dram_tensor("scfull", [128, S], F32,
                                 kind="ExternalInput").ap()
    g["ident"] = nc.dram_tensor("ident", [128, 128], F32,
                                kind="ExternalInput").ap()
    g["diagmask"] = nc.dram_tensor("diagmask", [128, 4 * 512], BF16,
                                   kind="ExternalInput").ap()
    g["diagmaskA"] = nc.dram_tensor("diagmaskA", [128, 4 * 512], F32,
                                    kind="ExternalInput").ap()
    y_h = nc.dram_tensor("y", [NROW, D], F32, kind="ExternalOutput").ap()
    x1_d = nc.dram_tensor("x1_d", [NROW, D], F32).ap()
    x2_d = nc.dram_tensor("x2_d", [NROW, D], F32).ap()
    x3_d = nc.dram_tensor("x3_d", [NROW, D], F32).ap()
    ln1loc = nc.dram_tensor("ln1loc", [NROW, D], BF16).ap()
    arin = nc.dram_tensor("arin", [B * S, D], BF16).ap()
    arout = nc.dram_tensor("arout", [B * S, D], BF16).ap()
    cuin = nc.dram_tensor("cuin", [S, D], BF16).ap()
    cuout = nc.dram_tensor("cuout", [S, D], BF16).ap()
    U32 = mybir.dt.uint32
    g["sidx"] = nc.dram_tensor("sidx", [128, 32], U32, kind="ExternalInput").ap()
    g["gidx"] = nc.dram_tensor("gidx", [128, 32], U32, kind="ExternalInput").ap()
    g["cidx"] = nc.dram_tensor("cidx", [128, 2], U32, kind="ExternalInput").ap()
    wgq_d = nc.dram_tensor("wgq_d", [D, I_FF], BF16).ap()
    wuq_d = nc.dram_tensor("wuq_d", [D, I_FF], BF16).ap()

    with ExitStack() as ctx0:
        sch = Sch(nc, ctx0)
        cc1 = ctx0.enter_context(nc.semaphore("cc1"))
        cc2 = ctx0.enter_context(nc.semaphore("cc2"))
        def _clr(e):
            e.sem_clear(cc1)
            e.sem_clear(cc2)
            return None
        sch.emit("gpsimd", _clr, inc=1)
        sch.flush()
        if "A" in phases:
            with nc.named_scope("phA"), ExitStack() as cA:
                emit_attn0(nc, sch, cA, x_h=g["x"], y_h=x1_d, wq_h=g["wq0"],
                           wkv_h=g["wkv0"], wo_h=g["wo0"], bq_h=g["bq0"],
                           bkv_h=g["bkv0"], bo_h=g["bo0"], cs_h=g["csfull"],
                           sc_h=g["scfull"], id_h=g["ident"], dm_h=g["diagmask"])
                sch.flush()
        whs = {k[2:]: g["f_" + k[2:]] for k in
               ("f_wf", "f_uf", "f_wi", "f_ui", "f_wc", "f_uc", "f_wo", "f_uo")}
        bhs = {k: g["f_" + k] for k in ("bf", "bi", "bc", "bo", "inc")}
        if "B" in phases:
            with nc.named_scope("phB"), ExitStack() as cB:
                emit_ffn0_fused(nc, sch, cB, x1_d, x2_d, whs, bhs, g["ident"],
                                ln1loc, arin, arout, cuin, cuout,
                                g["sidx"], g["gidx"], g["cidx"], cc1, cc2)
                sch.flush()
        if "C" in phases:
            with nc.named_scope("phC"), ExitStack() as cC:
                emit_attn1(nc, sch, cC, x_h=x2_d, y_h=x3_d, wq_h=g["wq1"],
                           wk_h=g["wk1"], wv_h=g["wv1"], wo_h=g["wo1"],
                           wf_h=g["wf1"], id_h=g["ident"], dm_h=g["diagmaskA"])
                sch.flush()
        if "D" in phases:
            with nc.named_scope("phD"), ExitStack() as cD:
                emit_ffn1(nc, sch, cD, x_h=x3_d, y_h=y_h, wg_h=g["wgate"],
                          wu_h=g["wup"], wd_h=g["wdown"], id_h=g["ident"],
                          wgq_d=wgq_d, wuq_d=wuq_d)
                sch.flush()
        if "D" not in phases:
            srcd = {"A": x1_d, "B": x2_d, "C": x3_d}[phases[-1]]
            with ExitStack() as cZ:
                tbuf = sbuf(nc, cZ, "cpz", [128, D], F32)
                for tt in range(NROW // 128):
                    tl = sch.dma("sync", tbuf[:], srcd[tt * 128:(tt + 1) * 128, :])
                    sch.dma("sync", y_h[tt * 128:(tt + 1) * 128, :], tbuf[:],
                            waits=[tl])
            sch.flush()
        t_fin = sch.tok("sync")
        sch.emit("gpsimd", lambda e: None, inc=1, waits=[t_fin])
        sch.flush()
    return nc


def kernel_fused(phases="ABCD", _profile=False, **inp):
    nc = build_fused(phases)
    cs, sc = _rope_cs_sc()
    x = np.asarray(inp["L0_attn_x"], np.float32)

    def f32(k):
        return np.ascontiguousarray(np.asarray(inp[k], np.float32))
    base = dict(
        wq0=f32("L0_attn_Wq"), wkv0=f32("L0_attn_Wkv"), wo0=f32("L0_attn_Wo"),
        bq0=f32("L0_attn_bq").reshape(D, 1),
        bkv0=f32("L0_attn_bkv").reshape(2 * D, 1),
        bo0=f32("L0_attn_bo").reshape(1, D),
        f_wf=f32("L0_ffn_wf"), f_uf=f32("L0_ffn_uf"), f_wi=f32("L0_ffn_wi"),
        f_ui=f32("L0_ffn_ui"), f_wc=f32("L0_ffn_wc"), f_uc=f32("L0_ffn_uc"),
        f_wo=f32("L0_ffn_wo"), f_uo=f32("L0_ffn_uo"),
        f_bf=f32("L0_ffn_bf").reshape(D, 1), f_bi=f32("L0_ffn_bi").reshape(D, 1),
        f_bc=f32("L0_ffn_bc").reshape(D, 1), f_bo=f32("L0_ffn_bo").reshape(D, 1),
        f_inc=f32("L0_ffn_in_c").reshape(D, 1),
        wq1=f32("L1_attn_Wq"), wk1=f32("L1_attn_Wk"), wv1=f32("L1_attn_Wv"),
        wo1=f32("L1_attn_Wo"), wf1=f32("L1_attn_Wf"),
        wgate=f32("L1_ffn_w_gate"), wup=f32("L1_ffn_w_up"),
        wdown=f32("L1_ffn_w_down"),
        csfull=cs, scfull=sc, ident=_ident(), diagmask=_diag_masks(),
        diagmaskA=_diag_masks_add(),
    )
    in_maps = []
    for c in range(NCORES):
        m = dict(base)
        m["x"] = np.ascontiguousarray(
            x[c * SEQ_PER_CORE:(c + 1) * SEQ_PER_CORE].reshape(
                SEQ_PER_CORE * S, D))
        k = np.arange(128, dtype=np.uint32)
        tt = np.arange(32, dtype=np.uint32)
        m["sidx"] = np.ascontiguousarray(
            ((2 * c + tt // 16) * 2048 + (tt % 16) * 128)[None, :] + k[:, None]
        ).astype(np.uint32)
        r = tt[None, :] * 128 + k[:, None]
        m["gidx"] = np.ascontiguousarray(
            (r // 256) * 2048 + c * 256 + (r % 256)).astype(np.uint32)
        jt = np.arange(2, dtype=np.uint32)
        m["cidx"] = np.ascontiguousarray(
            c * 256 + jt[None, :] * 128 + k[:, None]).astype(np.uint32)
        in_maps.append(m)
    res = run_bass_kernel_spmd(nc, in_maps, core_ids=list(range(NCORES)),
                               trace=_profile)
    if _profile:
        print("exec_time_ns:", res.exec_time_ns,
              "mean:", res.mean_exec_time_ns)
        if res.per_core_scope_times:
            for k, v in res.per_core_scope_times.items():
                vals = sorted(v.values())
                print(f"  scope {k}: min={min(vals)/1e3:.0f}us "
                      f"max={max(vals)/1e3:.0f}us")
    return np.concatenate([r["y"].reshape(SEQ_PER_CORE, S, D)
                           for r in res.results], 0)


def run_ffn0(x1, wf, uf, bf, wi, ui, bi, wc, uc, bc, wo, uo, bo, in_c):
    nc = build_ffn0()
    in_maps = []
    args = dict(
        wf=np.ascontiguousarray(np.asarray(wf, np.float32)),
        uf=np.ascontiguousarray(np.asarray(uf, np.float32)),
        wi=np.ascontiguousarray(np.asarray(wi, np.float32)),
        ui=np.ascontiguousarray(np.asarray(ui, np.float32)),
        wc=np.ascontiguousarray(np.asarray(wc, np.float32)),
        uc=np.ascontiguousarray(np.asarray(uc, np.float32)),
        wo=np.ascontiguousarray(np.asarray(wo, np.float32)),
        uo=np.ascontiguousarray(np.asarray(uo, np.float32)),
        bf=np.asarray(bf, np.float32).reshape(D, 1),
        bi=np.asarray(bi, np.float32).reshape(D, 1),
        bc=np.asarray(bc, np.float32).reshape(D, 1),
        bo=np.asarray(bo, np.float32).reshape(D, 1),
        inc=np.asarray(in_c, np.float32).reshape(D, 1),
        ident=_ident(),
    )
    for c in range(NCORES):
        xs = np.ascontiguousarray(x1[:, c * JW:(c + 1) * JW, :].reshape(
            T_STEPS * JW, D))
        m = dict(args)
        m["x"] = xs
        in_maps.append(m)
    res = run_bass_kernel_spmd(nc, in_maps, core_ids=list(range(NCORES)))
    out = np.empty((T_STEPS, S, D), np.float32)
    for c in range(NCORES):
        out[:, c * JW:(c + 1) * JW, :] = res.results[c]["y"].reshape(
            T_STEPS, JW, D)
    return out


I_FF = 2816
ISL = [(0, 512), (512, 512), (1024, 512), (1536, 512), (2048, 512), (2560, 256)]


def emit_quant(sch, src, scr2, out, bm, scale, rs, F, waits, out_guard=None):
    B16 = F // 16
    t_bm = sch.emit("vector", lambda e: e.tensor_reduce(
        bm[:, 0:B16], src.rearrange("p (g x) -> p g x", x=16), AX.X, ALU.max,
        apply_absolute_value=True), waits=waits)
    t_sc = sch.emit("vector", lambda e: e.tensor_scalar(
        scale[:, 0:B16], bm[:, 0:B16], 1.0 / 6.0, 1e-12, ALU.mult, ALU.max),
        waits=[t_bm])
    t_rs = sch.emit("vector", lambda e: e.reciprocal(
        rs[:, 0:B16], scale[:, 0:B16]), waits=[t_sc])
    rsb = rs[:, 0:B16].unsqueeze(2).to_broadcast((128, B16, 16))
    scb = scale[:, 0:B16].unsqueeze(2).to_broadcast((128, B16, 16))
    s3 = src.rearrange("p (g x) -> p g x", x=16)
    u3 = scr2.rearrange("p (g x) -> p g x", x=16)
    t_u = sch.emit("vector", lambda e: e.tensor_tensor(u3, s3, rsb, ALU.mult),
                   waits=[t_rs] + list(waits))
    t_v = sch.emit("vector", lambda e: e.tensor_scalar(
        src.bitcast(I32), scr2.bitcast(I32), 0x7F800000, None, ALU.bitwise_and),
        waits=[t_u])
    t_k = sch.emit("vector", lambda e: e.tensor_scalar(
        src, src, 1.0, 6291456.0, ALU.max, ALU.mult), waits=[t_v])
    t_w = sch.emit("vector", lambda e: e.tensor_tensor(
        scr2, scr2, src, ALU.add), waits=[t_k])
    t_q = sch.emit("vector", lambda e: e.tensor_tensor(
        scr2, scr2, src, ALU.subtract), waits=[t_w])
    o3 = out.rearrange("p (g x) -> p g x", x=16)
    t_o = sch.emit("vector", lambda e: e.tensor_tensor(o3, u3, scb, ALU.mult),
                   waits=[t_q, out_guard])
    return t_o


def emit_ffn1(nc, sch, ctx, x_h, y_h, wg_h, wu_h, wd_h, id_h, wgq_d, wuq_d):
    NROW = SEQ_PER_CORE * S
    if True:
        wdq = sbuf(nc, ctx, "wdq", [128, 22 * 1024], BF16)
        scr1 = sbuf(nc, ctx, "scr1", [128, I_FF], F32)
        scr2 = sbuf(nc, ctx, "scr2", [128, I_FF], F32)
        bm = sbuf(nc, ctx, "bm", [128, 176], F32)
        scale = sbuf(nc, ctx, "scale", [128, 176], F32)
        rs = sbuf(nc, ctx, "rs", [128, 176], F32)
        st = make_ln_state(nc, ctx, "D")
        ln32 = [sbuf(nc, ctx, f"ln32{i}", [128, D], F32) for i in range(2)]
        pA = psum(nc, ctx, "pA", [128, 512], F32)
        pB = psum(nc, ctx, "pB", [128, 512], F32)
        pC = psum(nc, ctx, "pC", [128, 512], F32)
        pD = psum(nc, ctx, "pD", [128, 512], F32)
        pE = psum(nc, ctx, "pE", [128, 512], F32)
        pF = psum(nc, ctx, "pF", [128, 512], F32)
        pT = psum(nc, ctx, "pT", [128, 1024], BF16)
        pT2 = psum(nc, ctx, "pT2", [128, 1024], BF16)
        st["ptr"] = [PHalf(pT, 0), PHalf(pT2, 0)]
        emit_load_ident(sch, st, id_h)
        sch.flush()

        with ExitStack() as c1:
            wq_tm = sbuf(nc, c1, "wqtm", [128, 22 * 1024], BF16)
            strip = sbuf(nc, c1, "strip", [128, I_FF], BF16)
            for w_hx, wq_dx in ((wg_h, wgq_d), (wu_h, wuq_d)):
                q_toks = []
                for it in range(22):
                    t_l = sch.dma("sync", scr1[:, 0:1024],
                                  w_hx[it * 128:(it + 1) * 128, :],
                                  waits=[st.get("q_prev")])
                    t_q = emit_quant(sch, scr1[:, 0:1024], scr2[:, 0:1024],
                                     wq_tm[:, it * 1024:(it + 1) * 1024],
                                     bm, scale, rs, 1024, [t_l])
                    st["q_prev"] = t_q
                    q_toks.append(t_q)
                sch.flush()
                for kc in range(8):
                    cp_toks = []
                    for it in range(22):
                        pb = st["ptr"][it % 2]
                        t_tr = sch.emit("tensor", lambda e, it=it, kc=kc, pb=pb:
                                        e.transpose(
                            pb[0:128, 0:128],
                            wq_tm[:, it * 1024 + kc * 128:it * 1024 + (kc + 1)
                                  * 128], st["id_bf"][:]),
                            waits=[st["ptr_free"][it % 2]])
                        t_cp = sch.emit("vector", lambda e, it=it, pb=pb:
                                        e.tensor_copy(
                            strip[:, it * 128:(it + 1) * 128], pb[0:128, 0:128]),
                            waits=[t_tr, st.get("strip_spill")])
                        st["ptr_free"][it % 2] = t_cp
                        cp_toks.append(t_cp)
                    st["spill_done"] = sch.dma(
                        "sync", wq_dx[kc * 128:(kc + 1) * 128, :],
                        strip[:], waits=[cp_toks[-1]])
                    st["strip_spill"] = st["spill_done"]
                    sch.flush()
            for et in range(8):
                t_l = sch.dma("sync", scr1[:], wd_h[et * 128:(et + 1) * 128, :],
                              waits=[st.get("q_prev")])
                t_q = emit_quant(sch, scr1[:], scr2[:],
                                 strip[:], bm, scale, rs, I_FF, [t_l],
                                 out_guard=st.get("strip_guard"))
                st["q_prev"] = t_q
                for ic in range(22):
                    pb = st["ptr"][ic % 2]
                    t_tr = sch.emit("tensor", lambda e, ic=ic, pb=pb: e.transpose(
                        pb[0:128, 0:128], strip[:, ic * 128:(ic + 1) * 128],
                        st["id_bf"][:]), waits=[t_q, st["ptr_free"][ic % 2]])
                    t_cp = sch.emit("vector", lambda e, ic=ic, et=et, pb=pb:
                                    e.tensor_copy(
                        wdq[:, ic * 1024 + et * 128:ic * 1024 + (et + 1) * 128],
                        pb[0:128, 0:128]), waits=[t_tr])
                    st["ptr_free"][ic % 2] = t_cp
                st["q_prev"] = sch.tok("vector")
                st["strip_guard"] = sch.tok("tensor")
                sch.flush()

        with ExitStack() as c2:
            xq_fm = sbuf(nc, c2, "xqfm", [128, 8 * 512], BF16)
            strip2 = sbuf(nc, c2, "strip2", [128, I_FF], BF16)
            hq_fm = sbuf(nc, c2, "hqfm", [128, 22 * 512], BF16)
            wg_sl = [sbuf(nc, c2, f"wgsl{i}", [128, 8 * 512], BF16) for i in range(2)]
            wu_sl = [sbuf(nc, c2, f"wusl{i}", [128, 8 * 512], BF16) for i in range(2)]
            silu_s = sbuf(nc, c2, "silus", [128, 512], F32)
            xq_tm = sbuf(nc, c2, "xqtm", [128, D], BF16)
            xrt = [sbuf(nc, c2, f"xrtD{i}", [128, D], F32) for i in range(2)]
            yt = [sbuf(nc, c2, f"ytD{i}", [128, D], F32) for i in range(2)]
            wsl_free = [None, None]
            for tg in range(8):
                row0 = tg * 512
                sums = st["stat"]
                for tt4 in range(4):
                    p = tt4 % 2
                    co = 8 * p
                    xb = st["xin"][p]
                    sq_scr = st["sq_scr"][p]
                    lnb = ln32[p]
                    t_x = sch.dma("sync", xb[:],
                                  x_h[row0 + tt4 * 128:row0 + (tt4 + 1) * 128, :],
                                  waits=[st["xin_free"][p]])
                    t_sum = sch.emit("vector", lambda e, x=xb, co=co:
                                     e.tensor_reduce(
                        sums[:, co:co + 1], x[:], AX.X, ALU.add),
                        waits=[t_x, st["stat_free"][p]])
                    t_sq0 = sch.emit("scalar", lambda e, x=xb, sq=sq_scr:
                                     e.activation(sq[:], x[:], AF.Square),
                                     waits=[t_x, st["sqw_free"][p]])
                    t_sq = sch.emit("vector", lambda e, sq=sq_scr, co=co:
                                    e.tensor_reduce(
                        sums[:, co + 4:co + 5], sq[:], AX.X, ALU.add),
                        waits=[t_sq0, st["stat_free"][p]])
                    st["sqw_free"][p] = t_sq
                    t_mean = sch.emit("vector", lambda e, co=co: e.tensor_scalar(
                        sums[:, co + 1:co + 2], sums[:, co:co + 1], 1.0 / D, None,
                        ALU.mult), waits=[t_sum])
                    t_msq = sch.emit("vector", lambda e, co=co: e.tensor_scalar(
                        sums[:, co + 2:co + 3], sums[:, co + 4:co + 5], 1.0 / D,
                        None, ALU.mult), waits=[t_sq])
                    t_mm = sch.emit("vector", lambda e, co=co: e.tensor_tensor(
                        sums[:, co + 3:co + 4], sums[:, co + 1:co + 2],
                        sums[:, co + 1:co + 2], ALU.mult), waits=[t_mean])
                    t_var = sch.emit("vector", lambda e, co=co: e.tensor_tensor(
                        sums[:, co + 5:co + 6], sums[:, co + 2:co + 3],
                        sums[:, co + 3:co + 4], ALU.subtract),
                        waits=[t_msq, t_mm])
                    t_sd = sch.emit("scalar", lambda e, co=co: e.activation(
                        sums[:, co + 6:co + 7], sums[:, co + 5:co + 6], AF.Sqrt,
                        bias=st["epsb"][:, 0:1]), waits=[t_var, st["t_eps"]])
                    t_rstd = sch.emit("vector", lambda e, co=co: e.reciprocal(
                        sums[:, co + 7:co + 8], sums[:, co + 6:co + 7]),
                        waits=[t_sd])
                    t_ap = sch.emit("vector", lambda e, x=xb, l=lnb, co=co:
                                    e.tensor_scalar(
                        l[:], x[:], sums[:, co + 1:co + 2],
                        sums[:, co + 7:co + 8], ALU.subtract, ALU.mult),
                        waits=[t_rstd, st["ln_free"][p]])
                    st["xin_free"][p] = t_ap
                    st["stat_free"][p] = t_ap
                    t_q = emit_quant(sch, lnb[:], scr2[:, 0:1024], xq_tm[:],
                                     bm, scale, rs, 1024, [t_ap],
                                     out_guard=st.get("q_xqtm"))
                    st["ln_free"][p] = t_q
                    last_cp = None
                    for c in range(8):
                        pb = st["ptr"][c % 2]
                        t_tr = sch.emit("tensor", lambda e, c=c, pb=pb: e.transpose(
                            pb[0:128, 0:128], xq_tm[:, c * 128:(c + 1) * 128],
                            st["id_bf"][:]),
                            waits=[t_q, st["ptr_free"][c % 2]])
                        t_cp = sch.emit("vector", lambda e, c=c, pb=pb, tt4=tt4:
                                        e.tensor_copy(
                            xq_fm[:, c * 512 + tt4 * 128:c * 512 + (tt4 + 1)
                                  * 128], pb[0:128, 0:128]), waits=[t_tr])
                        st["ptr_free"][c % 2] = t_cp
                        last_cp = t_cp
                    st["q_xqtm"] = last_cp
                sch.flush()

                for tt4 in range(4):
                    db = tt4 % 2
                    for isl, (i0, w) in enumerate(ISL):
                        st[f"wg_ld{isl % 2}"] = sch.dma(
                            "sync",
                            wg_sl[isl % 2].ap().rearrange(
                                "p (c m) -> p c m", c=8)[:, :, 0:w],
                            wgq_d[:, i0:i0 + w].rearrange(
                                "(c p) m -> p c m", p=128),
                            waits=[st.get(f"wg_used{isl % 2}"),
                                   st.get("spill_done")])
                        st[f"wu_ld{isl % 2}"] = sch.dma(
                            "sync",
                            wu_sl[isl % 2].ap().rearrange(
                                "p (c m) -> p c m", c=8)[:, :, 0:w],
                            wuq_d[:, i0:i0 + w].rearrange(
                                "(c p) m -> p c m", p=128),
                            waits=[st.get(f"wu_used{isl % 2}")])
                        mg = mu = None
                        for kc in range(8):
                            mg = sch.emit("tensor", lambda e, kc=kc, tt4=tt4,
                                          isl=isl, w=w: e.matmul(
                                pA[:, 0:w],
                                xq_fm[:, kc * 512 + tt4 * 128:kc * 512
                                      + (tt4 + 1) * 128],
                                wg_sl[isl % 2][:, kc * 512:kc * 512 + w],
                                start=(kc == 0), stop=(kc == 7)),
                                inc=(1 if kc == 7 else 0),
                                waits=[st.get(f"wg_ld{isl % 2}"),
                                       st.get("pA_free")] if kc == 0 else [])
                            mu = sch.emit("tensor", lambda e, kc=kc, tt4=tt4,
                                          isl=isl, w=w: e.matmul(
                                pB[:, 0:w],
                                xq_fm[:, kc * 512 + tt4 * 128:kc * 512
                                      + (tt4 + 1) * 128],
                                wu_sl[isl % 2][:, kc * 512:kc * 512 + w],
                                start=(kc == 0), stop=(kc == 7)),
                                inc=(1 if kc == 7 else 0),
                                waits=[st.get(f"wu_ld{isl % 2}"),
                                       st.get("pB_free")] if kc == 0 else [])
                        t_sil = sch.emit("scalar", lambda e, w=w: e.activation(
                            silu_s[:, 0:w], pA[:, 0:w], AF.Silu),
                            waits=[mg, st.get("pB_free")])
                        st["pA_free"] = t_sil
                        t_h = sch.emit("vector", lambda e, i0=i0, w=w:
                                       e.tensor_tensor(
                            scr1[:, i0:i0 + w], silu_s[:, 0:w], pB[:, 0:w],
                            ALU.mult), waits=[t_sil, mu, st.get("h_guard")])
                        st["pB_free"] = t_h
                        st[f"wg_used{isl % 2}"] = mg
                        st[f"wu_used{isl % 2}"] = mu
                    t_hq = emit_quant(sch, scr1[:], scr2[:], strip2[:],
                                      bm, scale, rs, I_FF, [sch.tok("vector")],
                                      out_guard=st.get("hq_guard"))
                    st["h_guard"] = t_hq
                    for ic in range(22):
                        pb = st["ptr"][ic % 2]
                        t_tr = sch.emit("tensor", lambda e, ic=ic, pb=pb:
                                        e.transpose(
                            pb[0:128, 0:128], strip2[:, ic * 128:(ic + 1) * 128],
                            st["id_bf"][:]), waits=[t_hq, st["ptr_free"][ic % 2]])
                        t_cp = sch.emit("vector", lambda e, ic=ic, tt4=tt4, pb=pb:
                                        e.tensor_copy(
                            hq_fm[:, ic * 512 + tt4 * 128:ic * 512 + (tt4 + 1)
                                  * 128], pb[0:128, 0:128]), waits=[t_tr])
                        st["ptr_free"][ic % 2] = t_cp
                        if ic == 21:
                            st["hq_guard"] = sch.tok("tensor")
                sch.flush()

                obank_free = [None, None]
                for tt4 in range(4):
                    p = tt4 % 2
                    xb = xrt[p]
                    yb = yt[p]
                    t_x = sch.dma("sync", xb[:],
                                  x_h[row0 + tt4 * 128:row0 + (tt4 + 1) * 128, :])
                    mml = [None, None]
                    for half in range(2):
                        bank = [pC, pD][half]
                        for ic in range(22):
                            mml[half] = sch.emit("tensor", lambda e, ic=ic,
                                                 tt4=tt4, half=half, bank=bank:
                                                 e.matmul(
                                bank[:, 0:512],
                                hq_fm[:, ic * 512 + tt4 * 128:ic * 512
                                      + (tt4 + 1) * 128],
                                wdq[:, ic * 1024 + half * 512:ic * 1024
                                    + half * 512 + 512],
                                start=(ic == 0), stop=(ic == 21)),
                                inc=(1 if ic == 21 else 0),
                                waits=[obank_free[half]] if ic == 0 else [])
                    t_y0 = sch.emit("vector", lambda e, x=xb, y=yb:
                                    e.tensor_tensor(
                        y[:, 0:512], x[:, 0:512], pC[:, 0:512], ALU.add),
                        waits=[t_x, mml[0]])
                    t_y1 = sch.emit("vector", lambda e, x=xb, y=yb:
                                    e.tensor_tensor(
                        y[:, 512:1024], x[:, 512:1024], pD[:, 0:512], ALU.add),
                        waits=[t_x, mml[1]])
                    obank_free[0] = t_y0
                    obank_free[1] = t_y1
                    sch.dma("sync",
                            y_h[row0 + tt4 * 128:row0 + (tt4 + 1) * 128, :],
                            yb[:], waits=[t_y0, t_y1])
                sch.flush()


def build_ffn1():
    nc = bass.Bass("TRN2", target_bir_lowering=False)
    NROW = SEQ_PER_CORE * S
    hs = {}
    hs["x_h"] = nc.dram_tensor("x", [NROW, D], F32, kind="ExternalInput").ap()
    hs["wg_h"] = nc.dram_tensor("wgate", [I_FF, D], F32, kind="ExternalInput").ap()
    hs["wu_h"] = nc.dram_tensor("wup", [I_FF, D], F32, kind="ExternalInput").ap()
    hs["wd_h"] = nc.dram_tensor("wdown", [D, I_FF], F32, kind="ExternalInput").ap()
    hs["id_h"] = nc.dram_tensor("ident", [128, 128], F32, kind="ExternalInput").ap()
    hs["y_h"] = nc.dram_tensor("y", [NROW, D], F32, kind="ExternalOutput").ap()
    hs["wgq_d"] = nc.dram_tensor("wgq_d", [D, I_FF], BF16)
    hs["wuq_d"] = nc.dram_tensor("wuq_d", [D, I_FF], BF16)
    with ExitStack() as ctx:
        sch = Sch(nc, ctx)
        emit_ffn1(nc, sch, ctx, **hs)
        t_fin = sch.tok("sync")
        sch.emit("gpsimd", lambda e: None, inc=1, waits=[t_fin])
        sch.flush()
    return nc


def run_ffn1(x3, w_gate, w_up, w_down):
    nc = build_ffn1()
    in_maps = []
    wg = np.ascontiguousarray(np.asarray(w_gate, np.float32))
    wu = np.ascontiguousarray(np.asarray(w_up, np.float32))
    wd = np.ascontiguousarray(np.asarray(w_down, np.float32))
    for c in range(NCORES):
        xs = np.ascontiguousarray(
            x3[c * SEQ_PER_CORE:(c + 1) * SEQ_PER_CORE].reshape(
                SEQ_PER_CORE * S, D))
        in_maps.append(dict(x=xs, wgate=wg, wup=wu, wdown=wd, ident=_ident()))
    res = run_bass_kernel_spmd(nc, in_maps, core_ids=list(range(NCORES)))
    return np.concatenate([r["y"].reshape(SEQ_PER_CORE, S, D)
                           for r in res.results], 0)


def _rope_cs_sc():
    inv = 1.0 / (10000.0 ** (np.arange(0, 64, 2, dtype=np.float64) / 64.0))
    pos = np.arange(S, dtype=np.float64)
    ang = inv[:, None] * pos[None, :]
    cos = np.cos(ang)
    sin = np.sin(ang)
    cs = np.concatenate([cos, sin, cos, sin], 0).astype(np.float32)
    sc = np.concatenate([-sin, cos, -sin, cos], 0).astype(np.float32)
    return cs, sc


def _diag_masks():
    from ml_dtypes import bfloat16
    m = np.zeros((128, 4, 512), np.float32)
    p = np.arange(128)[:, None]
    o = np.arange(512)[None, :]
    for d in range(4):
        m[:, d, :] = np.where(p > o - 128 * d, 0.0, 1.0)
    return m.reshape(128, 4 * 512).astype(bfloat16)


def _ident():
    return np.eye(128, dtype=np.float32)


def run_attn0(x, Wq, bq, Wkv, bkv, Wo, bo):
    nc = build_attn0()
    cs, sc = _rope_cs_sc()
    dm = _diag_masks()
    in_maps = []
    Wq = np.ascontiguousarray(np.asarray(Wq, np.float32))
    Wkv = np.ascontiguousarray(np.asarray(Wkv, np.float32))
    Wo = np.ascontiguousarray(np.asarray(Wo, np.float32))
    for c in range(NCORES):
        xs = np.ascontiguousarray(
            x[c * SEQ_PER_CORE:(c + 1) * SEQ_PER_CORE].reshape(SEQ_PER_CORE * S, D))
        in_maps.append(dict(x=xs, wq=Wq, wkv=Wkv, wo=Wo,
                            bq=np.asarray(bq, np.float32).reshape(D, 1),
                            bkv=np.asarray(bkv, np.float32).reshape(2 * D, 1),
                            bo=np.asarray(bo, np.float32).reshape(1, D),
                            csfull=cs, scfull=sc, diagmask=dm, ident=_ident()))
    res = run_bass_kernel_spmd(nc, in_maps, core_ids=list(range(NCORES)))
    return np.concatenate([r["y"].reshape(SEQ_PER_CORE, S, D)
                           for r in res.results], 0)


def kernel(**inp):
    return kernel_fused(phases="ABCD", **inp)


def kernel_multilaunch(**inp):
    x = np.asarray(inp["L0_attn_x"], np.float32)
    x1 = run_attn0(x, inp["L0_attn_Wq"], inp["L0_attn_bq"], inp["L0_attn_Wkv"],
                   inp["L0_attn_bkv"], inp["L0_attn_Wo"], inp["L0_attn_bo"])
    x2 = run_ffn0(x1, inp["L0_ffn_wf"], inp["L0_ffn_uf"], inp["L0_ffn_bf"],
                  inp["L0_ffn_wi"], inp["L0_ffn_ui"], inp["L0_ffn_bi"],
                  inp["L0_ffn_wc"], inp["L0_ffn_uc"], inp["L0_ffn_bc"],
                  inp["L0_ffn_wo"], inp["L0_ffn_uo"], inp["L0_ffn_bo"],
                  inp["L0_ffn_in_c"])
    x3 = run_attn1(x2, inp["L1_attn_Wq"], inp["L1_attn_Wk"], inp["L1_attn_Wv"],
                   inp["L1_attn_Wo"], inp["L1_attn_Wf"])
    x4 = run_ffn1(x3, inp["L1_ffn_w_gate"], inp["L1_ffn_w_up"],
                  inp["L1_ffn_w_down"])
    return x4


# version 07 (final)
# speedup vs baseline: 1.2196x; 1.0155x over previous
import numpy as np
from contextlib import ExitStack

import concourse.bass as bass
import concourse.mybir as mybir
from concourse.bass_utils import run_bass_kernel_spmd
from concourse.bass import IndirectOffsetOnAxis

F32 = mybir.dt.float32
BF16 = mybir.dt.bfloat16
I32 = mybir.dt.int32
AF = mybir.ActivationFunctionType
ALU = mybir.AluOpType
AX = mybir.AxisListType

NCORES = 8
B, S, D = 16, 2048, 1024
H, DH = 16, 64
SEQ_PER_CORE = B // NCORES
NEG = -1.0e30


class Sch:
    ENGINES = ("sync", "tensor", "vector", "scalar", "gpsimd")

    def __init__(self, nc, ctx):
        self.nc = nc
        self.ops = {e: [] for e in self.ENGINES}
        self.cnt = {e: 0 for e in self.ENGINES}
        self.sems = {e: ctx.enter_context(nc.semaphore(f"s_{e}")) for e in self.ENGINES}
        self.nblk = 0
        sems = list(self.sems.values())

        def clear_all(e, _s=tuple(sems)):
            for s_ in _s:
                e.sem_clear(s_)
        self.ops["gpsimd"].append(lambda e: clear_all(e))
        self.flush()

    def emit(self, eng, fn, inc=1, waits=()):
        best = {}
        for w in waits:
            if w is None:
                continue
            we, wv = w
            if we == eng and wv > self.cnt[eng]:
                raise AssertionError("same-engine wait on future op")
            if wv > best.get(we, -1):
                best[we] = wv
        waits = tuple(best.items())
        sem_map = self.sems
        if inc:
            self.cnt[eng] += inc
        tok = (eng, self.cnt[eng])

        def run(e, _fn=fn, _w=waits, _inc=inc, _sem=sem_map[eng]):
            for (we, wv) in _w:
                e.wait_ge(sem_map[we], wv)
            ins = _fn(e)
            if _inc:
                if ins is not None:
                    ins.then_inc(_sem, _inc)
                else:
                    e.nop().then_inc(_sem, _inc)
        self.ops[eng].append(run)
        return tok

    def dma(self, eng, out, in_, waits=()):
        tok = self.emit(eng, lambda e, o=out, i=in_: e.dma_start(out=o, in_=i),
                        inc=16, waits=waits)
        self.emit(eng, lambda e: None, inc=0, waits=[tok])
        return tok

    def dma_slow(self, eng, out, in_, waits=()):
        return self.emit(eng, lambda e, o=out, i=in_: e.dma_start(
            out=o, in_=i, allow_slow_non_contiguous=True), inc=16, waits=waits)

    def tok(self, eng):
        return (eng, self.cnt[eng])

    def flush(self):
        self.nblk += 1
        live = [(e, self.ops[e]) for e in self.ENGINES if self.ops[e]]
        if not live:
            return
        with self.nc.Block(f"blk{self.nblk}") as blk:
            for ename, oplist in live:
                dec = getattr(blk, ename)

                def body(engobj, _ops=tuple(oplist)):
                    for f in _ops:
                        f(engobj)
                dec(body)
        for e in self.ENGINES:
            self.ops[e] = []


_NAMEC = [0]


def sbuf(nc, ctx, name, shape, dt):
    _NAMEC[0] += 1
    return ctx.enter_context(nc.sbuf_tensor(f"{name}_{_NAMEC[0]}", shape, dt))


def psum(nc, ctx, name, shape, dt):
    _NAMEC[0] += 1
    return ctx.enter_context(nc.psum_tensor(f"{name}_{_NAMEC[0]}", shape, dt))


class PHalf:

    def __init__(self, t, off):
        self.t, self.off = t, off

    def __getitem__(self, idx):
        ps, fs = idx
        lo = fs.start or 0
        hi = fs.stop if fs.stop is not None else 512
        return self.t[ps, self.off + lo: self.off + hi]


def make_ln_state(nc, ctx, tag=""):
    st = {}
    st["xin"] = [sbuf(nc, ctx, f"lnx{i}{tag}", [128, D], F32) for i in range(2)]
    st["sq_scr"] = [sbuf(nc, ctx, f"lnsq{i}{tag}", [128, D], F32) for i in range(2)]
    st["ln_tm"] = [sbuf(nc, ctx, f"lnt{i}{tag}", [128, D], BF16) for i in range(2)]
    st["stat"] = sbuf(nc, ctx, f"lnstat{tag}", [128, 16], F32)
    st["id_bf"] = sbuf(nc, ctx, f"idbf{tag}", [128, 128], BF16)
    st["epsb"] = sbuf(nc, ctx, f"epsb{tag}", [128, 1], F32)
    st["xin_free"] = [None, None]
    st["sqw_free"] = [None, None]
    st["stat_free"] = [None, None]
    st["ln_free"] = [None, None]
    st["ptr_free"] = [None, None]
    return st


def emit_load_ident(sch, st, id_h):
    idf = st["xin"][0]
    t = sch.dma("sync", idf[:, 0:128], id_h)
    t2 = sch.emit("vector", lambda e: e.tensor_copy(st["id_bf"][:], idf[:, 0:128]),
                  waits=[t])
    st["xin_free"][0] = t2
    st["t_eps"] = sch.emit("gpsimd", lambda e: e.memset(st["epsb"][:], 1e-5))


def emit_ln_to_fm(sch, st, x_h, row0, ntiles, ln_fm, fm_stride, fm_col0,
                  dram_out=None, dram_rows=None, post_tile=None):
    sums = st["stat"]
    for tt in range(ntiles):
        p = tt % 2
        co = 8 * p
        xb = st["xin"][p]
        sq_scr = st["sq_scr"][p]
        lnb = st["ln_tm"][p]
        t_x = sch.dma("sync", xb[:], x_h[row0 + tt * 128: row0 + (tt + 1) * 128, :],
                      waits=[st["xin_free"][p]])
        t_sum = sch.emit("vector", lambda e, x=xb, co=co: e.tensor_reduce(
            sums[:, co + 0:co + 1], x[:], AX.X, ALU.add),
            waits=[t_x, st["stat_free"][p]])
        t_sq0 = sch.emit("vector", lambda e, x=xb, sq=sq_scr: e.tensor_tensor(
            sq[:], x[:], x[:], ALU.mult), waits=[t_x, st["sqw_free"][p]])
        t_sq = sch.emit("vector", lambda e, sq=sq_scr, co=co: e.tensor_reduce(
            sums[:, co + 4:co + 5], sq[:], AX.X, ALU.add),
            waits=[t_sq0, st["stat_free"][p]])
        st["sqw_free"][p] = t_sq
        t_mean = sch.emit("vector", lambda e, co=co: e.tensor_scalar(
            sums[:, co + 1:co + 2], sums[:, co + 0:co + 1], 1.0 / D, None, ALU.mult),
            waits=[t_sum])
        t_msq = sch.emit("vector", lambda e, co=co: e.tensor_scalar(
            sums[:, co + 2:co + 3], sums[:, co + 4:co + 5], 1.0 / D, None, ALU.mult),
            waits=[t_sq])
        t_mm = sch.emit("vector", lambda e, co=co: e.tensor_tensor(
            sums[:, co + 3:co + 4], sums[:, co + 1:co + 2], sums[:, co + 1:co + 2],
            ALU.mult), waits=[t_mean])
        t_var = sch.emit("vector", lambda e, co=co: e.tensor_tensor(
            sums[:, co + 5:co + 6], sums[:, co + 2:co + 3], sums[:, co + 3:co + 4],
            ALU.subtract), waits=[t_msq, t_mm])
        t_sd = sch.emit("scalar", lambda e, co=co: e.activation(
            sums[:, co + 6:co + 7], sums[:, co + 5:co + 6], AF.Sqrt,
            bias=st["epsb"][:, 0:1]), waits=[t_var, st["t_eps"]])
        t_sd2 = sch.emit("scalar", lambda e, co=co: e.activation(
            sums[:, co + 4:co + 5], sums[:, co + 5:co + 6], AF.Sqrt,
            bias=st["epsb"][:, 0:1]), waits=[t_sd])
        t_rstd = sch.emit("vector", lambda e, co=co: e.reciprocal(
            sums[:, co + 7:co + 8], sums[:, co + 6:co + 7]), waits=[t_sd2])
        t_ap = sch.emit("vector", lambda e, x=xb, l=lnb, co=co: e.tensor_scalar(
            l[:], x[:], sums[:, co + 1:co + 2], sums[:, co + 7:co + 8],
            ALU.subtract, ALU.mult), waits=[t_rstd, st["ln_free"][p]])
        st["xin_free"][p] = t_ap
        st["stat_free"][p] = t_ap
        if dram_out is not None:
            r0d = dram_rows[tt]
            t_dma = sch.dma("sync", dram_out[r0d:r0d + 128, :],
                            st["ln_tm"][p][:], waits=[t_ap])
            st["ln_free"][p] = t_dma
            if post_tile is not None:
                st["ln_free"][p] = post_tile(tt, p, t_dma)
            sch.flush()
            continue
        last_cp = None
        for c in range(8):
            pb = st["ptr"][c % 2]
            t_tr = sch.emit("tensor", lambda e, l=lnb, c=c, pb=pb: e.transpose(
                pb[0:128, 0:128], l[:, c * 128:(c + 1) * 128], st["id_bf"][:]),
                waits=[t_ap, st["ptr_free"][c % 2]])
            t_cp = sch.emit("vector", lambda e, c=c, pb=pb, tt=tt: e.tensor_copy(
                ln_fm[:, c * fm_stride + fm_col0 + tt * 128:
                      c * fm_stride + fm_col0 + (tt + 1) * 128], pb[0:128, 0:128]),
                waits=[t_tr])
            st["ptr_free"][c % 2] = t_cp
            last_cp = t_cp
        st["ln_free"][p] = last_cp
        sch.flush()


def emit_attn0(nc, sch, ctx, x_h, y_h, wqdup_h, wvt_h, wo_h, bq_h, bkv_h,
               bo_h, cs_h, sc_h, id_h, dm_h):
    if True:
        st = make_ln_state(nc, ctx)
        ln_fm = sbuf(nc, ctx, "lnfm", [128, 8 * S], BF16)
        attn_fm = sbuf(nc, ctx, "attnfm", [128, 8 * S], BF16)
        w_f = [sbuf(nc, ctx, f"wf{i}", [128, 1024], F32) for i in range(2)]
        wqA = sbuf(nc, ctx, "wqA", [128, 1024], BF16)
        wqB = sbuf(nc, ctx, "wqB", [128, 1024], BF16)
        wkA = sbuf(nc, ctx, "wkA", [128, 1024], BF16)
        wkB = sbuf(nc, ctx, "wkB", [128, 1024], BF16)
        wv_b = sbuf(nc, ctx, "wvb", [128, 1024], BF16)
        csf = sbuf(nc, ctx, "csf", [128, S], F32)
        scf = sbuf(nc, ctx, "scf", [128, S], F32)
        ropeS = [sbuf(nc, ctx, f"ropeS{i}", [128, 512], F32) for i in range(2)]
        qr_b = sbuf(nc, ctx, "qrb", [128, S], BF16)
        kr_b = sbuf(nc, ctx, "krb", [128, S], BF16)
        v_b = sbuf(nc, ctx, "vb", [128, S], BF16)
        vhat = [sbuf(nc, ctx, f"vhat{h}", [128, 16 * 66], BF16) for h in range(2)]
        idf32 = sbuf(nc, ctx, "idf32", [128, 128], F32)
        bstg = sbuf(nc, ctx, "bstg", [16, 128], F32)
        bqpair = sbuf(nc, ctx, "bqpair", [128, 8], F32)
        bkpair = sbuf(nc, ctx, "bkpair", [128, 8], F32)
        biaqA = sbuf(nc, ctx, "biaqA", [128, 8], F32)
        biaqB = sbuf(nc, ctx, "biaqB", [128, 8], F32)
        biakA = sbuf(nc, ctx, "biakA", [128, 8], F32)
        biakB = sbuf(nc, ctx, "biakB", [128, 8], F32)
        biav = sbuf(nc, ctx, "biav", [128, 8], F32)
        eA = [sbuf(nc, ctx, f"eA{i}", [128, 512], BF16) for i in range(2)]
        eB = [sbuf(nc, ctx, f"eB{i}", [128, 512], BF16) for i in range(2)]
        dmask = sbuf(nc, ctx, "dmask", [128, 4 * 512], BF16)
        rcp = sbuf(nc, ctx, "rcp", [1, 1024], F32)
        sbc = sbuf(nc, ctx, "sbc", [128, 1024], F32)
        onesf = sbuf(nc, ctx, "onesf", [1, 128], F32)
        bm4 = sbuf(nc, ctx, "bm4", [128, 1], F32)
        bo_bc = sbuf(nc, ctx, "bobc", [128, D], F32)
        borow = sbuf(nc, ctx, "borow", [1, D], F32)
        wob = sbuf(nc, ctx, "wob", [128, 8 * 1024], BF16)
        xrt = [sbuf(nc, ctx, f"xrt{i}", [128, D], F32) for i in range(2)]
        yt = [sbuf(nc, ctx, f"yt{i}", [128, D], F32) for i in range(2)]

        psA = psum(nc, ctx, "psA", [128, 512], F32)
        psB = psum(nc, ctx, "psB", [128, 512], F32)
        psA2 = psum(nc, ctx, "psA2", [128, 512], F32)
        psB2 = psum(nc, ctx, "psB2", [128, 512], F32)
        poA = psum(nc, ctx, "poA", [128, 512], F32)
        poB = psum(nc, ctx, "poB", [128, 512], F32)
        pT = psum(nc, ctx, "pT", [128, 1024], BF16)
        pT2 = psum(nc, ctx, "pT2", [128, 1024], BF16)
        st["ptr"] = [PHalf(pT, 0), PHalf(pT2, 0)]

        t_dm = sch.dma("sync", dmask[:], dm_h)
        emit_load_ident(sch, st, id_h)
        sch.dma("sync", csf[:], cs_h)
        sch.dma("sync", scf[:], sc_h)
        t_bo = sch.dma("sync", borow[:], bo_h)
        t_idf = sch.dma("sync", idf32[:], id_h)
        t_l1 = sch.dma("sync", bstg[0:8, :],
                       bq_h.rearrange("(c p) o -> c (p o)", c=8))
        tt1 = sch.emit("tensor", lambda e: e.transpose(
            psA2[:, 0:8], bstg[0:8, :], idf32[0:8, 0:8]), waits=[t_l1, t_idf])
        t_bq = sch.emit("vector", lambda e: e.tensor_copy(
            bqpair[:], psA2[:, 0:8]), waits=[tt1])
        t_l2 = sch.dma("sync", bstg[:],
                       bkv_h.rearrange("(c p) o -> c (p o)", c=16), waits=[t_bq])
        tt2 = sch.emit("tensor", lambda e: e.transpose(
            psB2[:, 0:16], bstg[:], idf32[0:16, 0:16]), waits=[t_l2, t_idf, t_bq])
        t_bk = sch.emit("vector", lambda e: e.tensor_copy(
            bkpair[:], psB2[:, 0:8]), waits=[tt2])
        t_bv = sch.emit("vector", lambda e: e.tensor_copy(
            biav[:], psB2[:, 8:16]), waits=[tt2])
        for (dst, srcb, seg0, twt) in ((biaqA, bqpair, 0, t_bq),
                                       (biaqB, bqpair, 32, t_bq),
                                       (biakA, bkpair, 0, t_bk),
                                       (biakB, bkpair, 32, t_bk)):
            for h_ in range(2):
                for dd in range(2):
                    sch.emit("vector", lambda e, dst=dst, srcb=srcb, seg0=seg0,
                             h_=h_, dd=dd: e.tensor_copy(
                        dst[h_ * 64 + dd * 32:h_ * 64 + dd * 32 + 32, :],
                        srcb[h_ * 64 + seg0:h_ * 64 + seg0 + 32, :]), waits=[twt])
        t_bias = sch.tok("vector")
        t_on = sch.emit("gpsimd", lambda e: e.memset(onesf[:], 1.0))
        sch.emit("gpsimd", lambda e: e.memset(bm4[:], -4.0))
        for hh in range(2):
            for kt in range(16):
                sch.emit("gpsimd", lambda e, h=hh, k=kt: e.memset(
                    vhat[h][:, k * 66 + 64: k * 66 + 65], 1.0))
        tb1 = sch.emit("tensor", lambda e: e.matmul(
            psA[:, 0:512], onesf[:, 0:128], borow[:, 0:512], start=True, stop=True),
            waits=[t_bo, t_on])
        tb2 = sch.emit("tensor", lambda e: e.matmul(
            psB[:, 0:512], onesf[:, 0:128], borow[:, 512:1024], start=True,
            stop=True), waits=[t_bo, t_on])
        sch.emit("vector", lambda e: e.tensor_copy(bo_bc[:, 0:512], psA[:, 0:512]),
                 waits=[tb1])
        t_bobc = sch.emit("vector", lambda e: e.tensor_copy(
            bo_bc[:, 512:1024], psB[:, 0:512]), waits=[tb2])
        sch.flush()

        wo_loaded = False
        a0_exp = [[None, None], [None, None]]
        a0_pv = [[None, None], [None, None]]
        a0_norm = [None, None]
        a0_attnw = [None, None]
        for s in range(SEQ_PER_CORE):
            row0 = s * S
            emit_ln_to_fm(sch, st, x_h, row0, 16, ln_fm, S, 0)
            sch.flush()

            for hp in range(8):
                c0 = hp * 128

                wfree = [None, None]
                tw = {}
                for vi, (nm, dstb) in enumerate((("qA", wqA), ("qB", wqB),
                                                 ("kA", wkA), ("kB", wkB))):
                    sl = vi % 2
                    t_w = sch.dma("sync", w_f[sl][:],
                                  wqdup_h[(vi * 8 + hp) * 128:
                                          (vi * 8 + hp + 1) * 128, :],
                                  waits=[wfree[sl]])
                    tw[nm] = sch.emit("vector", lambda e, sl=sl, d=dstb:
                                      e.tensor_copy(d[:], w_f[sl][:]),
                                      waits=[t_w])
                    wfree[sl] = tw[nm]
                t_w = sch.dma("sync", w_f[0][:],
                              wvt_h[hp * 128:(hp + 1) * 128, :],
                              waits=[wfree[0]])
                tw["v"] = sch.emit("vector", lambda e: e.tensor_copy(
                    wv_b[:], w_f[0][:]), waits=[t_w])
                wfree[0] = tw["v"]
                sch.flush()

                bank_free = {id(psA): [], id(psB): [], id(psA2): [], id(psB2): []}

                def mmgroup(pb, wb, tc, extra_waits):
                    mmt = None
                    for c in range(8):
                        mmt = sch.emit("tensor", lambda e, c=c, tc=tc, wb=wb, pb=pb:
                                       e.matmul(
                            pb[:, 0:512], wb[:, c * 128:(c + 1) * 128],
                            ln_fm[:, c * S + tc * 512:c * S + (tc + 1) * 512],
                            start=(c == 0), stop=(c == 7)),
                            inc=(1 if c == 7 else 0),
                            waits=(bank_free[id(pb)] + extra_waits) if c == 0 else [])
                    return mmt

                for tc in range(4):
                    pa = [psA, psB][tc % 2]
                    pb2 = [psA2, psB2][tc % 2]
                    mA = mmgroup(pa, wqA, tc, [tw["qA"]])
                    mB = mmgroup(pb2, wqB, tc, [tw["qB"]])
                    tr1 = sch.emit("vector", lambda e, pa=pa, hp=hp, tc=tc:
                                   e.scalar_tensor_tensor(
                        ropeS[0][:], pa[:, 0:512], biaqA[:, hp:hp + 1],
                        csf[:, tc * 512:(tc + 1) * 512], ALU.add, ALU.mult),
                        waits=[mA, t_bias])
                    tr2 = sch.emit("vector", lambda e, pb2=pb2, hp=hp, tc=tc:
                                   e.scalar_tensor_tensor(
                        ropeS[1][:], pb2[:, 0:512], biaqB[:, hp:hp + 1],
                        scf[:, tc * 512:(tc + 1) * 512], ALU.add, ALU.mult),
                        waits=[mB, t_bias])
                    tq = sch.emit("vector", lambda e, tc=tc: e.tensor_tensor(
                        qr_b[:, tc * 512:(tc + 1) * 512], ropeS[0][:], ropeS[1][:],
                        ALU.add), waits=[tr1, tr2])
                    bank_free[id(pa)] = [tr1]
                    bank_free[id(pb2)] = [tr2]
                    mA = mmgroup(pa, wkA, tc, [tw["kA"]])
                    mB = mmgroup(pb2, wkB, tc, [tw["kB"]])
                    tr1 = sch.emit("vector", lambda e, pa=pa, hp=hp, tc=tc:
                                   e.scalar_tensor_tensor(
                        ropeS[0][:], pa[:, 0:512], biakA[:, hp:hp + 1],
                        csf[:, tc * 512:(tc + 1) * 512], ALU.add, ALU.mult),
                        waits=[mA])
                    tr2 = sch.emit("vector", lambda e, pb2=pb2, hp=hp, tc=tc:
                                   e.scalar_tensor_tensor(
                        ropeS[1][:], pb2[:, 0:512], biakB[:, hp:hp + 1],
                        scf[:, tc * 512:(tc + 1) * 512], ALU.add, ALU.mult),
                        waits=[mB])
                    tk = sch.emit("vector", lambda e, tc=tc: e.tensor_tensor(
                        kr_b[:, tc * 512:(tc + 1) * 512], ropeS[0][:], ropeS[1][:],
                        ALU.add), waits=[tr1, tr2])
                    bank_free[id(pa)] = [tr1]
                    bank_free[id(pb2)] = [tr2]
                    mV = mmgroup(pa, wv_b, tc, [tw["v"]])
                    tv = sch.emit("scalar", lambda e, pa=pa, hp=hp, tc=tc:
                                  e.activation(
                        v_b[:, tc * 512:(tc + 1) * 512], pa[:, 0:512], AF.Identity,
                        bias=biav[:, hp:hp + 1]), waits=[mV, t_bias])
                    bank_free[id(pa)] = [tv]
                t_v = sch.tok("scalar")
                sch.flush()

                for hh in range(2):
                    for kt in range(16):
                        pb = st["ptr"][kt % 2]
                        t_tr = sch.emit("tensor", lambda e, hh=hh, kt=kt, pb=pb:
                                        e.transpose(
                            pb[0:128, 0:64],
                            v_b[hh * 64:(hh + 1) * 64, kt * 128:(kt + 1) * 128],
                            st["id_bf"][hh * 64:(hh + 1) * 64,
                                        hh * 64:(hh + 1) * 64]),
                            waits=[st["ptr_free"][kt % 2], t_v])
                        t_cp = sch.emit("vector", lambda e, hh=hh, kt=kt, pb=pb:
                                        e.tensor_copy(
                            vhat[hh][:, kt * 66:kt * 66 + 64], pb[0:128, 0:64]),
                            waits=[t_tr])
                        st["ptr_free"][kt % 2] = t_cp
                sch.flush()

                for qc in range(4):
                    kt_hi = 4 * (qc + 1)
                    for kt in range(kt_hi):
                        sb = kt % 2
                        psx = [[psA, psA2][sb], [psB, psB2][sb]]
                        ebufs = [eA[sb], eB[sb]]
                        d = kt - 4 * qc
                        for hh in range(2):
                            tqk = sch.emit("tensor", lambda e, p=psx[hh], hh=hh,
                                           kt=kt, qc=qc: e.matmul(
                                p[:, 0:512],
                                kr_b[hh * 64:(hh + 1) * 64, kt * 128:(kt + 1) * 128],
                                qr_b[hh * 64:(hh + 1) * 64, qc * 512:(qc + 1) * 512],
                                start=True, stop=True, tile_position=(hh * 64, 0)),
                                waits=[a0_exp[hh][sb], a0_norm[hh]])
                            texp = sch.emit("scalar", lambda e, p=psx[hh],
                                            d2=ebufs[hh]: e.activation(
                                d2[:], p[:, 0:512], AF.Exp, scale=0.125,
                                bias=bm4[:, 0:1]),
                                waits=[tqk, a0_pv[hh][sb]])
                            if d >= 0:
                                texp = sch.emit("vector", lambda e, d=d,
                                                d2=ebufs[hh]: e.tensor_tensor(
                                    d2[:], d2[:], dmask[:, d * 512:(d + 1) * 512],
                                    ALU.mult), waits=[texp, t_dm])
                            po = [poA, poB][hh]
                            tpv = sch.emit("tensor", lambda e, hh=hh, kt=kt, po=po,
                                           d2=ebufs[hh], last=(kt == kt_hi - 1):
                                           e.matmul(
                                po[0:65, 0:512], vhat[hh][:, kt * 66:kt * 66 + 65],
                                d2[:], start=(kt == 0), stop=last),
                                waits=[texp] + ([a0_attnw[hh]] if kt == 0 else []))
                            a0_exp[hh][sb] = texp
                            a0_pv[hh][sb] = tpv
                    for hh in range(2):
                        po = [poA, poB][hh]
                        pn = [psA, psB][hh]
                        trc = sch.emit("vector", lambda e, po=po, hh=hh:
                                       e.reciprocal(rcp[0:1, hh * 512:(hh + 1) * 512],
                                                    po[64:65, 0:512]),
                                       waits=[a0_pv[hh][(kt_hi - 1) % 2]])
                        tbc = sch.emit("tensor", lambda e, hh=hh, pn=pn: e.matmul(
                            pn[:, 0:512], onesf[:, 0:128],
                            rcp[0:1, hh * 512:(hh + 1) * 512], start=True, stop=True),
                            waits=[trc, a0_exp[hh][0], a0_exp[hh][1]])
                        tcb = sch.emit("vector", lambda e, hh=hh, pn=pn:
                                       e.tensor_copy(
                            sbc[:, hh * 512:(hh + 1) * 512], pn[:, 0:512]),
                            waits=[tbc])
                        a0_norm[hh] = tcb
                        a0_attnw[hh] = sch.emit(
                            "vector", lambda e, po=po, hh=hh, qc=qc, hp=hp:
                            e.tensor_tensor(
                                attn_fm[hh * 64:(hh + 1) * 64,
                                        hp * S + qc * 512:hp * S + (qc + 1) * 512],
                                po[0:64, 0:512],
                                sbc[0:64, hh * 512:(hh + 1) * 512],
                                ALU.mult), waits=[tcb])

            if not wo_loaded:
                wo_loaded = True
                prev = None
                for pc in range(8):
                    wf = w_f[pc % 2]
                    t_l = sch.dma("sync", wf[:],
                                  wo_h[pc * 128:(pc + 1) * 128, :],
                                  waits=[prev])
                    prev = sch.emit("vector", lambda e, pc=pc, wf=wf: e.tensor_copy(
                        wob[:, pc * 1024:(pc + 1) * 1024], wf[:]), waits=[t_l])
                sch.flush()

            obank_free = [[a0_norm[0]], [a0_norm[1]]]
            prev_y = [None, None]
            prev_st = [None, None]
            for tt in range(16):
                p = tt % 2
                xb = xrt[p]
                yb = yt[p]
                t_x = sch.dma("sync", xb[:],
                              x_h[row0 + tt * 128:row0 + (tt + 1) * 128, :],
                              waits=[prev_y[p]])
                mml = [None, None]
                for half in range(2):
                    pb = [psA, psB][half]
                    for pc in range(8):
                        mml[half] = sch.emit("tensor", lambda e, pc=pc, half=half,
                                             tt=tt, pb=pb: e.matmul(
                            pb[:, 0:512],
                            attn_fm[:, pc * S + tt * 128:pc * S + (tt + 1) * 128],
                            wob[:, pc * 1024 + half * 512:
                                pc * 1024 + half * 512 + 512],
                            start=(pc == 0), stop=(pc == 7)),
                            inc=(1 if pc == 7 else 0),
                            waits=obank_free[half] if pc == 0 else [])
                t_a1 = sch.emit("vector", lambda e, x=xb, y=yb: e.tensor_tensor(
                    y[:], x[:], bo_bc[:], ALU.add),
                    waits=[t_x, t_bobc, prev_st[p]])
                t_y0 = sch.emit("vector", lambda e, y=yb: e.tensor_tensor(
                    y[:, 0:512], y[:, 0:512], psA[:, 0:512], ALU.add),
                    waits=[t_a1, mml[0]])
                t_y1 = sch.emit("vector", lambda e, y=yb: e.tensor_tensor(
                    y[:, 512:1024], y[:, 512:1024], psB[:, 0:512], ALU.add),
                    waits=[t_a1, mml[1]])
                obank_free[0] = [t_y0]
                obank_free[1] = [t_y1]
                prev_y[p] = t_y1
                prev_st[p] = sch.dma(
                    "sync", y_h[row0 + tt * 128:row0 + (tt + 1) * 128, :],
                    yb[:], waits=[t_y0, t_y1])
            sch.flush()


def build_attn0():
    nc = bass.Bass("TRN2", target_bir_lowering=False)
    hs = {}
    hs["x_h"] = nc.dram_tensor("x", [SEQ_PER_CORE * S, D], F32,
                               kind="ExternalInput").ap()
    hs["wq_h"] = nc.dram_tensor("wq", [D, D], F32, kind="ExternalInput").ap()
    hs["wkv_h"] = nc.dram_tensor("wkv", [D, 2 * D], F32, kind="ExternalInput").ap()
    hs["wo_h"] = nc.dram_tensor("wo", [D, D], F32, kind="ExternalInput").ap()
    hs["bq_h"] = nc.dram_tensor("bq", [D, 1], F32, kind="ExternalInput").ap()
    hs["bkv_h"] = nc.dram_tensor("bkv", [2 * D, 1], F32, kind="ExternalInput").ap()
    hs["bo_h"] = nc.dram_tensor("bo", [1, D], F32, kind="ExternalInput").ap()
    hs["cs_h"] = nc.dram_tensor("csfull", [128, S], F32, kind="ExternalInput").ap()
    hs["sc_h"] = nc.dram_tensor("scfull", [128, S], F32, kind="ExternalInput").ap()
    hs["id_h"] = nc.dram_tensor("ident", [128, 128], F32, kind="ExternalInput").ap()
    hs["dm_h"] = nc.dram_tensor("diagmask", [128, 4 * 512], BF16,
                                kind="ExternalInput").ap()
    hs["y_h"] = nc.dram_tensor("y", [SEQ_PER_CORE * S, D], F32,
                               kind="ExternalOutput").ap()
    with ExitStack() as ctx:
        sch = Sch(nc, ctx)
        emit_attn0(nc, sch, ctx, **hs)
        t_fin = sch.tok("sync")
        sch.emit("gpsimd", lambda e: None, inc=1, waits=[t_fin])
        sch.flush()
    return nc


def emit_attn1(nc, sch, ctx, x_h, y_h, wqt_h, wkt_h, wvt_h, wo_h, wf_h,
               id_h, dm_h):
    if True:
        st = make_ln_state(nc, ctx)
        ln_fm = sbuf(nc, ctx, "lnfm", [128, 8 * S], BF16)
        attn_fm = sbuf(nc, ctx, "attnfm", [128, 8 * S], BF16)
        w_f = [sbuf(nc, ctx, f"wf{i}", [128, 1024], F32) for i in range(2)]
        wq_b = sbuf(nc, ctx, "wqb", [128, 1024], BF16)
        wk_b = sbuf(nc, ctx, "wkb", [128, 1024], BF16)
        wv_b = sbuf(nc, ctx, "wvb", [128, 1024], BF16)
        wfb = sbuf(nc, ctx, "wfb", [128, 128], BF16)
        qhA = sbuf(nc, ctx, "qhA", [68, S], BF16)
        qhB = sbuf(nc, ctx, "qhB", [68, S], BF16)
        khA = sbuf(nc, ctx, "khA", [68, S], BF16)
        khB = sbuf(nc, ctx, "khB", [68, S], BF16)
        v_b = sbuf(nc, ctx, "vb", [128, S], BF16)
        vhat = [sbuf(nc, ctx, f"vhat{h}", [128, 16 * 66], BF16) for h in range(2)]
        U1 = sbuf(nc, ctx, "U1", [128, S], F32)
        U2 = sbuf(nc, ctx, "U2", [128, S], F32)
        Ubf = sbuf(nc, ctx, "Ubf", [128, S], BF16)
        lst = [sbuf(nc, ctx, f"lst{i}", [128, 16], F32) for i in range(4)]
        lsig = sbuf(nc, ctx, "lsig", [128, 16], F32)
        eA = [sbuf(nc, ctx, f"eA{i}", [128, 512], BF16) for i in range(2)]
        dmask = sbuf(nc, ctx, "dmask", [128, 4 * 512], F32)
        rcp = sbuf(nc, ctx, "rcp", [1, 1024], F32)
        sbc = sbuf(nc, ctx, "sbc", [128, 512], F32)
        onesf = sbuf(nc, ctx, "onesf", [1, 128], F32)
        bm4 = sbuf(nc, ctx, "bm4", [128, 1], F32)
        idf32 = sbuf(nc, ctx, "idf32C", [128, 128], F32)
        wob = sbuf(nc, ctx, "wob", [128, 8 * 1024], BF16)
        xrt = [sbuf(nc, ctx, f"xrt{i}", [128, D], F32) for i in range(2)]
        yt = [sbuf(nc, ctx, f"yt{i}", [128, D], F32) for i in range(2)]

        psA = psum(nc, ctx, "psA", [128, 512], F32)
        psB = psum(nc, ctx, "psB", [128, 512], F32)
        psA2 = psum(nc, ctx, "psA2", [128, 512], F32)
        psB2 = psum(nc, ctx, "psB2", [128, 512], F32)
        poA = psum(nc, ctx, "poA", [128, 512], F32)
        poB = psum(nc, ctx, "poB", [128, 512], F32)
        pT = psum(nc, ctx, "pT", [128, 1024], BF16)
        pT2 = psum(nc, ctx, "pT2", [128, 1024], BF16)
        st["ptr"] = [PHalf(pT, 0), PHalf(pT2, 0)]

        t_dm = sch.dma("sync", dmask[:], dm_h)
        emit_load_ident(sch, st, id_h)
        t_idf = sch.dma("sync", idf32[:], id_h)
        t_on = sch.emit("gpsimd", lambda e: e.memset(onesf[:], 1.0))
        sch.emit("gpsimd", lambda e: e.memset(bm4[:], -4.0))
        t_z = sch.emit("gpsimd", lambda e: e.memset(U2[0:16, :], 0.0))
        for qk in (qhA, qhB, khA, khB):
            sch.emit("gpsimd", lambda e, qk=qk: e.memset(qk[64:68, :], 1.0))
        for hh in range(2):
            for kt in range(16):
                sch.emit("gpsimd", lambda e, h=hh, k=kt: e.memset(
                    vhat[h][:, k * 66 + 64: k * 66 + 65], 1.0))
        t_wf = sch.dma("sync", w_f[0].ap().rearrange(
            "p (c m) -> p c m", c=8)[:, :, 0:16],
            wf_h.rearrange("(c p) m -> p c m", p=128))
        t_wfb = sch.emit("vector", lambda e: e.tensor_copy(
            wfb.ap().rearrange("p (c m) -> p c m", c=8),
            w_f[0].ap().rearrange("p (c m) -> p c m", c=8)[:, :, 0:16]),
            waits=[t_wf])
        st["xin_free"][0] = t_wfb
        wo_toks = []
        for pc in range(8):
            wfs = w_f[pc % 2]
            t_l = sch.dma("sync", wfs[:], wo_h[pc * 128:(pc + 1) * 128, :],
                          waits=[wo_toks[-1] if wo_toks else t_wfb])
            t_c = sch.emit("vector", lambda e, pc=pc, wfs=wfs: e.tensor_copy(
                wob[:, pc * 1024:(pc + 1) * 1024], wfs[:]), waits=[t_l])
            wo_toks.append(t_c)
        sch.flush()

        g_exp = [None, None]
        g_pv = [None, None]
        g_attn = None
        for s in range(SEQ_PER_CORE):
            row0 = s * S
            emit_ln_to_fm(sch, st, x_h, row0, 16, ln_fm, S, 0)
            sch.flush()

            pv_a = pv_m = pv_tr = pv_cp = None
            for tt in range(16):
                mmt = None
                for c in range(8):
                    mmt = sch.emit("tensor", lambda e, c=c, tt=tt: e.matmul(
                        psB2[:, 0:16],
                        ln_fm[:, c * S + tt * 128:c * S + (tt + 1) * 128],
                        wfb[:, c * 16:(c + 1) * 16],
                        start=(c == 0), stop=(c == 7)),
                        inc=(1 if c == 7 else 0),
                        waits=[pv_a, pv_m] if c == 0 else [])
                t_a = sch.emit("scalar", lambda e: e.activation(
                    lst[0][:], psB2[:, 0:16], AF.Abs), waits=[mmt])
                t_e = sch.emit("scalar", lambda e: e.activation(
                    lst[1][:], lst[0][:], AF.Exp, scale=-1.0), waits=[t_a])
                t_l = sch.emit("scalar", lambda e: e.activation(
                    lst[2][:], lst[1][:], AF.Ln, bias=1.0), waits=[t_e])
                t_m = sch.emit("vector", lambda e: e.tensor_scalar(
                    lst[3][:], psB2[:, 0:16], 0.0, None, ALU.min), waits=[mmt])
                t_s = sch.emit("vector", lambda e: e.tensor_tensor(
                    lsig[:], lst[3][:], lst[2][:], ALU.subtract),
                    waits=[t_l, t_m, pv_tr])
                t_tr = sch.emit("tensor", lambda e: e.transpose(
                    psA2[0:16, 0:128], lsig[:], idf32[:]),
                    waits=[t_s, t_idf, pv_cp])
                t_cp = sch.emit("vector", lambda e, tt=tt: e.tensor_copy(
                    U1[0:16, tt * 128:(tt + 1) * 128], psA2[0:16, 0:128]),
                    waits=[t_tr])
                pv_a, pv_m, pv_tr, pv_cp = t_a, t_m, t_tr, t_cp
            t_scan = sch.emit("vector", lambda e: e.tensor_tensor_scan(
                U1[32:48, :], U1[0:16, :], U2[0:16, :], 0.0, ALU.add, ALU.add),
                waits=[t_z, sch.tok("vector")])
            t_hi = sch.emit("vector", lambda e: e.tensor_copy(
                Ubf[0:16, :], U1[32:48, :]), waits=[t_scan])
            t_hf = sch.emit("vector", lambda e: e.tensor_copy(
                U2[32:48, :], Ubf[0:16, :]), waits=[t_hi])
            t_lo = sch.emit("vector", lambda e: e.tensor_tensor(
                U1[64:80, :], U1[32:48, :], U2[32:48, :], ALU.subtract),
                waits=[t_hf])
            t_lob = sch.emit("vector", lambda e: e.tensor_copy(
                Ubf[64:80, :], U1[64:80, :]), waits=[t_lo])
            t_nhi = sch.emit("vector", lambda e: e.tensor_scalar(
                Ubf[32:48, :], Ubf[0:16, :], -1.0, None, ALU.mult), waits=[t_hi])
            t_nlo = sch.emit("vector", lambda e: e.tensor_scalar(
                Ubf[96:112, :], Ubf[64:80, :], -1.0, None, ALU.mult),
                waits=[t_lob])
            sch.flush()

            for hp in range(8):
                c0 = hp * 128
                tws = []
                for (w_hx, dstb) in ((wqt_h, wq_b), (wkt_h, wk_b), (wvt_h, wv_b)):
                    sl = len(tws) % 2
                    t_w = sch.dma("sync", w_f[sl][:],
                                  w_hx[hp * 128:(hp + 1) * 128, :],
                                  waits=[st.get(f"wfree{sl}")])
                    t_c = sch.emit("vector", lambda e, sl=sl, d=dstb: e.tensor_copy(
                        d[:], w_f[sl][:]), waits=[t_w, g_exp[0], g_exp[1]])
                    st[f"wfree{sl}"] = t_c
                    tws.append(t_c)

                bank_free = {id(psA): [g_exp[0], g_exp[1]], id(psB): [g_attn]}

                def mmgroup(pb, wb, tc, extra):
                    mmt = None
                    for c in range(8):
                        mmt = sch.emit("tensor", lambda e, c=c, tc=tc, wb=wb,
                                       pb=pb: e.matmul(
                            pb[:, 0:512], wb[:, c * 128:(c + 1) * 128],
                            ln_fm[:, c * S + tc * 512:c * S + (tc + 1) * 512],
                            start=(c == 0), stop=(c == 7)),
                            inc=(1 if c == 7 else 0),
                            waits=(bank_free[id(pb)] + extra) if c == 0 else [])
                    return mmt

                for tc in range(4):
                    pb = [psA, psB][tc % 2]
                    mq = mmgroup(pb, wq_b, tc, [tws[0]])
                    tq0 = sch.emit("scalar", lambda e, pb=pb, tc=tc: e.activation(
                        qhA[0:64, tc * 512:(tc + 1) * 512],
                        pb[0:64, 0:512], AF.Identity, scale=0.125), waits=[mq])
                    tq1 = sch.emit("scalar", lambda e, pb=pb, tc=tc: e.activation(
                        qhB[0:64, tc * 512:(tc + 1) * 512],
                        pb[64:128, 0:512], AF.Identity, scale=0.125), waits=[mq])
                    bank_free[id(pb)] = [tq0, tq1]
                    mk = mmgroup(pb, wk_b, tc, [tws[1]])
                    tk0 = sch.emit("scalar", lambda e, pb=pb, tc=tc: e.activation(
                        khA[0:64, tc * 512:(tc + 1) * 512],
                        pb[0:64, 0:512], AF.Identity), waits=[mk])
                    tk1 = sch.emit("scalar", lambda e, pb=pb, tc=tc: e.activation(
                        khB[0:64, tc * 512:(tc + 1) * 512],
                        pb[64:128, 0:512], AF.Identity), waits=[mk])
                    bank_free[id(pb)] = [tk0, tk1]
                    mv = mmgroup(pb, wv_b, tc, [tws[2]])
                    tv = sch.emit("scalar", lambda e, pb=pb, tc=tc: e.activation(
                        v_b[:, tc * 512:(tc + 1) * 512], pb[:, 0:512],
                        AF.Identity), waits=[mv])
                    bank_free[id(pb)] = [tv]
                t_v = sch.tok("scalar")

                t_rs = [[], []]
                for hh2, (qh, kh) in enumerate(((qhA, khA), (qhB, khB))):
                    h_abs = 2 * hp + hh2
                    t_rs[hh2].append(sch.dma(
                        "sync", qh[64:65, :], Ubf[h_abs:h_abs + 1, :],
                        waits=[t_hi, g_exp[0], g_exp[1]]))
                    t_rs[hh2].append(sch.dma(
                        "sync", qh[66:67, :], Ubf[64 + h_abs:64 + h_abs + 1, :],
                        waits=[t_lob]))
                    t_rs[hh2].append(sch.dma(
                        "sync", kh[65:66, :], Ubf[32 + h_abs:32 + h_abs + 1, :],
                        waits=[t_nhi]))
                    t_rs[hh2].append(sch.dma(
                        "sync", kh[67:68, :], Ubf[96 + h_abs:96 + h_abs + 1, :],
                        waits=[t_nlo]))

                for h2 in range(2):
                    for kt in range(16):
                        pb2 = st["ptr"][kt % 2]
                        t_tr = sch.emit("tensor", lambda e, h2=h2, kt=kt,
                                        pb2=pb2: e.transpose(
                            pb2[0:128, 0:64],
                            v_b[h2 * 64:(h2 + 1) * 64,
                                kt * 128:(kt + 1) * 128],
                            st["id_bf"][h2 * 64:(h2 + 1) * 64,
                                        h2 * 64:(h2 + 1) * 64]),
                            waits=[st["ptr_free"][kt % 2], t_v])
                        t_cp = sch.emit("vector", lambda e, h2=h2, kt=kt,
                                        pb2=pb2: e.tensor_copy(
                            vhat[h2][:, kt * 66:kt * 66 + 64],
                            pb2[0:128, 0:64]), waits=[t_tr])
                        st["ptr_free"][kt % 2] = t_cp

                for hh in range(2):
                    qh, kh = ((qhA, khA), (qhB, khB))[hh]
                    vh = vhat[hh]
                    for qc in range(4):
                        kt_hi = 4 * (qc + 1)
                        for kt in range(kt_hi):
                            sb = kt % 2
                            px = [psA2, psB2][sb]
                            ebuf = eA[sb]
                            d = kt - 4 * qc
                            w0 = [g_exp[sb]]
                            if qc == 0 and kt == 0:
                                w0 += t_rs[hh]
                            tqk = sch.emit("tensor", lambda e, px=px, kt=kt,
                                           qc=qc, qh=qh, kh=kh: e.matmul(
                                px[:, 0:512],
                                kh[0:68, kt * 128:(kt + 1) * 128],
                                qh[0:68, qc * 512:(qc + 1) * 512],
                                start=True, stop=True), waits=w0)
                            tmask = None
                            if d >= 0:
                                tmask = sch.emit("vector", lambda e, px=px, d=d:
                                                 e.tensor_tensor(
                                    px[:, 0:512], px[:, 0:512],
                                    dmask[:, d * 512:(d + 1) * 512], ALU.add),
                                    waits=[tqk, t_dm])
                            texp = sch.emit("scalar", lambda e, px=px, d2=ebuf:
                                            e.activation(
                                d2[:], px[:, 0:512], AF.Exp, scale=1.0,
                                bias=bm4[:, 0:1]),
                                waits=[tmask or tqk, g_pv[sb]])
                            tpv = sch.emit("tensor", lambda e, kt=kt, vh=vh,
                                           d2=ebuf, last=(kt == kt_hi - 1):
                                           e.matmul(
                                poA[0:65, 0:512],
                                vh[:, kt * 66:kt * 66 + 65],
                                d2[:], start=(kt == 0), stop=last),
                                waits=[texp] + ([g_attn] if kt == 0 else []))
                            g_exp[sb] = texp
                            g_pv[sb] = tpv
                        trc = sch.emit("vector", lambda e: e.reciprocal(
                            rcp[0:1, 0:512], poA[64:65, 0:512]),
                            waits=[g_pv[(kt_hi - 1) % 2]])
                        tbc = sch.emit("tensor", lambda e: e.matmul(
                            psB[:, 0:512], onesf[:, 0:128], rcp[0:1, 0:512],
                            start=True, stop=True), waits=[trc])
                        tcb = sch.emit("vector", lambda e: e.tensor_copy(
                            sbc[:], psB[:, 0:512]), waits=[tbc])
                        g_attn = sch.emit("vector", lambda e, hh=hh, qc=qc,
                                          hp=hp: e.tensor_tensor(
                            attn_fm[hh * 64:(hh + 1) * 64,
                                    hp * S + qc * 512:hp * S + (qc + 1) * 512],
                            poA[0:64, 0:512], sbc[0:64, :], ALU.mult),
                            waits=[tcb])
            sch.flush()

            obank_free = [None, None]
            prev_y = [None, None]
            prev_st = [None, None]
            for tt in range(16):
                p = tt % 2
                xb = xrt[p]
                yb = yt[p]
                t_x = sch.dma("sync", xb[:],
                              x_h[row0 + tt * 128:row0 + (tt + 1) * 128, :],
                              waits=[prev_y[p]])
                mml = [None, None]
                for half in range(2):
                    pb = [psA, psB][half]
                    for pc in range(8):
                        mml[half] = sch.emit("tensor", lambda e, pc=pc, half=half,
                                             tt=tt, pb=pb: e.matmul(
                            pb[:, 0:512],
                            attn_fm[:, pc * S + tt * 128:pc * S + (tt + 1) * 128],
                            wob[:, pc * 1024 + half * 512:
                                pc * 1024 + half * 512 + 512],
                            start=(pc == 0), stop=(pc == 7)),
                            inc=(1 if pc == 7 else 0),
                            waits=[obank_free[half]] if pc == 0 else [])
                t_y0 = sch.emit("vector", lambda e, x=xb, y=yb: e.tensor_tensor(
                    y[:, 0:512], x[:, 0:512], psA[:, 0:512], ALU.add),
                    waits=[t_x, mml[0], prev_st[p]])
                t_y1 = sch.emit("vector", lambda e, x=xb, y=yb: e.tensor_tensor(
                    y[:, 512:1024], x[:, 512:1024], psB[:, 0:512], ALU.add),
                    waits=[t_x, mml[1], prev_st[p]])
                obank_free[0] = t_y0
                obank_free[1] = t_y1
                prev_y[p] = t_y1
                prev_st[p] = sch.dma(
                    "sync", y_h[row0 + tt * 128:row0 + (tt + 1) * 128, :],
                    yb[:], waits=[t_y0, t_y1])
            sch.flush()


def build_attn1():
    nc = bass.Bass("TRN2", target_bir_lowering=False)
    hs = {}
    hs["x_h"] = nc.dram_tensor("x", [SEQ_PER_CORE * S, D], F32,
                               kind="ExternalInput").ap()
    hs["wq_h"] = nc.dram_tensor("wq", [D, D], F32, kind="ExternalInput").ap()
    hs["wk_h"] = nc.dram_tensor("wk", [D, D], F32, kind="ExternalInput").ap()
    hs["wv_h"] = nc.dram_tensor("wv", [D, D], F32, kind="ExternalInput").ap()
    hs["wo_h"] = nc.dram_tensor("wo", [D, D], F32, kind="ExternalInput").ap()
    hs["wf_h"] = nc.dram_tensor("wf", [D, H], F32, kind="ExternalInput").ap()
    hs["id_h"] = nc.dram_tensor("ident", [128, 128], F32, kind="ExternalInput").ap()
    hs["dm_h"] = nc.dram_tensor("diagmaskA", [128, 4 * 512], F32,
                                kind="ExternalInput").ap()
    hs["y_h"] = nc.dram_tensor("y", [SEQ_PER_CORE * S, D], F32,
                               kind="ExternalOutput").ap()
    with ExitStack() as ctx:
        sch = Sch(nc, ctx)
        emit_attn1(nc, sch, ctx, **hs)
        t_fin = sch.tok("sync")
        sch.emit("gpsimd", lambda e: None, inc=1, waits=[t_fin])
        sch.flush()
    return nc


def _diag_masks_add():
    m = np.zeros((128, 4, 512), np.float32)
    p = np.arange(128)[:, None]
    o = np.arange(512)[None, :]
    for d in range(4):
        m[:, d, :] = np.where(p > o - 128 * d, NEG, 0.0)
    return m.reshape(128, 4 * 512)


def run_attn1(x, Wq, Wk, Wv, Wo, Wf):
    nc = build_attn1()
    in_maps = []
    Wq = np.ascontiguousarray(np.asarray(Wq, np.float32))
    Wk = np.ascontiguousarray(np.asarray(Wk, np.float32))
    Wv = np.ascontiguousarray(np.asarray(Wv, np.float32))
    Wo = np.ascontiguousarray(np.asarray(Wo, np.float32))
    Wf = np.ascontiguousarray(np.asarray(Wf, np.float32))
    dm = _diag_masks_add()
    for c in range(NCORES):
        xs = np.ascontiguousarray(
            x[c * SEQ_PER_CORE:(c + 1) * SEQ_PER_CORE].reshape(SEQ_PER_CORE * S, D))
        in_maps.append(dict(x=xs, wq=Wq, wk=Wk, wv=Wv, wo=Wo, wf=Wf,
                            diagmaskA=dm, ident=_ident()))
    res = run_bass_kernel_spmd(nc, in_maps, core_ids=list(range(NCORES)))
    return np.concatenate([r["y"].reshape(SEQ_PER_CORE, S, D)
                           for r in res.results], 0)


JW = 256
T_STEPS = 16


def build_ffn0():
    nc = bass.Bass("TRN2", target_bir_lowering=False)
    NROW = T_STEPS * JW
    x_h = nc.dram_tensor("x", [NROW, D], F32, kind="ExternalInput").ap()
    w_hs = {}
    for nm in ("wf", "uf", "wi", "ui", "wc", "uc", "wo", "uo"):
        w_hs[nm] = nc.dram_tensor(nm, [D, D], F32, kind="ExternalInput").ap()
    b_hs = {}
    for nm in ("bf", "bi", "bc", "bo", "inc"):
        b_hs[nm] = nc.dram_tensor(nm, [D, 1], F32, kind="ExternalInput").ap()
    id_h = nc.dram_tensor("ident", [128, 128], F32, kind="ExternalInput").ap()
    y_h = nc.dram_tensor("y", [NROW, D], F32, kind="ExternalOutput").ap()

    with ExitStack() as ctx:
        sch = Sch(nc, ctx)
        xg_fm = sbuf(nc, ctx, "xgfm", [128, 8 * NROW], BF16)
        wT = {nm: sbuf(nc, ctx, f"wT_{nm}", [128, 64 * 128], BF16)
              for nm in ("wf", "uf", "wi", "ui", "wc", "uc")}
        c_fm = sbuf(nc, ctx, "cfm", [128, 8 * JW], F32)
        c_prev = sbuf(nc, ctx, "cprev", [128, 8 * JW], BF16)
        biasp = sbuf(nc, ctx, "biasp", [128, 4 * 8], F32)
        idf32 = sbuf(nc, ctx, "idf32B", [128, 128], F32)
        pg = [psum(nc, ctx, f"pg{i}", [128, 512], F32) for i in range(6)]
        pT = psum(nc, ctx, "pTB", [128, 1024], BF16)
        pT2 = psum(nc, ctx, "pT2B", [128, 1024], BF16)

        with ExitStack() as c1:
            st = make_ln_state(nc, c1, "B")
            st["ptr"] = [PHalf(pT, 0), PHalf(pT2, 0)]
            wstg = st["xin"]
            wstgb = st["ln_tm"]
            bstg = sbuf(nc, c1, "bstgB", [32, 128], F32)

            t_idf = sch.dma("sync", idf32[:], id_h)
            emit_load_ident(sch, st, id_h)
            for i, nm in enumerate(("bf", "bi", "bc", "inc")):
                sch.dma("sync", bstg[i * 8:(i + 1) * 8, :],
                        b_hs[nm].rearrange("(c p) o -> c (p o)", c=8))
            t_bl = sch.tok("sync")
            t_btr = sch.emit("tensor", lambda e: e.transpose(
                pg[0][:, 0:32], bstg[:], idf32[0:32, 0:32]), waits=[t_bl, t_idf])
            t_bias = sch.emit("vector", lambda e: e.tensor_copy(
                biasp[:], pg[0][:, 0:32]), waits=[t_btr])
            sch.flush()

            ptr_free = [None, None]
            for wi_, nm in enumerate(("wf", "uf", "wi", "ui", "wc", "uc")):
                for mc in range(8):
                    sl = (wi_ * 8 + mc) % 2
                    t_l = sch.dma("sync", wstg[sl][:],
                                  w_hs[nm][mc * 128:(mc + 1) * 128, :],
                                  waits=[st.get(f"wstg_free{sl}")])
                    t_c = sch.emit("vector", lambda e, sl=sl: e.tensor_copy(
                        wstgb[sl][:], wstg[sl][:]),
                        waits=[t_l, st.get(f"wstgb_free{sl}")])
                    st[f"wstg_free{sl}"] = t_c
                    last = None
                    for kc in range(8):
                        pb = st["ptr"][kc % 2]
                        t_tr = sch.emit("tensor", lambda e, sl=sl, kc=kc, pb=pb:
                                        e.transpose(
                            pb[0:128, 0:128], wstgb[sl][:, kc * 128:(kc + 1) * 128],
                            st["id_bf"][:]), waits=[t_c, ptr_free[kc % 2]])
                        t_cp = sch.emit("vector", lambda e, nm=nm, kc=kc, mc=mc,
                                        pb=pb: e.tensor_copy(
                            wT[nm][:, (kc * 8 + mc) * 128:(kc * 8 + mc + 1) * 128],
                            pb[0:128, 0:128]), waits=[t_tr])
                        ptr_free[kc % 2] = t_cp
                        last = t_cp
                    st[f"wstgb_free{sl}"] = last
                    st["xin_free"][sl] = st[f"wstg_free{sl}"]
                    st["ln_free"][sl] = last
                sch.flush()

            st["ptr_free"] = ptr_free
            emit_ln_to_fm(sch, st, x_h, 0, 32, xg_fm, NROW, 0)
            sch.flush()

        with ExitStack() as c2:
            gsc = [sbuf(nc, c2, f"gsc{i}", [128, JW], F32) for i in range(3)]
            tmp1 = sbuf(nc, c2, "tmp1", [128, JW], F32)
            tmp2 = sbuf(nc, c2, "tmp2", [128, JW], F32)
            t_ms = sch.emit("gpsimd", lambda e: e.memset(c_fm[:], 0.0))
            for mc in range(8):
                sch.emit("vector", lambda e, mc=mc: e.tensor_scalar(
                    c_fm[:, mc * JW:(mc + 1) * JW], c_fm[:, mc * JW:(mc + 1) * JW],
                    biasp[:, 24 + mc:24 + mc + 1], None, ALU.add),
                    waits=[t_ms, t_bias])
            sch.flush()

            last_up = sch.tok("vector")
            last_mm_prev = None
            for t_ in range(T_STEPS):
                t_cpv = sch.emit("vector", lambda e: e.tensor_copy(
                    c_prev[:], c_fm[:]), waits=[last_up, last_mm_prev])
                evac_done = {}
                upd_done = {}
                for mc in range(8):
                    for gi_, (wnm, unm, act, bcol) in enumerate((
                            ("wf", "uf", AF.Sigmoid, 0),
                            ("wi", "ui", AF.Sigmoid, 1),
                            ("wc", "uc", AF.Tanh, 2))):
                        bank = pg[(mc % 2) * 3 + gi_]
                        mm = None
                        for kc in range(8):
                            mm = sch.emit("tensor", lambda e, wnm=wnm, kc=kc,
                                          mc=mc, t_=t_, bank=bank: e.matmul(
                                bank[:, 0:JW],
                                wT[wnm][:, (kc * 8 + mc) * 128:
                                        (kc * 8 + mc + 1) * 128],
                                xg_fm[:, kc * NROW + t_ * JW:
                                      kc * NROW + (t_ + 1) * JW],
                                start=(kc == 0), stop=False), inc=0,
                                waits=[evac_done.get(id(bank)), t_cpv]
                                if kc == 0 else [])
                        for kc in range(8):
                            mm = sch.emit("tensor", lambda e, unm=unm, kc=kc,
                                          mc=mc, bank=bank: e.matmul(
                                bank[:, 0:JW],
                                wT[unm][:, (kc * 8 + mc) * 128:
                                        (kc * 8 + mc + 1) * 128],
                                c_prev[:, kc * JW:(kc + 1) * JW],
                                start=False, stop=(kc == 7)),
                                inc=(1 if kc == 7 else 0))
                        evac_done[id(bank)] = sch.emit(
                            "scalar", lambda e, gi_=gi_, mc=mc, act=act,
                            bcol=bcol, bank=bank: e.activation(
                                gsc[gi_][:], bank[:, 0:JW], act,
                                bias=biasp[:, bcol * 8 + mc:bcol * 8 + mc + 1]),
                            waits=[mm, upd_done.get(gi_)])
                    tk1 = sch.emit("vector", lambda e, mc=mc: e.tensor_tensor(
                        tmp1[:], gsc[0][:], c_fm[:, mc * JW:(mc + 1) * JW],
                        ALU.mult), waits=[evac_done[id(pg[(mc % 2) * 3])], t_cpv])
                    tk2 = sch.emit("vector", lambda e: e.tensor_tensor(
                        tmp2[:], gsc[1][:], gsc[2][:], ALU.mult),
                        waits=[evac_done[id(pg[(mc % 2) * 3 + 1])],
                               evac_done[id(pg[(mc % 2) * 3 + 2])]])
                    t_up = sch.emit("vector", lambda e, mc=mc: e.tensor_tensor(
                        c_fm[:, mc * JW:(mc + 1) * JW], tmp1[:], tmp2[:], ALU.add),
                        waits=[tk1, tk2])
                    for gi_ in range(3):
                        upd_done[gi_] = t_up
                last_up = sch.tok("vector")
                last_mm_prev = sch.tok("tensor")
            sch.emit("vector", lambda e: e.tensor_copy(c_prev[:], c_fm[:]),
                     waits=[last_up, last_mm_prev])
            sch.flush()

        with ExitStack() as c3:
            wstg = [sbuf(nc, c3, f"wstgC{i}", [128, 1024], F32) for i in range(2)]
            wstgb = [sbuf(nc, c3, f"wstgbC{i}", [128, 1024], BF16) for i in range(2)]
            idb = sbuf(nc, c3, "idbC", [128, 128], BF16)
            borow = sbuf(nc, c3, "borowC", [1, D], F32)
            onesf = sbuf(nc, c3, "onesfC", [1, 128], F32)
            bo_bc = sbuf(nc, c3, "bobcC", [128, D], F32)
            cu_bo = [sbuf(nc, c3, f"cubo{i}", [128, D], F32) for i in range(2)]
            hsc = sbuf(nc, c3, "hsc", [128, D], F32)
            ptr_free = [None, None]
            t_idb = sch.emit("vector", lambda e: e.tensor_copy(idb[:], idf32[:]))
            woT, uoT = wT["wf"], wT["uf"]
            stg_free = [None, None]
            stgb_free = [None, None]
            for wi_, (nm, dst) in enumerate((("wo", woT), ("uo", uoT))):
                for mc in range(8):
                    sl = (wi_ * 8 + mc) % 2
                    t_l = sch.dma("sync", wstg[sl][:],
                                  w_hs[nm][mc * 128:(mc + 1) * 128, :],
                                  waits=[stg_free[sl]])
                    t_c = sch.emit("vector", lambda e, sl=sl: e.tensor_copy(
                        wstgb[sl][:], wstg[sl][:]), waits=[t_l, stgb_free[sl]])
                    stg_free[sl] = t_c
                    last = None
                    for kc in range(8):
                        pb = [PHalf(pT, 0), PHalf(pT2, 0)][kc % 2]
                        t_tr = sch.emit("tensor", lambda e, sl=sl, kc=kc, pb=pb:
                                        e.transpose(
                            pb[0:128, 0:128], wstgb[sl][:, kc * 128:(kc + 1) * 128],
                            idb[:]), waits=[t_c, ptr_free[kc % 2], t_idb])
                        t_cp = sch.emit("vector", lambda e, dst=dst, kc=kc, mc=mc,
                                        pb=pb: e.tensor_copy(
                            dst[:, (kc * 8 + mc) * 128:(kc * 8 + mc + 1) * 128],
                            pb[0:128, 0:128]), waits=[t_tr])
                        ptr_free[kc % 2] = t_cp
                        last = t_cp
                    stgb_free[sl] = last
            t_bo = sch.dma("sync", borow[:], b_hs["bo"].rearrange("p o -> o p"))
            t_on = sch.emit("gpsimd", lambda e: e.memset(onesf[:], 1.0))
            tb1 = sch.emit("tensor", lambda e: e.matmul(
                pg[0][:, 0:512], onesf[:, 0:128], borow[:, 0:512], start=True,
                stop=True), waits=[t_bo, t_on])
            tb2 = sch.emit("tensor", lambda e: e.matmul(
                pg[1][:, 0:512], onesf[:, 0:128], borow[:, 512:1024], start=True,
                stop=True), waits=[t_bo, t_on])
            sch.emit("vector", lambda e: e.tensor_copy(bo_bc[:, 0:512],
                                                       pg[0][:, 0:512]), waits=[tb1])
            t_bobc = sch.emit("vector", lambda e: e.tensor_copy(
                bo_bc[:, 512:1024], pg[1][:, 0:512]), waits=[tb2])
            sch.flush()

            for jt in range(2):
                mml = [None, None]
                for half in range(2):
                    bank = pg[2 + half]
                    for kc in range(8):
                        mml[half] = sch.emit("tensor", lambda e, kc=kc, jt=jt,
                                             half=half, bank=bank: e.matmul(
                            bank[:, 0:512],
                            c_prev[:, kc * JW + jt * 128:kc * JW + (jt + 1) * 128],
                            uoT[:, (kc * 8) * 128 + half * 512:
                                (kc * 8) * 128 + half * 512 + 512],
                            start=(kc == 0), stop=(kc == 7)),
                            inc=(1 if kc == 7 else 0))
                t_c1 = sch.emit("vector", lambda e, jt=jt: e.tensor_tensor(
                    cu_bo[jt][:, 0:512], pg[2][:, 0:512], bo_bc[:, 0:512], ALU.add),
                    waits=[mml[0], t_bobc])
                t_c2 = sch.emit("vector", lambda e, jt=jt: e.tensor_tensor(
                    cu_bo[jt][:, 512:1024], pg[3][:, 0:512], bo_bc[:, 512:1024],
                    ALU.add), waits=[mml[1], t_bobc])
            sch.flush()
            xrt = [wstg[0], bo_bc]
            yt = wstg[1]

            obank_free = [None, None]
            prev_th = None
            prev_store = None
            xrt_free = [sch.tok("vector"), sch.tok("vector")]
            for rt in range(32):
                p = rt % 2
                xb = xrt[p]
                yb = yt
                jt = rt % 2
                t_x = sch.dma("sync", xb[:], x_h[rt * 128:(rt + 1) * 128, :],
                              waits=[xrt_free[p]])
                mml = [None, None]
                for half in range(2):
                    bank = pg[4 + half]
                    for kc in range(8):
                        mml[half] = sch.emit("tensor", lambda e, kc=kc, rt=rt,
                                             half=half, bank=bank: e.matmul(
                            bank[:, 0:512],
                            xg_fm[:, kc * NROW + rt * 128:kc * NROW + (rt + 1) * 128],
                            woT[:, kc * 1024 + half * 512:kc * 1024 + half * 512
                                + 512],
                            start=(kc == 0), stop=(kc == 7)),
                            inc=(1 if kc == 7 else 0),
                            waits=[obank_free[half]] if kc == 0 else [])
                t_h0 = sch.emit("vector", lambda e, jt=jt: e.tensor_tensor(
                    hsc[:, 0:512], pg[4][:, 0:512], cu_bo[jt][:, 0:512], ALU.add),
                    waits=[mml[0], prev_th])
                t_h1 = sch.emit("vector", lambda e, jt=jt: e.tensor_tensor(
                    hsc[:, 512:1024], pg[5][:, 0:512], cu_bo[jt][:, 512:1024],
                    ALU.add), waits=[mml[1], t_h0])
                t_th = sch.emit("scalar", lambda e: e.activation(
                    hsc[:], hsc[:], AF.Tanh), waits=[t_h1])
                prev_th = t_th
                t_y = sch.emit("vector", lambda e, x=xb, y=yb: e.tensor_tensor(
                    y[:], x[:], hsc[:], ALU.add),
                    waits=[t_th, t_x, prev_store])
                xrt_free[p] = t_y
                obank_free[0] = t_y
                obank_free[1] = t_y
                prev_store = sch.dma("sync",
                                     y_h[rt * 128:(rt + 1) * 128, :], yb[:],
                                     waits=[t_y])
            sch.flush()

        t_fin = sch.tok("sync")
        sch.emit("gpsimd", lambda e: None, inc=1, waits=[t_fin])
        sch.flush()
    return nc


def emit_ffn0_fused(nc, sch, ctx, x1_d, x2_d, whs, bhs, id_h,
                    ln1loc, arin, arout, cuin, cuout,
                    sidx_h, gidx_h, cidx_h, cc1, cc2):
    NROW = T_STEPS * JW

    wT = {nm: sbuf(nc, ctx, f"wT_{nm}", [128, 64 * 128], BF16)
          for nm in ("wf", "uf")}
    c_fm = sbuf(nc, ctx, "cfm", [128, 8 * JW], F32)
    c_prev = sbuf(nc, ctx, "cprev", [128, 8 * JW], BF16)
    biasp = sbuf(nc, ctx, "biasp", [128, 4 * 8], F32)
    idf32 = sbuf(nc, ctx, "idf32F", [128, 128], F32)
    idb = sbuf(nc, ctx, "idbF", [128, 128], BF16)
    sidx = sbuf(nc, ctx, "sidx", [128, 32], mybir.dt.uint32)
    gidx = sbuf(nc, ctx, "gidx", [128, 32], mybir.dt.uint32)
    cidx = sbuf(nc, ctx, "cidx", [128, 2], mybir.dt.uint32)
    pg = [psum(nc, ctx, f"pg{i}", [128, 512], F32) for i in range(6)]
    pT = psum(nc, ctx, "pTF", [128, 1024], BF16)
    pT2 = psum(nc, ctx, "pT2F", [128, 1024], BF16)
    ptrs = [PHalf(pT, 0), PHalf(pT2, 0)]

    def idma(out, in_off, in_, out_off, waits=()):
        tok = sch.emit("gpsimd", lambda e, o=out, i=in_, io=in_off, oo=out_off:
                       e.indirect_dma_start(out=o, out_offset=oo, in_=i,
                                            in_offset=io), inc=16, waits=waits)
        sch.emit("gpsimd", lambda e: None, inc=0, waits=[tok])
        return tok

    with ExitStack() as c1:
        st = make_ln_state(nc, c1, "F")
        st["ptr"] = ptrs
        zt = sbuf(nc, c1, "zt", [128, 16384], BF16)
        emit_load_ident(sch, st, id_h)
        t_idf = sch.dma("sync", idf32[:], id_h)
        t_idb = sch.emit("vector", lambda e: e.tensor_copy(idb[:], idf32[:]),
                         waits=[t_idf])
        t_zm = sch.emit("gpsimd", lambda e: e.memset(zt[:], 0.0))
        sch.dma("sync", sidx[:], sidx_h)
        sch.dma("sync", gidx[:], gidx_h)
        sch.dma("sync", cidx[:], cidx_h)
        t_zero = None
        for kk in range(16):
            t_zero = sch.dma("sync",
                             arin[kk * 2048:(kk + 1) * 2048, :].rearrange(
                                 "(a p) d -> p a d", p=128),
                             zt.ap().rearrange("p (a d) -> p a d", a=16),
                             waits=[t_zm])
        t_zero2 = sch.dma("sync", cuin.rearrange("(a p) d -> p a d", p=128),
                          zt.ap().rearrange("p (a d) -> p a d", a=16)
                          [:, 0:16, :], waits=[t_zm])
        sch.flush()

        def post_tile(tt, p, t_dma):
            return idma(arin, None, st["ln_tm"][p][:],
                        IndirectOffsetOnAxis(sidx[:, tt:tt + 1], 0),
                        waits=[t_dma, t_zero, t_zero2])

        rows = [tt * 128 for tt in range(32)]
        emit_ln_to_fm(sch, st, x1_d, 0, 32, None, 0, 0,
                      dram_out=ln1loc, dram_rows=rows, post_tile=post_tile)
        sch.flush()

    t_lg = sch.tok("gpsimd")

    def _coll_ar1(e):
        ins_ = e.collective_compute(
            "AllReduce", ALU.add, replica_groups=[list(range(NCORES))],
            ins=[arin.opt()], outs=[arout.opt()])
        ins_.then_inc(cc1)
        return None
    sch.emit("gpsimd", _coll_ar1, inc=1, waits=[t_lg])
    sch.flush()

    with ExitStack() as cXG:
        xg_fm = sbuf(nc, cXG, "xgfm", [128, 8 * NROW], BF16)
        for nm in ("wi", "ui", "wc", "uc"):
            wT[nm] = sbuf(nc, cXG, f"wT_{nm}", [128, 64 * 128], BF16)

        with ExitStack() as c2:
            xgin = [sbuf(nc, c2, f"xgin{i}", [128, D], BF16) for i in range(2)]
            xg_free = [None, None]
            ptr_free = [None, None]
            first = [True]
            for tt in range(32):
                p = tt % 2
                xb = xgin[p]
                if first[0]:
                    first[0] = False
                    t_l = sch.emit("gpsimd", lambda e, xb=xb, tt=tt: (
                        e.wait_ge(cc1, 1),
                        e.indirect_dma_start(
                            out=xb[:], out_offset=None, in_=arout,
                            in_offset=IndirectOffsetOnAxis(
                                gidx[:, tt:tt + 1], 0)))[1],
                        inc=16, waits=[xg_free[p]])
                    sch.emit("gpsimd", lambda e: None, inc=0, waits=[t_l])
                else:
                    t_l = idma(xb[:], IndirectOffsetOnAxis(gidx[:, tt:tt + 1], 0),
                               arout, None, waits=[xg_free[p]])
                last = None
                for c in range(8):
                    pb = ptrs[c % 2]
                    t_tr = sch.emit("tensor", lambda e, xb=xb, c=c, pb=pb:
                                    e.transpose(
                        pb[0:128, 0:128], xb[:, c * 128:(c + 1) * 128], idb[:]),
                        waits=[t_l, ptr_free[c % 2], t_idb])
                    t_cp = sch.emit("vector", lambda e, c=c, pb=pb, tt=tt:
                                    e.tensor_copy(
                        xg_fm[:, c * NROW + tt * 128:c * NROW + (tt + 1) * 128],
                        pb[0:128, 0:128]), waits=[t_tr])
                    ptr_free[c % 2] = t_cp
                    last = t_cp
                xg_free[p] = last
            sch.flush()

            wstg = [sbuf(nc, c2, f"wstgF{i}", [128, 1024], F32) for i in range(2)]
            wstgb = [sbuf(nc, c2, f"wstgbF{i}", [128, 1024], BF16)
                     for i in range(2)]
            bstg = sbuf(nc, c2, "bstgF", [32, 128], F32)
            stf = {}
            for i, nm in enumerate(("bf", "bi", "bc", "inc")):
                sch.dma("sync", bstg[i * 8:(i + 1) * 8, :],
                        bhs[nm].rearrange("(c p) o -> c (p o)", c=8))
            t_bl = sch.tok("sync")
            t_btr = sch.emit("tensor", lambda e: e.transpose(
                pg[0][:, 0:32], bstg[:], idf32[0:32, 0:32]), waits=[t_bl, t_idf])
            t_bias = sch.emit("vector", lambda e: e.tensor_copy(
                biasp[:], pg[0][:, 0:32]), waits=[t_btr])
            sch.flush()
            for wi_, nm in enumerate(("wf", "uf", "wi", "ui", "wc", "uc")):
                for mc in range(8):
                    sl = (wi_ * 8 + mc) % 2
                    t_l = sch.dma("sync", wstg[sl][:],
                                  whs[nm][mc * 128:(mc + 1) * 128, :],
                                  waits=[stf.get(f"ws{sl}")])
                    t_c = sch.emit("vector", lambda e, sl=sl: e.tensor_copy(
                        wstgb[sl][:], wstg[sl][:]),
                        waits=[t_l, stf.get(f"wb{sl}")])
                    stf[f"ws{sl}"] = t_c
                    last = None
                    for kc in range(8):
                        pb = ptrs[kc % 2]
                        t_tr = sch.emit("tensor", lambda e, sl=sl, kc=kc, pb=pb:
                                        e.transpose(
                            pb[0:128, 0:128], wstgb[sl][:, kc * 128:(kc + 1)
                                                        * 128], idb[:]),
                            waits=[t_c, ptr_free[kc % 2], t_idb])
                        t_cp = sch.emit("vector", lambda e, nm=nm, kc=kc, mc=mc,
                                        pb=pb: e.tensor_copy(
                            wT[nm][:, (kc * 8 + mc) * 128:
                                   (kc * 8 + mc + 1) * 128],
                            pb[0:128, 0:128]), waits=[t_tr])
                        ptr_free[kc % 2] = t_cp
                        last = t_cp
                    stf[f"wb{sl}"] = last
                sch.flush()

        with ExitStack() as c3:
            gsc = [sbuf(nc, c3, f"gscF{i}", [128, JW], F32) for i in range(3)]
            tmp1 = sbuf(nc, c3, "tmp1F", [128, JW], F32)
            tmp2 = sbuf(nc, c3, "tmp2F", [128, JW], F32)
            t_ms = sch.emit("gpsimd", lambda e: e.memset(c_fm[:], 0.0))
            for mc in range(8):
                sch.emit("vector", lambda e, mc=mc: e.tensor_scalar(
                    c_fm[:, mc * JW:(mc + 1) * JW],
                    c_fm[:, mc * JW:(mc + 1) * JW],
                    biasp[:, 24 + mc:24 + mc + 1], None, ALU.add),
                    waits=[t_ms, t_bias])
            sch.flush()

            last_up = sch.tok("vector")
            last_mm_prev = None
            for t_ in range(T_STEPS):
                t_cpv = sch.emit("vector", lambda e: e.tensor_copy(
                    c_prev[:], c_fm[:]), waits=[last_up, last_mm_prev])
                evac_done = {}
                upd_done = {}
                for mc in range(8):
                    for gi_, (wnm, unm, act, bcol) in enumerate((
                            ("wf", "uf", AF.Sigmoid, 0),
                            ("wi", "ui", AF.Sigmoid, 1),
                            ("wc", "uc", AF.Tanh, 2))):
                        bank = pg[(mc % 2) * 3 + gi_]
                        mm = None
                        for kc in range(8):
                            mm = sch.emit("tensor", lambda e, wnm=wnm, kc=kc,
                                          mc=mc, t_=t_, bank=bank: e.matmul(
                                bank[:, 0:JW],
                                wT[wnm][:, (kc * 8 + mc) * 128:
                                        (kc * 8 + mc + 1) * 128],
                                xg_fm[:, kc * NROW + t_ * JW:
                                      kc * NROW + (t_ + 1) * JW],
                                start=(kc == 0), stop=False), inc=0,
                                waits=[evac_done.get(id(bank)), t_cpv]
                                if kc == 0 else [])
                        for kc in range(8):
                            mm = sch.emit("tensor", lambda e, unm=unm, kc=kc,
                                          mc=mc, bank=bank: e.matmul(
                                bank[:, 0:JW],
                                wT[unm][:, (kc * 8 + mc) * 128:
                                        (kc * 8 + mc + 1) * 128],
                                c_prev[:, kc * JW:(kc + 1) * JW],
                                start=False, stop=(kc == 7)),
                                inc=(1 if kc == 7 else 0))
                        evac_done[id(bank)] = sch.emit(
                            "scalar", lambda e, gi_=gi_, mc=mc, act=act,
                            bcol=bcol, bank=bank: e.activation(
                                gsc[gi_][:], bank[:, 0:JW], act,
                                bias=biasp[:, bcol * 8 + mc:bcol * 8 + mc + 1]),
                            waits=[mm, upd_done.get(gi_)])
                    tk1 = sch.emit("vector", lambda e, mc=mc: e.tensor_tensor(
                        tmp1[:], gsc[0][:], c_fm[:, mc * JW:(mc + 1) * JW],
                        ALU.mult), waits=[evac_done[id(pg[(mc % 2) * 3])],
                                          t_cpv])
                    tk2 = sch.emit("vector", lambda e: e.tensor_tensor(
                        tmp2[:], gsc[1][:], gsc[2][:], ALU.mult),
                        waits=[evac_done[id(pg[(mc % 2) * 3 + 1])],
                               evac_done[id(pg[(mc % 2) * 3 + 2])]])
                    t_up = sch.emit("vector", lambda e, mc=mc: e.tensor_tensor(
                        c_fm[:, mc * JW:(mc + 1) * JW], tmp1[:], tmp2[:],
                        ALU.add), waits=[tk1, tk2])
                    for gi_ in range(3):
                        upd_done[gi_] = t_up
                last_up = sch.tok("vector")
                last_mm_prev = sch.tok("tensor")
            t_cfin = sch.emit("vector", lambda e: e.tensor_copy(
                c_prev[:], c_fm[:]), waits=[last_up, last_mm_prev])
            sch.flush()

    with ExitStack() as c4:
        wstg = [sbuf(nc, c4, f"wstgG{i}", [128, 1024], F32) for i in range(2)]
        wstgb = [sbuf(nc, c4, f"wstgbG{i}", [128, 1024], BF16) for i in range(2)]
        woT, uoT = wT["wf"], wT["uf"]
        ptr_free = [None, None]
        stf = {}
        for wi_, (nm, dst) in enumerate((("wo", woT), ("uo", uoT))):
            for mc in range(8):
                sl = (wi_ * 8 + mc) % 2
                t_l = sch.dma("sync", wstg[sl][:],
                              whs[nm][mc * 128:(mc + 1) * 128, :],
                              waits=[stf.get(f"ws{sl}")])
                t_c = sch.emit("vector", lambda e, sl=sl: e.tensor_copy(
                    wstgb[sl][:], wstg[sl][:]), waits=[t_l, stf.get(f"wb{sl}")])
                stf[f"ws{sl}"] = t_c
                last = None
                for kc in range(8):
                    pb = [PHalf(pT, 0), PHalf(pT2, 0)][kc % 2]
                    t_tr = sch.emit("tensor", lambda e, sl=sl, kc=kc, pb=pb:
                                    e.transpose(
                        pb[0:128, 0:128], wstgb[sl][:, kc * 128:(kc + 1) * 128],
                        idb[:]), waits=[t_c, ptr_free[kc % 2]])
                    t_cp = sch.emit("vector", lambda e, dst=dst, kc=kc, mc=mc,
                                    pb=pb: e.tensor_copy(
                        dst[:, (kc * 8 + mc) * 128:(kc * 8 + mc + 1) * 128],
                        pb[0:128, 0:128]), waits=[t_tr])
                    ptr_free[kc % 2] = t_cp
                    last = t_cp
                stf[f"wb{sl}"] = last
        sch.flush()

        cu_bf = [sbuf(nc, c4, f"cubf{i}", [128, D], BF16) for i in range(2)]
        cu_evac = [None, None]
        for jt in range(2):
            mml = [None, None]
            for half in range(2):
                bank = pg[2 + half]
                for kc in range(8):
                    mml[half] = sch.emit("tensor", lambda e, kc=kc, jt=jt,
                                         half=half, bank=bank: e.matmul(
                        bank[:, 0:512],
                        c_prev[:, kc * JW + jt * 128:kc * JW + (jt + 1) * 128],
                        uoT[:, kc * 1024 + half * 512:kc * 1024 + half * 512
                            + 512],
                        start=(kc == 0), stop=(kc == 7)),
                        inc=(1 if kc == 7 else 0),
                        waits=[cu_evac[half]] if kc == 0 else [])
            t_c1 = sch.emit("vector", lambda e, jt=jt: e.tensor_copy(
                cu_bf[jt][:, 0:512], pg[2][:, 0:512]), waits=[mml[0]])
            t_c2 = sch.emit("vector", lambda e, jt=jt: e.tensor_copy(
                cu_bf[jt][:, 512:1024], pg[3][:, 0:512]), waits=[mml[1]])
            cu_evac[0], cu_evac[1] = t_c1, t_c2
            idma(cuin, None, cu_bf[jt][:],
                 IndirectOffsetOnAxis(cidx[:, jt:jt + 1], 0),
                 waits=[t_c1, t_c2])
        t_lg2 = sch.tok("gpsimd")

        def _coll_ar2(e):
            ins_ = e.collective_compute(
                "AllReduce", ALU.add, replica_groups=[list(range(NCORES))],
                ins=[cuin.opt()], outs=[cuout.opt()])
            ins_.then_inc(cc2)
            return None
        sch.emit("gpsimd", _coll_ar2, inc=1, waits=[t_lg2])
        sch.flush()

        own_fm = sbuf(nc, c4, "ownfm", [128, 8 * NROW], BF16)
        xgin = [sbuf(nc, c4, f"oxin{i}", [128, D], BF16) for i in range(2)]
        xg_free = [None, None]
        for tt in range(32):
            p = tt % 2
            xb = xgin[p]
            t_l = sch.dma("sync", xb[:], ln1loc[tt * 128:(tt + 1) * 128, :],
                          waits=[xg_free[p]])
            last = None
            for c in range(8):
                pb = [PHalf(pT, 0), PHalf(pT2, 0)][c % 2]
                t_tr = sch.emit("tensor", lambda e, xb=xb, c=c, pb=pb:
                                e.transpose(
                    pb[0:128, 0:128], xb[:, c * 128:(c + 1) * 128], idb[:]),
                    waits=[t_l, ptr_free[c % 2]])
                t_cp = sch.emit("vector", lambda e, c=c, pb=pb, tt=tt:
                                e.tensor_copy(
                    own_fm[:, c * NROW + tt * 128:c * NROW + (tt + 1) * 128],
                    pb[0:128, 0:128]), waits=[t_tr])
                ptr_free[c % 2] = t_cp
                last = t_cp
            xg_free[p] = last
        sch.flush()

        borow = sbuf(nc, c4, "borowG", [1, D], F32)
        onesf = sbuf(nc, c4, "onesfG", [1, 128], F32)
        bo_bc = sbuf(nc, c4, "bobcG", [128, D], F32)
        cu32 = sbuf(nc, c4, "cu32", [128, D], F32)
        cubo = sbuf(nc, c4, "cubo", [128, D], F32)
        hsc = sbuf(nc, c4, "hscG", [128, D], F32)
        xrt = [sbuf(nc, c4, f"xrtG{i}", [128, D], F32) for i in range(2)]
        yt = sbuf(nc, c4, "ytG", [128, D], F32)
        t_bo = sch.dma("sync", borow[:], bhs["bo"].rearrange("p o -> o p"))
        t_on = sch.emit("gpsimd", lambda e: e.memset(onesf[:], 1.0))
        tb1 = sch.emit("tensor", lambda e: e.matmul(
            pg[0][:, 0:512], onesf[:, 0:128], borow[:, 0:512], start=True,
            stop=True), waits=[t_bo, t_on])
        tb2 = sch.emit("tensor", lambda e: e.matmul(
            pg[1][:, 0:512], onesf[:, 0:128], borow[:, 512:1024], start=True,
            stop=True), waits=[t_bo, t_on])
        sch.emit("vector", lambda e: e.tensor_copy(bo_bc[:, 0:512],
                                                   pg[0][:, 0:512]), waits=[tb1])
        t_bobc = sch.emit("vector", lambda e: e.tensor_copy(
            bo_bc[:, 512:1024], pg[1][:, 0:512]), waits=[tb2])
        sch.flush()

        obank_free = [None, None]
        prev_th = None
        prev_store = None
        xrt_free = [None, None]
        cu_used = None
        first_cu = [True]
        for st_ in range(16):
            if first_cu[0]:
                first_cu[0] = False
                t_cl = sch.emit("sync", lambda e, st_=st_: (
                    e.wait_ge(cc2, 1),
                    e.dma_start(out=cu_bf[0][:],
                                in_=cuout[st_ * 128:(st_ + 1) * 128, :]))[1],
                    inc=16, waits=[])
                sch.emit("sync", lambda e: None, inc=0, waits=[t_cl])
            else:
                t_cl = sch.dma("sync", cu_bf[0][:],
                               cuout[st_ * 128:(st_ + 1) * 128, :],
                               waits=[cu_used])
            t_c32 = sch.emit("vector", lambda e: e.tensor_copy(
                cu32[:], cu_bf[0][:]), waits=[t_cl, prev_th])
            t_cub = sch.emit("vector", lambda e: e.tensor_tensor(
                cubo[:], cu32[:], bo_bc[:], ALU.add), waits=[t_c32, t_bobc])
            cu_used = t_c32
            for b_l in range(2):
                rt = b_l * 16 + st_
                p = rt % 2
                xb = xrt[p]
                t_x = sch.dma("sync", xb[:],
                              x1_d[rt * 128:(rt + 1) * 128, :],
                              waits=[xrt_free[p]])
                mml = [None, None]
                for half in range(2):
                    bank = pg[4 + half]
                    for kc in range(8):
                        mml[half] = sch.emit("tensor", lambda e, kc=kc, rt=rt,
                                             half=half, bank=bank: e.matmul(
                            bank[:, 0:512],
                            own_fm[:, kc * NROW + rt * 128:
                                   kc * NROW + (rt + 1) * 128],
                            woT[:, kc * 1024 + half * 512:kc * 1024
                                + half * 512 + 512],
                            start=(kc == 0), stop=(kc == 7)),
                            inc=(1 if kc == 7 else 0),
                            waits=[obank_free[half]] if kc == 0 else [])
                t_h0 = sch.emit("vector", lambda e: e.tensor_tensor(
                    hsc[:, 0:512], pg[4][:, 0:512], cubo[:, 0:512], ALU.add),
                    waits=[mml[0], prev_th, t_cub])
                t_h1 = sch.emit("vector", lambda e: e.tensor_tensor(
                    hsc[:, 512:1024], pg[5][:, 0:512], cubo[:, 512:1024],
                    ALU.add), waits=[mml[1], t_h0])
                t_th = sch.emit("scalar", lambda e: e.activation(
                    hsc[:], hsc[:], AF.Tanh), waits=[t_h1])
                prev_th = t_th
                t_y = sch.emit("vector", lambda e, x=xb: e.tensor_tensor(
                    yt[:], x[:], hsc[:], ALU.add),
                    waits=[t_th, t_x, prev_store])
                xrt_free[p] = t_y
                obank_free[0] = t_y
                obank_free[1] = t_y
                prev_store = sch.dma("sync",
                                     x2_d[rt * 128:(rt + 1) * 128, :], yt[:],
                                     waits=[t_y])
        sch.flush()


def build_fused(phases="ABCD"):
    nc = bass.Bass("TRN2", target_bir_lowering=False)
    NROW = SEQ_PER_CORE * S
    g = {}
    g["x"] = nc.dram_tensor("x", [NROW, D], F32, kind="ExternalInput").ap()
    for nm, shp in (("wq0dup", [4 * D, D]), ("wv0t", [D, D]),
                    ("wo0", [D, D]),
                    ("wq1t", [D, D]), ("wk1t", [D, D]), ("wv1t", [D, D]),
                    ("wo1", [D, D]), ("wf1", [D, H]),
                    ("f_wf", [D, D]), ("f_uf", [D, D]), ("f_wi", [D, D]),
                    ("f_ui", [D, D]), ("f_wc", [D, D]), ("f_uc", [D, D]),
                    ("f_wo", [D, D]), ("f_uo", [D, D]),
                    ("wgate", [I_FF, D]), ("wup", [I_FF, D]),
                    ("wdown", [D, I_FF])):
        g[nm] = nc.dram_tensor(nm, shp, F32, kind="ExternalInput").ap()
    for nm in ("bq0", "bkv0", "f_bf", "f_bi", "f_bc", "f_bo", "f_inc"):
        shp = [2 * D, 1] if nm == "bkv0" else [D, 1]
        g[nm] = nc.dram_tensor(nm, shp, F32, kind="ExternalInput").ap()
    g["bo0"] = nc.dram_tensor("bo0", [1, D], F32, kind="ExternalInput").ap()
    g["csfull"] = nc.dram_tensor("csfull", [128, S], F32,
                                 kind="ExternalInput").ap()
    g["scfull"] = nc.dram_tensor("scfull", [128, S], F32,
                                 kind="ExternalInput").ap()
    g["ident"] = nc.dram_tensor("ident", [128, 128], F32,
                                kind="ExternalInput").ap()
    g["diagmask"] = nc.dram_tensor("diagmask", [128, 4 * 512], BF16,
                                   kind="ExternalInput").ap()
    g["diagmaskA"] = nc.dram_tensor("diagmaskA", [128, 4 * 512], F32,
                                    kind="ExternalInput").ap()
    y_h = nc.dram_tensor("y", [NROW, D], F32, kind="ExternalOutput").ap()
    x1_d = nc.dram_tensor("x1_d", [NROW, D], F32).ap()
    x2_d = nc.dram_tensor("x2_d", [NROW, D], F32).ap()
    x3_d = nc.dram_tensor("x3_d", [NROW, D], F32).ap()
    ln1loc = nc.dram_tensor("ln1loc", [NROW, D], BF16).ap()
    arin = nc.dram_tensor("arin", [B * S, D], BF16).ap()
    arout = nc.dram_tensor("arout", [B * S, D], BF16).ap()
    cuin = nc.dram_tensor("cuin", [S, D], BF16).ap()
    cuout = nc.dram_tensor("cuout", [S, D], BF16).ap()
    U32 = mybir.dt.uint32
    g["sidx"] = nc.dram_tensor("sidx", [128, 32], U32, kind="ExternalInput").ap()
    g["gidx"] = nc.dram_tensor("gidx", [128, 32], U32, kind="ExternalInput").ap()
    g["cidx"] = nc.dram_tensor("cidx", [128, 2], U32, kind="ExternalInput").ap()
    wgq_d = nc.dram_tensor("wgq_d", [D, I_FF], BF16).ap()
    wuq_d = nc.dram_tensor("wuq_d", [D, I_FF], BF16).ap()

    with ExitStack() as ctx0:
        sch = Sch(nc, ctx0)
        cc1 = ctx0.enter_context(nc.semaphore("cc1"))
        cc2 = ctx0.enter_context(nc.semaphore("cc2"))
        def _clr(e):
            e.sem_clear(cc1)
            e.sem_clear(cc2)
            return None
        sch.emit("gpsimd", _clr, inc=1)
        sch.flush()
        if "A" in phases:
            with nc.named_scope("phA"), ExitStack() as cA:
                emit_attn0(nc, sch, cA, x_h=g["x"], y_h=x1_d,
                           wqdup_h=g["wq0dup"], wvt_h=g["wv0t"],
                           wo_h=g["wo0"], bq_h=g["bq0"],
                           bkv_h=g["bkv0"], bo_h=g["bo0"], cs_h=g["csfull"],
                           sc_h=g["scfull"], id_h=g["ident"], dm_h=g["diagmask"])
                sch.flush()
        whs = {k[2:]: g["f_" + k[2:]] for k in
               ("f_wf", "f_uf", "f_wi", "f_ui", "f_wc", "f_uc", "f_wo", "f_uo")}
        bhs = {k: g["f_" + k] for k in ("bf", "bi", "bc", "bo", "inc")}
        if "B" in phases:
            with nc.named_scope("phB"), ExitStack() as cB:
                emit_ffn0_fused(nc, sch, cB, x1_d, x2_d, whs, bhs, g["ident"],
                                ln1loc, arin, arout, cuin, cuout,
                                g["sidx"], g["gidx"], g["cidx"], cc1, cc2)
                sch.flush()
        if "C" in phases:
            with nc.named_scope("phC"), ExitStack() as cC:
                emit_attn1(nc, sch, cC, x_h=x2_d, y_h=x3_d, wqt_h=g["wq1t"],
                           wkt_h=g["wk1t"], wvt_h=g["wv1t"], wo_h=g["wo1"],
                           wf_h=g["wf1"], id_h=g["ident"], dm_h=g["diagmaskA"])
                sch.flush()
        if "D" in phases:
            with nc.named_scope("phD"), ExitStack() as cD:
                emit_ffn1(nc, sch, cD, x_h=x3_d, y_h=y_h, wg_h=g["wgate"],
                          wu_h=g["wup"], wd_h=g["wdown"], id_h=g["ident"],
                          wgq_d=wgq_d, wuq_d=wuq_d)
                sch.flush()
        if "D" not in phases:
            srcd = {"A": x1_d, "B": x2_d, "C": x3_d}[phases[-1]]
            with ExitStack() as cZ:
                tbuf = sbuf(nc, cZ, "cpz", [128, D], F32)
                for tt in range(NROW // 128):
                    tl = sch.dma("sync", tbuf[:], srcd[tt * 128:(tt + 1) * 128, :])
                    sch.dma("sync", y_h[tt * 128:(tt + 1) * 128, :], tbuf[:],
                            waits=[tl])
            sch.flush()
        t_fin = sch.tok("sync")
        sch.emit("gpsimd", lambda e: None, inc=1, waits=[t_fin])
        sch.flush()
    return nc


def kernel_fused(phases="ABCD", _profile=False, **inp):
    nc = build_fused(phases)
    cs, sc = _rope_cs_sc()
    x = np.asarray(inp["L0_attn_x"], np.float32)

    def f32(k):
        return np.ascontiguousarray(np.asarray(inp[k], np.float32))
    def _dup_tiles(W):
        Wr = W.reshape(8, 128, 8, 2, 2, 32)
        out = np.empty((2, 8, 128, 1024), np.float32)
        for si in range(2):
            A = Wr[:, :, :, :, si, :]
            Ad = np.stack([A[:, :, :, 0], A[:, :, :, 0],
                           A[:, :, :, 1], A[:, :, :, 1]], axis=3)
            out[si] = Ad.transpose(2, 1, 0, 3, 4).reshape(8, 128, 1024)
        return out

    def _pair_tiles(W):
        V = W.reshape(8, 128, 8, 128)
        return np.ascontiguousarray(
            V.transpose(2, 1, 0, 3).reshape(8 * 128, 1024))

    wq0f = f32("L0_attn_Wq")
    wkv0f = f32("L0_attn_Wkv")
    qd = _dup_tiles(wq0f)
    kd = _dup_tiles(wkv0f[:, :D])
    wq0dup = np.ascontiguousarray(np.concatenate(
        [qd[0], qd[1], kd[0], kd[1]], 0).reshape(4 * 8 * 128, D))
    base = dict(
        wq0dup=wq0dup, wv0t=_pair_tiles(wkv0f[:, D:]), wo0=f32("L0_attn_Wo"),
        bq0=f32("L0_attn_bq").reshape(D, 1),
        bkv0=f32("L0_attn_bkv").reshape(2 * D, 1),
        bo0=f32("L0_attn_bo").reshape(1, D),
        f_wf=f32("L0_ffn_wf"), f_uf=f32("L0_ffn_uf"), f_wi=f32("L0_ffn_wi"),
        f_ui=f32("L0_ffn_ui"), f_wc=f32("L0_ffn_wc"), f_uc=f32("L0_ffn_uc"),
        f_wo=f32("L0_ffn_wo"), f_uo=f32("L0_ffn_uo"),
        f_bf=f32("L0_ffn_bf").reshape(D, 1), f_bi=f32("L0_ffn_bi").reshape(D, 1),
        f_bc=f32("L0_ffn_bc").reshape(D, 1), f_bo=f32("L0_ffn_bo").reshape(D, 1),
        f_inc=f32("L0_ffn_in_c").reshape(D, 1),
        wq1t=_pair_tiles(f32("L1_attn_Wq")),
        wk1t=_pair_tiles(f32("L1_attn_Wk")),
        wv1t=_pair_tiles(f32("L1_attn_Wv")),
        wo1=f32("L1_attn_Wo"), wf1=f32("L1_attn_Wf"),
        wgate=f32("L1_ffn_w_gate"), wup=f32("L1_ffn_w_up"),
        wdown=f32("L1_ffn_w_down"),
        csfull=cs, scfull=sc, ident=_ident(), diagmask=_diag_masks(),
        diagmaskA=_diag_masks_add(),
    )
    in_maps = []
    for c in range(NCORES):
        m = dict(base)
        m["x"] = np.ascontiguousarray(
            x[c * SEQ_PER_CORE:(c + 1) * SEQ_PER_CORE].reshape(
                SEQ_PER_CORE * S, D))
        k = np.arange(128, dtype=np.uint32)
        tt = np.arange(32, dtype=np.uint32)
        m["sidx"] = np.ascontiguousarray(
            ((2 * c + tt // 16) * 2048 + (tt % 16) * 128)[None, :] + k[:, None]
        ).astype(np.uint32)
        r = tt[None, :] * 128 + k[:, None]
        m["gidx"] = np.ascontiguousarray(
            (r // 256) * 2048 + c * 256 + (r % 256)).astype(np.uint32)
        jt = np.arange(2, dtype=np.uint32)
        m["cidx"] = np.ascontiguousarray(
            c * 256 + jt[None, :] * 128 + k[:, None]).astype(np.uint32)
        in_maps.append(m)
    res = run_bass_kernel_spmd(nc, in_maps, core_ids=list(range(NCORES)),
                               trace=_profile)
    if _profile:
        print("exec_time_ns:", res.exec_time_ns,
              "mean:", res.mean_exec_time_ns)
        if res.per_core_scope_times:
            for k, v in res.per_core_scope_times.items():
                vals = sorted(v.values())
                print(f"  scope {k}: min={min(vals)/1e3:.0f}us "
                      f"max={max(vals)/1e3:.0f}us")
    return np.concatenate([r["y"].reshape(SEQ_PER_CORE, S, D)
                           for r in res.results], 0)


def run_ffn0(x1, wf, uf, bf, wi, ui, bi, wc, uc, bc, wo, uo, bo, in_c):
    nc = build_ffn0()
    in_maps = []
    args = dict(
        wf=np.ascontiguousarray(np.asarray(wf, np.float32)),
        uf=np.ascontiguousarray(np.asarray(uf, np.float32)),
        wi=np.ascontiguousarray(np.asarray(wi, np.float32)),
        ui=np.ascontiguousarray(np.asarray(ui, np.float32)),
        wc=np.ascontiguousarray(np.asarray(wc, np.float32)),
        uc=np.ascontiguousarray(np.asarray(uc, np.float32)),
        wo=np.ascontiguousarray(np.asarray(wo, np.float32)),
        uo=np.ascontiguousarray(np.asarray(uo, np.float32)),
        bf=np.asarray(bf, np.float32).reshape(D, 1),
        bi=np.asarray(bi, np.float32).reshape(D, 1),
        bc=np.asarray(bc, np.float32).reshape(D, 1),
        bo=np.asarray(bo, np.float32).reshape(D, 1),
        inc=np.asarray(in_c, np.float32).reshape(D, 1),
        ident=_ident(),
    )
    for c in range(NCORES):
        xs = np.ascontiguousarray(x1[:, c * JW:(c + 1) * JW, :].reshape(
            T_STEPS * JW, D))
        m = dict(args)
        m["x"] = xs
        in_maps.append(m)
    res = run_bass_kernel_spmd(nc, in_maps, core_ids=list(range(NCORES)))
    out = np.empty((T_STEPS, S, D), np.float32)
    for c in range(NCORES):
        out[:, c * JW:(c + 1) * JW, :] = res.results[c]["y"].reshape(
            T_STEPS, JW, D)
    return out


I_FF = 2816
ISL = [(0, 512), (512, 512), (1024, 512), (1536, 512), (2048, 512), (2560, 256)]


def emit_quant(sch, src, scr2, out, bm, scale, rs, F, waits, out_guard=None):
    B16 = F // 16
    t_bm = sch.emit("vector", lambda e: e.tensor_reduce(
        bm[:, 0:B16], src.rearrange("p (g x) -> p g x", x=16), AX.X, ALU.max,
        apply_absolute_value=True), waits=waits)
    t_sc = sch.emit("vector", lambda e: e.tensor_scalar(
        scale[:, 0:B16], bm[:, 0:B16], 1.0 / 6.0, 1e-12, ALU.mult, ALU.max),
        waits=[t_bm])
    t_rs = sch.emit("vector", lambda e: e.reciprocal(
        rs[:, 0:B16], scale[:, 0:B16]), waits=[t_sc])
    rsb = rs[:, 0:B16].unsqueeze(2).to_broadcast((128, B16, 16))
    scb = scale[:, 0:B16].unsqueeze(2).to_broadcast((128, B16, 16))
    s3 = src.rearrange("p (g x) -> p g x", x=16)
    u3 = scr2.rearrange("p (g x) -> p g x", x=16)
    t_u = sch.emit("vector", lambda e: e.tensor_tensor(u3, s3, rsb, ALU.mult),
                   waits=[t_rs] + list(waits))
    t_v = sch.emit("vector", lambda e: e.tensor_scalar(
        src.bitcast(I32), scr2.bitcast(I32), 0x7F800000, None, ALU.bitwise_and),
        waits=[t_u])
    t_k = sch.emit("vector", lambda e: e.tensor_scalar(
        src, src, 1.0, 6291456.0, ALU.max, ALU.mult), waits=[t_v])
    t_w = sch.emit("vector", lambda e: e.tensor_tensor(
        scr2, scr2, src, ALU.add), waits=[t_k])
    t_q = sch.emit("vector", lambda e: e.tensor_tensor(
        scr2, scr2, src, ALU.subtract), waits=[t_w])
    o3 = out.rearrange("p (g x) -> p g x", x=16)
    t_o = sch.emit("vector", lambda e: e.tensor_tensor(o3, u3, scb, ALU.mult),
                   waits=[t_q, out_guard])
    return t_o


def emit_ffn1(nc, sch, ctx, x_h, y_h, wg_h, wu_h, wd_h, id_h, wgq_d, wuq_d):
    NROW = SEQ_PER_CORE * S
    if True:
        wdq = sbuf(nc, ctx, "wdq", [128, 22 * 1024], BF16)
        scr1 = sbuf(nc, ctx, "scr1", [128, I_FF], F32)
        scr2 = sbuf(nc, ctx, "scr2", [128, I_FF], F32)
        bm = sbuf(nc, ctx, "bm", [128, 176], F32)
        scale = sbuf(nc, ctx, "scale", [128, 176], F32)
        rs = sbuf(nc, ctx, "rs", [128, 176], F32)
        st = make_ln_state(nc, ctx, "D")
        ln32 = [sbuf(nc, ctx, f"ln32{i}", [128, D], F32) for i in range(2)]
        pA = psum(nc, ctx, "pA", [128, 512], F32)
        pB = psum(nc, ctx, "pB", [128, 512], F32)
        pC = psum(nc, ctx, "pC", [128, 512], F32)
        pD = psum(nc, ctx, "pD", [128, 512], F32)
        pE = psum(nc, ctx, "pE", [128, 512], F32)
        pF = psum(nc, ctx, "pF", [128, 512], F32)
        pT = psum(nc, ctx, "pT", [128, 1024], BF16)
        pT2 = psum(nc, ctx, "pT2", [128, 1024], BF16)
        st["ptr"] = [PHalf(pT, 0), PHalf(pT2, 0)]
        emit_load_ident(sch, st, id_h)
        sch.flush()

        with ExitStack() as c1:
            wq_tm = sbuf(nc, c1, "wqtm", [128, 22 * 1024], BF16)
            strip = sbuf(nc, c1, "strip", [128, I_FF], BF16)
            for w_hx, wq_dx in ((wg_h, wgq_d), (wu_h, wuq_d)):
                q_toks = []
                for it in range(22):
                    t_l = sch.dma("sync", scr1[:, 0:1024],
                                  w_hx[it * 128:(it + 1) * 128, :],
                                  waits=[st.get("q_prev")])
                    t_q = emit_quant(sch, scr1[:, 0:1024], scr2[:, 0:1024],
                                     wq_tm[:, it * 1024:(it + 1) * 1024],
                                     bm, scale, rs, 1024, [t_l])
                    st["q_prev"] = t_q
                    q_toks.append(t_q)
                sch.flush()
                for kc in range(8):
                    cp_toks = []
                    for it in range(22):
                        pb = st["ptr"][it % 2]
                        t_tr = sch.emit("tensor", lambda e, it=it, kc=kc, pb=pb:
                                        e.transpose(
                            pb[0:128, 0:128],
                            wq_tm[:, it * 1024 + kc * 128:it * 1024 + (kc + 1)
                                  * 128], st["id_bf"][:]),
                            waits=[st["ptr_free"][it % 2]])
                        t_cp = sch.emit("vector", lambda e, it=it, pb=pb:
                                        e.tensor_copy(
                            strip[:, it * 128:(it + 1) * 128], pb[0:128, 0:128]),
                            waits=[t_tr, st.get("strip_spill")])
                        st["ptr_free"][it % 2] = t_cp
                        cp_toks.append(t_cp)
                    st["spill_done"] = sch.dma(
                        "sync", wq_dx[kc * 128:(kc + 1) * 128, :],
                        strip[:], waits=[cp_toks[-1]])
                    st["strip_spill"] = st["spill_done"]
                    sch.flush()
            for et in range(8):
                t_l = sch.dma("sync", scr1[:], wd_h[et * 128:(et + 1) * 128, :],
                              waits=[st.get("q_prev")])
                t_q = emit_quant(sch, scr1[:], scr2[:],
                                 strip[:], bm, scale, rs, I_FF, [t_l],
                                 out_guard=st.get("strip_guard"))
                st["q_prev"] = t_q
                for ic in range(22):
                    pb = st["ptr"][ic % 2]
                    t_tr = sch.emit("tensor", lambda e, ic=ic, pb=pb: e.transpose(
                        pb[0:128, 0:128], strip[:, ic * 128:(ic + 1) * 128],
                        st["id_bf"][:]), waits=[t_q, st["ptr_free"][ic % 2]])
                    t_cp = sch.emit("vector", lambda e, ic=ic, et=et, pb=pb:
                                    e.tensor_copy(
                        wdq[:, ic * 1024 + et * 128:ic * 1024 + (et + 1) * 128],
                        pb[0:128, 0:128]), waits=[t_tr])
                    st["ptr_free"][ic % 2] = t_cp
                st["q_prev"] = sch.tok("vector")
                st["strip_guard"] = sch.tok("tensor")
                sch.flush()

        with ExitStack() as c2:
            xq_fm = sbuf(nc, c2, "xqfm", [128, 8 * 512], BF16)
            strip2 = sbuf(nc, c2, "strip2", [128, I_FF], BF16)
            hq_fm = sbuf(nc, c2, "hqfm", [128, 22 * 512], BF16)
            wg_sl = [sbuf(nc, c2, f"wgsl{i}", [128, 8 * 512], BF16) for i in range(2)]
            wu_sl = [sbuf(nc, c2, f"wusl{i}", [128, 8 * 512], BF16) for i in range(2)]
            silu_s = sbuf(nc, c2, "silus", [128, 512], F32)
            xq_tm = sbuf(nc, c2, "xqtm", [128, D], BF16)
            xrt = [sbuf(nc, c2, f"xrtD{i}", [128, D], F32) for i in range(2)]
            yt = [sbuf(nc, c2, f"ytD{i}", [128, D], F32) for i in range(2)]
            wsl_free = [None, None]
            for tg in range(8):
                row0 = tg * 512
                sums = st["stat"]
                for tt4 in range(4):
                    p = tt4 % 2
                    co = 8 * p
                    xb = st["xin"][p]
                    sq_scr = st["sq_scr"][p]
                    lnb = ln32[p]
                    t_x = sch.dma("sync", xb[:],
                                  x_h[row0 + tt4 * 128:row0 + (tt4 + 1) * 128, :],
                                  waits=[st["xin_free"][p]])
                    t_sum = sch.emit("vector", lambda e, x=xb, co=co:
                                     e.tensor_reduce(
                        sums[:, co:co + 1], x[:], AX.X, ALU.add),
                        waits=[t_x, st["stat_free"][p]])
                    t_sq0 = sch.emit("scalar", lambda e, x=xb, sq=sq_scr:
                                     e.activation(sq[:], x[:], AF.Square),
                                     waits=[t_x, st["sqw_free"][p]])
                    t_sq = sch.emit("vector", lambda e, sq=sq_scr, co=co:
                                    e.tensor_reduce(
                        sums[:, co + 4:co + 5], sq[:], AX.X, ALU.add),
                        waits=[t_sq0, st["stat_free"][p]])
                    st["sqw_free"][p] = t_sq
                    t_mean = sch.emit("vector", lambda e, co=co: e.tensor_scalar(
                        sums[:, co + 1:co + 2], sums[:, co:co + 1], 1.0 / D, None,
                        ALU.mult), waits=[t_sum])
                    t_msq = sch.emit("vector", lambda e, co=co: e.tensor_scalar(
                        sums[:, co + 2:co + 3], sums[:, co + 4:co + 5], 1.0 / D,
                        None, ALU.mult), waits=[t_sq])
                    t_mm = sch.emit("vector", lambda e, co=co: e.tensor_tensor(
                        sums[:, co + 3:co + 4], sums[:, co + 1:co + 2],
                        sums[:, co + 1:co + 2], ALU.mult), waits=[t_mean])
                    t_var = sch.emit("vector", lambda e, co=co: e.tensor_tensor(
                        sums[:, co + 5:co + 6], sums[:, co + 2:co + 3],
                        sums[:, co + 3:co + 4], ALU.subtract),
                        waits=[t_msq, t_mm])
                    t_sd = sch.emit("scalar", lambda e, co=co: e.activation(
                        sums[:, co + 6:co + 7], sums[:, co + 5:co + 6], AF.Sqrt,
                        bias=st["epsb"][:, 0:1]), waits=[t_var, st["t_eps"]])
                    t_rstd = sch.emit("vector", lambda e, co=co: e.reciprocal(
                        sums[:, co + 7:co + 8], sums[:, co + 6:co + 7]),
                        waits=[t_sd])
                    t_ap = sch.emit("vector", lambda e, x=xb, l=lnb, co=co:
                                    e.tensor_scalar(
                        l[:], x[:], sums[:, co + 1:co + 2],
                        sums[:, co + 7:co + 8], ALU.subtract, ALU.mult),
                        waits=[t_rstd, st["ln_free"][p]])
                    st["xin_free"][p] = t_ap
                    st["stat_free"][p] = t_ap
                    t_q = emit_quant(sch, lnb[:], scr2[:, 0:1024], xq_tm[:],
                                     bm, scale, rs, 1024, [t_ap],
                                     out_guard=st.get("q_xqtm"))
                    st["ln_free"][p] = t_q
                    last_cp = None
                    for c in range(8):
                        pb = st["ptr"][c % 2]
                        t_tr = sch.emit("tensor", lambda e, c=c, pb=pb: e.transpose(
                            pb[0:128, 0:128], xq_tm[:, c * 128:(c + 1) * 128],
                            st["id_bf"][:]),
                            waits=[t_q, st["ptr_free"][c % 2]])
                        t_cp = sch.emit("vector", lambda e, c=c, pb=pb, tt4=tt4:
                                        e.tensor_copy(
                            xq_fm[:, c * 512 + tt4 * 128:c * 512 + (tt4 + 1)
                                  * 128], pb[0:128, 0:128]), waits=[t_tr])
                        st["ptr_free"][c % 2] = t_cp
                        last_cp = t_cp
                    st["q_xqtm"] = last_cp
                sch.flush()

                for tt4 in range(4):
                    db = tt4 % 2
                    for isl, (i0, w) in enumerate(ISL):
                        st[f"wg_ld{isl % 2}"] = sch.dma(
                            "sync",
                            wg_sl[isl % 2].ap().rearrange(
                                "p (c m) -> p c m", c=8)[:, :, 0:w],
                            wgq_d[:, i0:i0 + w].rearrange(
                                "(c p) m -> p c m", p=128),
                            waits=[st.get(f"wg_used{isl % 2}"),
                                   st.get("spill_done")])
                        st[f"wu_ld{isl % 2}"] = sch.dma(
                            "sync",
                            wu_sl[isl % 2].ap().rearrange(
                                "p (c m) -> p c m", c=8)[:, :, 0:w],
                            wuq_d[:, i0:i0 + w].rearrange(
                                "(c p) m -> p c m", p=128),
                            waits=[st.get(f"wu_used{isl % 2}")])
                        mg = mu = None
                        for kc in range(8):
                            mg = sch.emit("tensor", lambda e, kc=kc, tt4=tt4,
                                          isl=isl, w=w: e.matmul(
                                pA[:, 0:w],
                                xq_fm[:, kc * 512 + tt4 * 128:kc * 512
                                      + (tt4 + 1) * 128],
                                wg_sl[isl % 2][:, kc * 512:kc * 512 + w],
                                start=(kc == 0), stop=(kc == 7)),
                                inc=(1 if kc == 7 else 0),
                                waits=[st.get(f"wg_ld{isl % 2}"),
                                       st.get("pA_free")] if kc == 0 else [])
                            mu = sch.emit("tensor", lambda e, kc=kc, tt4=tt4,
                                          isl=isl, w=w: e.matmul(
                                pB[:, 0:w],
                                xq_fm[:, kc * 512 + tt4 * 128:kc * 512
                                      + (tt4 + 1) * 128],
                                wu_sl[isl % 2][:, kc * 512:kc * 512 + w],
                                start=(kc == 0), stop=(kc == 7)),
                                inc=(1 if kc == 7 else 0),
                                waits=[st.get(f"wu_ld{isl % 2}"),
                                       st.get("pB_free")] if kc == 0 else [])
                        t_sil = sch.emit("scalar", lambda e, w=w: e.activation(
                            silu_s[:, 0:w], pA[:, 0:w], AF.Silu),
                            waits=[mg, st.get("pB_free")])
                        st["pA_free"] = t_sil
                        t_h = sch.emit("vector", lambda e, i0=i0, w=w:
                                       e.tensor_tensor(
                            scr1[:, i0:i0 + w], silu_s[:, 0:w], pB[:, 0:w],
                            ALU.mult), waits=[t_sil, mu, st.get("h_guard")])
                        st["pB_free"] = t_h
                        st[f"wg_used{isl % 2}"] = mg
                        st[f"wu_used{isl % 2}"] = mu
                    t_hq = emit_quant(sch, scr1[:], scr2[:], strip2[:],
                                      bm, scale, rs, I_FF, [sch.tok("vector")],
                                      out_guard=st.get("hq_guard"))
                    st["h_guard"] = t_hq
                    for ic in range(22):
                        pb = st["ptr"][ic % 2]
                        t_tr = sch.emit("tensor", lambda e, ic=ic, pb=pb:
                                        e.transpose(
                            pb[0:128, 0:128], strip2[:, ic * 128:(ic + 1) * 128],
                            st["id_bf"][:]), waits=[t_hq, st["ptr_free"][ic % 2]])
                        t_cp = sch.emit("vector", lambda e, ic=ic, tt4=tt4, pb=pb:
                                        e.tensor_copy(
                            hq_fm[:, ic * 512 + tt4 * 128:ic * 512 + (tt4 + 1)
                                  * 128], pb[0:128, 0:128]), waits=[t_tr])
                        st["ptr_free"][ic % 2] = t_cp
                        if ic == 21:
                            st["hq_guard"] = sch.tok("tensor")
                sch.flush()

                obank_free = [None, None]
                for tt4 in range(4):
                    p = tt4 % 2
                    xb = xrt[p]
                    yb = yt[p]
                    t_x = sch.dma("sync", xb[:],
                                  x_h[row0 + tt4 * 128:row0 + (tt4 + 1) * 128, :])
                    mml = [None, None]
                    for half in range(2):
                        bank = [pC, pD][half]
                        for ic in range(22):
                            mml[half] = sch.emit("tensor", lambda e, ic=ic,
                                                 tt4=tt4, half=half, bank=bank:
                                                 e.matmul(
                                bank[:, 0:512],
                                hq_fm[:, ic * 512 + tt4 * 128:ic * 512
                                      + (tt4 + 1) * 128],
                                wdq[:, ic * 1024 + half * 512:ic * 1024
                                    + half * 512 + 512],
                                start=(ic == 0), stop=(ic == 21)),
                                inc=(1 if ic == 21 else 0),
                                waits=[obank_free[half]] if ic == 0 else [])
                    t_y0 = sch.emit("vector", lambda e, x=xb, y=yb:
                                    e.tensor_tensor(
                        y[:, 0:512], x[:, 0:512], pC[:, 0:512], ALU.add),
                        waits=[t_x, mml[0]])
                    t_y1 = sch.emit("vector", lambda e, x=xb, y=yb:
                                    e.tensor_tensor(
                        y[:, 512:1024], x[:, 512:1024], pD[:, 0:512], ALU.add),
                        waits=[t_x, mml[1]])
                    obank_free[0] = t_y0
                    obank_free[1] = t_y1
                    sch.dma("sync",
                            y_h[row0 + tt4 * 128:row0 + (tt4 + 1) * 128, :],
                            yb[:], waits=[t_y0, t_y1])
                sch.flush()


def build_ffn1():
    nc = bass.Bass("TRN2", target_bir_lowering=False)
    NROW = SEQ_PER_CORE * S
    hs = {}
    hs["x_h"] = nc.dram_tensor("x", [NROW, D], F32, kind="ExternalInput").ap()
    hs["wg_h"] = nc.dram_tensor("wgate", [I_FF, D], F32, kind="ExternalInput").ap()
    hs["wu_h"] = nc.dram_tensor("wup", [I_FF, D], F32, kind="ExternalInput").ap()
    hs["wd_h"] = nc.dram_tensor("wdown", [D, I_FF], F32, kind="ExternalInput").ap()
    hs["id_h"] = nc.dram_tensor("ident", [128, 128], F32, kind="ExternalInput").ap()
    hs["y_h"] = nc.dram_tensor("y", [NROW, D], F32, kind="ExternalOutput").ap()
    hs["wgq_d"] = nc.dram_tensor("wgq_d", [D, I_FF], BF16)
    hs["wuq_d"] = nc.dram_tensor("wuq_d", [D, I_FF], BF16)
    with ExitStack() as ctx:
        sch = Sch(nc, ctx)
        emit_ffn1(nc, sch, ctx, **hs)
        t_fin = sch.tok("sync")
        sch.emit("gpsimd", lambda e: None, inc=1, waits=[t_fin])
        sch.flush()
    return nc


def run_ffn1(x3, w_gate, w_up, w_down):
    nc = build_ffn1()
    in_maps = []
    wg = np.ascontiguousarray(np.asarray(w_gate, np.float32))
    wu = np.ascontiguousarray(np.asarray(w_up, np.float32))
    wd = np.ascontiguousarray(np.asarray(w_down, np.float32))
    for c in range(NCORES):
        xs = np.ascontiguousarray(
            x3[c * SEQ_PER_CORE:(c + 1) * SEQ_PER_CORE].reshape(
                SEQ_PER_CORE * S, D))
        in_maps.append(dict(x=xs, wgate=wg, wup=wu, wdown=wd, ident=_ident()))
    res = run_bass_kernel_spmd(nc, in_maps, core_ids=list(range(NCORES)))
    return np.concatenate([r["y"].reshape(SEQ_PER_CORE, S, D)
                           for r in res.results], 0)


def _rope_cs_sc():
    inv = 1.0 / (10000.0 ** (np.arange(0, 64, 2, dtype=np.float64) / 64.0))
    pos = np.arange(S, dtype=np.float64)
    ang = inv[:, None] * pos[None, :]
    cos = np.cos(ang)
    sin = np.sin(ang)
    cs = np.concatenate([cos, sin, cos, sin], 0).astype(np.float32)
    sc = np.concatenate([-sin, cos, -sin, cos], 0).astype(np.float32)
    return cs, sc


def _diag_masks():
    from ml_dtypes import bfloat16
    m = np.zeros((128, 4, 512), np.float32)
    p = np.arange(128)[:, None]
    o = np.arange(512)[None, :]
    for d in range(4):
        m[:, d, :] = np.where(p > o - 128 * d, 0.0, 1.0)
    return m.reshape(128, 4 * 512).astype(bfloat16)


def _ident():
    return np.eye(128, dtype=np.float32)


def run_attn0(x, Wq, bq, Wkv, bkv, Wo, bo):
    nc = build_attn0()
    cs, sc = _rope_cs_sc()
    dm = _diag_masks()
    in_maps = []
    Wq = np.ascontiguousarray(np.asarray(Wq, np.float32))
    Wkv = np.ascontiguousarray(np.asarray(Wkv, np.float32))
    Wo = np.ascontiguousarray(np.asarray(Wo, np.float32))
    for c in range(NCORES):
        xs = np.ascontiguousarray(
            x[c * SEQ_PER_CORE:(c + 1) * SEQ_PER_CORE].reshape(SEQ_PER_CORE * S, D))
        in_maps.append(dict(x=xs, wq=Wq, wkv=Wkv, wo=Wo,
                            bq=np.asarray(bq, np.float32).reshape(D, 1),
                            bkv=np.asarray(bkv, np.float32).reshape(2 * D, 1),
                            bo=np.asarray(bo, np.float32).reshape(1, D),
                            csfull=cs, scfull=sc, diagmask=dm, ident=_ident()))
    res = run_bass_kernel_spmd(nc, in_maps, core_ids=list(range(NCORES)))
    return np.concatenate([r["y"].reshape(SEQ_PER_CORE, S, D)
                           for r in res.results], 0)


def kernel(**inp):
    return kernel_fused(phases="ABCD", **inp)


def kernel_multilaunch(**inp):
    x = np.asarray(inp["L0_attn_x"], np.float32)
    x1 = run_attn0(x, inp["L0_attn_Wq"], inp["L0_attn_bq"], inp["L0_attn_Wkv"],
                   inp["L0_attn_bkv"], inp["L0_attn_Wo"], inp["L0_attn_bo"])
    x2 = run_ffn0(x1, inp["L0_ffn_wf"], inp["L0_ffn_uf"], inp["L0_ffn_bf"],
                  inp["L0_ffn_wi"], inp["L0_ffn_ui"], inp["L0_ffn_bi"],
                  inp["L0_ffn_wc"], inp["L0_ffn_uc"], inp["L0_ffn_bc"],
                  inp["L0_ffn_wo"], inp["L0_ffn_uo"], inp["L0_ffn_bo"],
                  inp["L0_ffn_in_c"])
    x3 = run_attn1(x2, inp["L1_attn_Wq"], inp["L1_attn_Wk"], inp["L1_attn_Wv"],
                   inp["L1_attn_Wo"], inp["L1_attn_Wf"])
    x4 = run_ffn1(x3, inp["L1_ffn_w_gate"], inp["L1_ffn_w_up"],
                  inp["L1_ffn_w_down"])
    return x4
